# version 03
# baseline (speedup 1.0000x reference)
import numpy as np
import ml_dtypes
import concourse.bass as bass
import concourse.mybir as mybir
from concourse.bass_utils import run_bass_kernel_spmd

F32 = mybir.dt.float32
BF16 = mybir.dt.bfloat16
U16 = mybir.dt.uint16
AF = mybir.ActivationFunctionType
ALU = mybir.AluOpType
AX = mybir.AxisListType
BF = ml_dtypes.bfloat16

D, T, H, DH, WIN, DEPTH, DI = 1024, 2048, 16, 64, 256, 6, 2730
QKS = 8.0
NPT, NTT = 8, 16
ZOFF = 1
ZW = T + ZOFF
NEG = -1.0e30

CFG = {"depth": DEPTH, "do_outer": True, "do_inner": True, "barrier_level": 1,
       "do_spill": True}


def rotary_tables():
    half = np.arange(0, DH, 2, dtype=np.float64)
    inv_freq = 1.0 / 10000.0 ** (half / DH)
    t = np.arange(T, dtype=np.float64)
    freqs = t[:, None] * inv_freq[None, :]
    freqs = np.concatenate([freqs, freqs], -1)
    base = (half + 0.4 * DH) / (1.4 * DH)
    power = (t - T // 2) / (WIN // 2)
    scale = base[None, :] ** power[:, None]
    scale = np.concatenate([scale, scale], -1)
    cos, sin = np.cos(freqs), np.sin(freqs)
    sgn = np.ones((1, DH))
    sgn[:, : DH // 2] = -1.0
    c1q = QKS * cos * scale
    c2q = QKS * sin * scale * sgn
    c1k = cos / scale
    c2k = sin / scale * sgn
    return [np.ascontiguousarray(a, dtype=np.float32) for a in (c1q, c2q, c1k, c2k)]


def mask_tiles():
    u = np.arange(128)[:, None]
    tt = np.arange(128)[None, :]
    L = np.where(tt <= u, 0.0, NEG)
    Ui = np.where(tt >= u, 0.0, NEG)
    Us = np.where(tt > u, 0.0, NEG)
    Zt = np.zeros((128, 128))
    inner = np.concatenate([Ui, Zt, L], 1).astype(BF)
    outer = np.concatenate([Us, Zt, L], 1).astype(BF)
    zl = np.concatenate([Zt, L], 1).astype(BF)
    return inner, outer, zl


class Bld:
    def __init__(self, nc):
        self.nc = nc
        self.q = {k: [] for k in ("T", "V", "S", "G", "Y")}
        self.sem = {}
        self.cnt = {}
        self.cleared = {}
        for name in ("T", "V", "S", "G", "BAR"):
            self.sem[name] = nc.alloc_semaphore("sem_" + name)
            self.cnt[name] = 0

    def _ch(self, ch):
        name = "DW_" + ch
        if name not in self.sem:
            self.sem[name] = self.nc.alloc_semaphore("sem_" + name)
            self.cnt[name] = 0
        return name

    def _app(self, eng, fn):
        self.q[eng].append(fn)

    def wait(self, eng, name, val=None):
        v = self.cnt[name] if val is None else val
        v -= self.cleared.get(name, 0)
        if v <= 0:
            return
        h = self.sem[name]
        self._app(eng, lambda e, h=h, v=v: e.wait_ge(h, v))

    def tok(self, name):
        return self.cnt[name]

    def emit(self, eng, fn, inc=None, by=1):
        if inc is not None:
            self.cnt[inc] += by
            h = self.sem[inc]
            self._app(eng, lambda e, fn=fn, h=h, by=by: fn(e).then_inc(h, by))
            return self.cnt[inc]
        self._app(eng, fn)
        return None

    def dma(self, eng, out_ap, in_ap, ch="misc"):
        return self.emit(eng, lambda e, o=out_ap, i=in_ap: e.dma_start(out=o, in_=i),
                         inc=self._ch(ch), by=16)

    def dtok(self, ch):
        return self.cnt[self._ch(ch)]

    def dwait(self, eng, ch, val=None):
        self.wait(eng, self._ch(ch), val)

    def wait_all_dma(self, eng):
        for name in list(self.sem):
            if name.startswith("DW_"):
                self.wait(eng, name)

    def barrier(self):
        if not hasattr(self, "_bar_dw"):
            self._bar_dw = {}
        for name in ("T", "V", "S"):
            self.wait("G", name)
        for name in list(self.sem):
            if name.startswith("DW_") and self.cnt[name] > self._bar_dw.get(name, -1):
                self.wait("G", name)
                self._bar_dw[name] = self.cnt[name]
        hG = self.sem["G"]
        self.cnt["G"] += 1
        self._app("G", lambda e, h=hG: e.sem_inc(h, 1))
        g1 = self.cnt["G"]
        for eng in ("T", "V", "S"):
            self.wait(eng, "G", g1)
            h = self.sem[eng]
            self.cnt[eng] += 1
            self._app(eng, lambda e, h=h: e.sem_inc(h, 1))
        self.wait("Y", "G", g1)
        hB = self.sem["BAR"]
        self.cnt["BAR"] += 1
        self._app("Y", lambda e, h=hB: e.sem_inc(h, 1))
        for name in ("T", "V", "S", "BAR"):
            self.wait("G", name)
        for name in ("T", "V", "S"):
            h = self.sem[name]
            self._app("G", lambda e, h=h: e.sem_clear(h))
            self.cleared[name] = self.cnt[name]
        self.cnt["G"] += 1
        self._app("G", lambda e, h=hG: e.sem_inc(h, 1))
        g2 = self.cnt["G"]
        for eng in ("T", "V", "S", "Y"):
            self.wait(eng, "G", g2)

    def phase(self):
        if CFG["barrier_level"] >= 1:
            self.barrier()

    def finalize(self, block):
        qq = self.q

        @block.tensor
        def _(e):
            for fn in qq["T"]:
                fn(e)

        @block.vector
        def _(e):
            for fn in qq["V"]:
                fn(e)

        @block.scalar
        def _(e):
            for fn in qq["S"]:
                fn(e)

        @block.gpsimd
        def _(e):
            for fn in qq["G"]:
                fn(e)

        @block.sync
        def _(e):
            for fn in qq["Y"]:
                fn(e)


def build_program(nc, ext_dr=None):
    if ext_dr is not None:
        dr = dict(ext_dr)
    else:
        dr = {}
        dr["x"] = nc.dram_tensor("x", [T, D], F32, kind="ExternalInput")
        for n, sh in (("wq", [D, D]), ("wk", [D, D]), ("wv", [D, D]), ("wo", [D, D]),
                      ("fwqkv", [DEPTH, D, 3 * D]), ("fqs", [DEPTH, DH]),
                      ("fks", [DEPTH, DH]),
                      ("fwo", [DEPTH, D, D]), ("flng", [DEPTH, D]), ("flnb", [DEPTH, D]),
                      ("fw1", [DEPTH, D, 2 * DI]), ("fb1", [DEPTH, 2 * DI]),
                      ("fw2", [DEPTH, DI, D]), ("fb2", [DEPTH, D])):
            dr[n] = nc.dram_tensor(n, sh, F32, kind="ExternalInput")
        dr["y"] = nc.dram_tensor("y", [T, D], F32, kind="ExternalOutput")
    dr["xmid"] = nc.dram_tensor("xmid", [T, D], F32, kind="Internal")

    c1q, c2q, c1k, c2k = rotary_tables()
    m_inner, m_outer, m_zl = mask_tiles()
    co = {}
    co["c1q"] = nc.inline_tensor(c1q, name="c1q")
    co["c2q"] = nc.inline_tensor(c2q, name="c2q")
    co["c1k"] = nc.inline_tensor(c1k, name="c1k")
    co["c2k"] = nc.inline_tensor(c2k, name="c2k")
    co["m_inner"] = nc.inline_tensor(m_inner.view(np.uint16), name="mask_inner")
    co["m_outer"] = nc.inline_tensor(m_outer.view(np.uint16), name="mask_outer")
    co["m_zl"] = nc.inline_tensor(m_zl.view(np.uint16), name="mask_zl")
    co["i16"] = nc.inline_tensor(np.eye(128, dtype=BF).view(np.uint16), name="ident16")
    co["i32"] = nc.inline_tensor(np.eye(128, dtype=np.float32), name="ident32")
    co["onesrow"] = nc.inline_tensor(np.ones((1, 128), np.float32), name="onesrow")
    co["onescol16"] = nc.inline_tensor(np.ones((128, 1), dtype=BF).view(np.uint16),
                                       name="onescol16")
    co["ones1"] = nc.inline_tensor(np.ones((1, 1), np.float32), name="ones1")
    co["epsc"] = nc.inline_tensor(np.full((1, 1), 1e-5, np.float32), name="epsc")

    sb = {}

    def alo(name, shape, dtype):
        sb[name] = nc.alloc_sbuf_tensor("s_" + name, list(shape), dtype)
        return sb[name]

    for p in range(NPT):
        alo(f"z{p}", [128, ZW], F32)
    for p in range(NPT):
        alo(f"xn{p}", [128, T], BF16)
    for n, sh, dt in (("ident16", [128, 128], BF16), ("ident32", [128, 128], F32),
                      ("m_inner", [128, 384], BF16), ("m_outer", [128, 384], BF16),
                      ("m_zl", [128, 256], BF16), ("onesrow", [1, 128], F32),
                      ("onescol16", [128, 1], BF16), ("ones1", [1, 1], F32),
                      ("epsc", [1, 1], F32),
                      ("qsrow", [1, DH], F32), ("ksrow", [1, DH], F32),
                      ("qs_sw", [1, DH], F32), ("ks_sw", [1, DH], F32),
                      ("qsB", [128, DH], F32), ("qsSB", [128, DH], F32),
                      ("ksB", [128, DH], F32), ("ksSB", [128, DH], F32),
                      ("gcol", [128, NPT], F32), ("bcol", [128, NPT], F32),
                      ("b2col", [128, NPT], F32), ("b1c", [128, 4], F32),
                      ("savecol", [128, 4], F32), ("nmax", [128, 8], F32),
                      ("rsum", [128, 8], F32), ("rrec", [128, 8], F32),
                      ("ssb", [128, 8], F32), ("rnb", [128, 8], F32)):
        alo(n, sh, dt)

    ARENA = 120 * 1024
    arena = nc.bump_sbuf(ARENA, "left")
    assert arena is not None, "arena bump failed"
    abase = arena[0]

    def alo_at(name, shape, dtype, off):
        sz = 1
        for d in shape[1:]:
            sz *= d
        sz *= 4 if dtype == F32 else 2
        assert off + sz <= ARENA, (name, off, sz)
        sb[name] = nc.alloc_sbuf_tensor_at("s_" + name, list(shape), dtype,
                                           offset=abase + off)
        return off + ((sz + 31) // 32) * 32

    o = 0
    for p in range(NPT):
        o = alo_at(f"kbuf{p}", [128, 1280], BF16, o)
    for i in range(10):
        o = alo_at(f"vbuf{i}", [128, 1024], BF16, o)
    for p in range(NPT):
        o = alo_at(f"otbuf{p}", [128, 1024], BF16, o)
    for i in range(2):
        o = alo_at(f"wland{i}", [128, 1024], F32, o)
        o = alo_at(f"wbf{i}", [128, 1024], BF16, o)
        o = alo_at(f"woland{i}", [128, 1024], F32, o)
        o = alo_at(f"wobf{i}", [128, 1024], BF16, o)
        o = alo_at(f"wT{i}", [128, 1024], BF16, o)
    for n in ("c1qh", "c2qh", "c1kh", "c2kh", "aq", "bq", "ak", "bk"):
        o = alo_at(n, [128, 512], F32, o)
    for i in range(3):
        o = alo_at(f"qraw{i}", [128, 128], F32, o)
        o = alo_at(f"qhat{i}", [128, 128], F32, o)
        o = alo_at(f"t1b{i}", [128, 128], F32, o)
        o = alo_at(f"t2b{i}", [128, 128], F32, o)
        o = alo_at(f"qpp{i}", [128, 128], BF16, o)
        o = alo_at(f"pbuf{i}", [128, 384], BF16, o)
        o = alo_at(f"pnorm{i}", [128, 384], BF16, o)
    o = alo_at("sqb0", [128, 128], F32, o)
    o = alo_at("sqb1", [128, 128], F32, o)
    o = alo_at("sqb2", [128, 128], F32, o)
    o = alo_at("ptb", [128, 768], BF16, o)
    assert o <= ARENA, f"attn set {o}"

    o = 0
    for i in range(6):
        o = alo_at(f"act{i}", [128, 2048], BF16, o)
    for i in range(6):
        o = alo_at(f"w2bf{i}", [128, 1024], BF16, o)
    o = alo_at("w1land", [128, 2048], F32, o)
    o = alo_at("w1abf", [128, 1024], BF16, o)
    o = alo_at("w1gbf", [128, 1024], BF16, o)
    o = alo_at("w2land", [128, 1024], F32, o)
    o = alo_at("b1row", [1, 2 * DI], F32, o)
    o = alo_at("grow", [1, D], F32, o)
    o = alo_at("brow", [1, D], F32, o)
    o = alo_at("b2row", [1, D], F32, o)
    for n in ("srow", "srow2", "mrow", "vrow"):
        o = alo_at(n, [1, 512], F32, o)
    for i in range(2):
        o = alo_at(f"aprime{i}", [128, 512], F32, o)
        o = alo_at(f"ggel{i}", [128, 512], F32, o)
        o = alo_at(f"tb{i}", [128, 512], BF16, o)
        o = alo_at(f"tsq{i}", [128, 512], BF16, o)
        o = alo_at(f"s0buf{i}", [128, 512], F32, o)
    ffn_top = o
    assert o <= ARENA, f"ffn set {o}"

    o = 0
    for i in range(2):
        o = alo_at(f"xland{i}", [128, 1024], F32, o)
        o = alo_at(f"outb{i}", [128, 1024], F32, o)
    assert o <= ARENA, f"final set {o}"

    tshift = [sb[f"xn{p}"][:, 0:1024] for p in range(NPT)]
    qfull = [sb[f"xn{p}"][:, 1024:2048] for p in range(NPT)]

    def qview(flavor, hf, p):
        if flavor == "inner":
            return qfull[p]
        o = 1024 - 1024 * hf
        return sb[f"xn{p}"][:, o:o + 1024]

    ps = {}
    for n in ("A", "B", "C", "G", "H", "D", "E", "F"):
        ps[n] = nc.alloc_psum_tensor("ps" + n, [128, 512], F32)

    B = Bld(nc)
    Z = lambda p: sb[f"z{p}"]
    I16 = lambda: sb["ident16"][:]

    def load_consts():
        for n, c in (("m_inner", "m_inner"), ("m_outer", "m_outer"), ("m_zl", "m_zl"),
                     ("ident16", "i16"), ("onescol16", "onescol16")):
            B.dma("Y", sb[n][:].bitcast(U16), co[c].ap(), ch="init")
        for n, c in (("ident32", "i32"), ("onesrow", "onesrow"), ("ones1", "ones1"),
                     ("epsc", "epsc")):
            B.dma("Y", sb[n][:], co[c].ap(), ch="init")
        for eng in ("T", "V", "S", "G", "Y"):
            B.dwait(eng, "init")

    def load_x():
        for p in range(NPT):
            B.emit("V", lambda e, p=p: e.memset(Z(p)[:, 0:ZOFF], 0.0), inc="V")
        war = {0: 0, 1: 0}
        regwar = {}
        for tt in range(NTT):
            par = tt % 2
            buf = sb[f"xland{par}"]
            B.wait("Y", "V", war[par])
            B.dma("Y", buf[:], dr["x"].ap()[128 * tt:128 * (tt + 1), :], ch=f"x{par}")
            dwt = B.dtok(f"x{par}")
            B.dwait("T", f"x{par}", dwt)
            for p in range(NPT):
                bk = ("A", "B", "C", "D")[p % 4]
                bank = ps[bk]
                B.wait("T", "V", regwar.get(bk, 0))
                B.emit("T", lambda e, buf=buf, p=p, bank=bank:
                       e.matmul(out=bank[:, 0:128],
                                lhsT=buf[:, 128 * p:128 * (p + 1)],
                                rhs=sb["ident32"][:], start=True, stop=True), inc="T")
                tts = B.tok("T")
                B.wait("V", "T", tts)
                B.emit("V", lambda e, p=p, tt=tt, bank=bank:
                       e.tensor_copy(out=Z(p)[:, ZOFF + 128 * tt: ZOFF + 128 * (tt + 1)],
                                     in_=bank[:, 0:128]), inc="V")
                regwar[bk] = B.tok("V")
            war[par] = B.tok("V")
        B.phase()

    def ln_full(use_shift, affine, dst, tcs=range(4)):
        sum_tok = {0: 0, 1: 0}
        stat_read = 0
        apply_done = 0
        for tc in tcs:
            c0 = ZOFF + 512 * tc
            for p in range(NPT):
                par = p % 2
                tbuf, sqb = sb[f"tb{par}"], sb[f"tsq{par}"]
                B.wait("V", "T", sum_tok[par])
                if p == 0:
                    B.wait("T", "V", stat_read)
                off = c0 - (1 if (use_shift and p >= 4) else 0)
                cv = B.emit("V", lambda e, p=p, off=off, tbuf=tbuf:
                            e.tensor_copy(out=tbuf[:], in_=Z(p)[:, off:off + 512]), inc="V")
                B.wait("V", "V", cv)
                B.emit("V", lambda e, tbuf=tbuf, sqb=sqb:
                       e.tensor_tensor(out=sqb[:], in0=tbuf[:], in1=tbuf[:],
                                       op=ALU.mult), inc="V")
                vt = B.tok("V")
                B.wait("T", "V", vt)
                B.emit("T", lambda e, p=p, tbuf=tbuf:
                       e.matmul(out=ps["D"][0:1, 0:512], lhsT=sb["onescol16"][:],
                                rhs=tbuf[:], start=(p == 0), stop=(p == NPT - 1)), inc="T")
                B.emit("T", lambda e, p=p, sqb=sqb:
                       e.matmul(out=ps["E"][0:1, 0:512], lhsT=sb["onescol16"][:],
                                rhs=sqb[:], start=(p == 0), stop=(p == NPT - 1)), inc="T")
                sum_tok[par] = B.tok("T")
            t_sums = B.tok("T")
            rstd = sb["srow"][:]
            nmr = sb["srow2"][:]
            mu, vr = sb["mrow"][:], sb["vrow"][:]
            B.wait("V", "T", t_sums)
            B.emit("V", lambda e, mu=mu: e.tensor_scalar_mul(
                out=mu, in0=ps["D"][0:1, 0:512], scalar1=1.0 / D), inc="V")
            vrt = B.emit("V", lambda e, vr=vr: e.tensor_scalar_mul(
                out=vr, in0=ps["E"][0:1, 0:512], scalar1=1.0 / D), inc="V")
            stat_read = B.tok("V")
            vt = B.tok("V")
            B.wait("S", "V", vt)
            B.emit("S", lambda e, mu=mu, nmr=nmr: e.activation(
                out=nmr, in_=mu, func=AF.Square), inc="S")
            st = B.tok("S")
            B.wait("V", "S", st)
            B.wait("V", "V", vrt)
            B.emit("V", lambda e, vr=vr, nmr=nmr: e.tensor_tensor(
                out=vr, in0=vr, in1=nmr, op=ALU.subtract), inc="V")
            vt = B.tok("V")
            B.wait("S", "V", vt)
            lnt = B.emit("S", lambda e, vr=vr, mu=mu: e.activation(
                out=vr, in_=vr, func=AF.Ln, bias=sb["epsc"][:]), inc="S")
            B.wait("S", "S", lnt)
            B.emit("S", lambda e, vr=vr, rstd=rstd: e.activation(
                out=rstd, in_=vr, func=AF.Exp, scale=-0.5), inc="S")
            st = B.tok("S")
            B.wait("V", "S", st)
            B.wait("V", "V", vrt)
            B.emit("V", lambda e, mu=mu, rstd=rstd, nmr=nmr: e.scalar_tensor_tensor(
                out=nmr, in0=mu, scalar=-1.0, in1=rstd, op0=ALU.mult, op1=ALU.mult),
                   inc="V")
            vt_rows = B.tok("V")
            B.wait("T", "V", vt_rows)
            B.wait("T", "V", apply_done)
            B.emit("T", lambda e, rstd=rstd: e.matmul(
                out=ps["F"][:, 0:512], lhsT=sb["onesrow"][:], rhs=rstd,
                start=True, stop=True), inc="T")
            B.emit("T", lambda e, nmr=nmr: e.matmul(
                out=ps["C"][:, 0:512], lhsT=sb["onesrow"][:], rhs=nmr,
                start=True, stop=True), inc="T")
            t_bc = B.tok("T")
            B.wait("V", "T", t_bc)
            s3t = {0: 0, 1: 0}
            for pp in range(0, NPT, 2):
                s0t = {}
                s1t = {}
                for p in (pp, pp + 1):
                    off = c0 - (1 if (use_shift and p >= 4) else 0)
                    buf = sb[f"s0buf{p % 2}"]
                    B.wait("V", "V", s3t[p % 2])
                    s0t[p] = B.emit("V", lambda e, p=p, off=off, buf=buf: e.tensor_tensor(
                        out=buf[:], in0=Z(p)[:, off:off + 512], in1=ps["F"][:, 0:512],
                        op=ALU.mult), inc="V")
                for p in (pp, pp + 1):
                    buf = sb[f"s0buf{p % 2}"]
                    B.wait("V", "V", s0t[p])
                    if affine or dst == "z":
                        s1t[p] = B.emit("V", lambda e, buf=buf: e.tensor_tensor(
                            out=buf[:], in0=buf[:], in1=ps["C"][:, 0:512],
                            op=ALU.add), inc="V")
                    else:
                        s1t[p] = B.emit("V", lambda e, p=p, tc=tc, buf=buf: e.tensor_tensor(
                            out=sb[f"xn{p}"][:, 512 * tc:512 * (tc + 1)], in0=buf[:],
                            in1=ps["C"][:, 0:512], op=ALU.add), inc="V")
                for p in (pp, pp + 1):
                    buf = sb[f"s0buf{p % 2}"]
                    B.wait("V", "V", s1t[p])
                    if affine:
                        s3t[p % 2] = B.emit("V", lambda e, p=p, tc=tc, buf=buf:
                                            e.tensor_scalar(
                            out=sb[f"xn{p}"][:, 512 * tc:512 * (tc + 1)], in0=buf[:],
                            scalar1=sb["gcol"][:, p:p + 1], scalar2=sb["bcol"][:, p:p + 1],
                            op0=ALU.mult, op1=ALU.add), inc="V")
                    elif dst == "z":
                        s3t[p % 2] = B.emit("V", lambda e, p=p, c0=c0, buf=buf:
                                            e.tensor_copy(
                            out=Z(p)[:, c0:c0 + 512], in_=buf[:]), inc="V")
                    else:
                        s3t[p % 2] = s1t[p]
            apply_done = B.tok("V")
        B.phase()

    def colmaker(rows_and_dsts):
        banks = ["A", "B", "C", "D", "E", "F"]
        i = 0
        while i < len(rows_and_dsts):
            grp = rows_and_dsts[i:i + 6]
            for k, (rap, _) in enumerate(grp):
                B.emit("T", lambda e, rap=rap, bk=banks[k]: e.matmul(
                    out=ps[bk][:, 0:1], lhsT=rap, rhs=sb["ones1"][:],
                    start=True, stop=True), inc="T")
            tt = B.tok("T")
            B.wait("V", "T", tt)
            for k, (_, dap) in enumerate(grp):
                B.emit("V", lambda e, dap=dap, bk=banks[k]: e.tensor_copy(
                    out=dap, in_=ps[bk][:, 0:1]), inc="V")
            vt = B.tok("V")
            B.wait("T", "V", vt)
            i += 6

    def load_attn_params(i):
        B.dma("Y", sb["qsrow"][:], dr["fqs"].ap()[i:i + 1, :], ch="par")
        B.dma("Y", sb["ksrow"][:], dr["fks"].ap()[i:i + 1, :], ch="par")
        B.dwait("V", "par")
        hd = DH // 2
        B.emit("V", lambda e: e.tensor_copy(out=sb["qs_sw"][:, hd:DH],
                                            in_=sb["qsrow"][:, 0:hd]), inc="V")
        B.emit("V", lambda e: e.tensor_copy(out=sb["qs_sw"][:, 0:hd],
                                            in_=sb["qsrow"][:, hd:DH]), inc="V")
        B.emit("V", lambda e: e.tensor_copy(out=sb["ks_sw"][:, hd:DH],
                                            in_=sb["ksrow"][:, 0:hd]), inc="V")
        B.emit("V", lambda e: e.tensor_copy(out=sb["ks_sw"][:, 0:hd],
                                            in_=sb["ksrow"][:, hd:DH]), inc="V")
        vt = B.tok("V")
        B.wait("T", "V", vt)
        rows = [("qsrow", "qsB"), ("qs_sw", "qsSB"), ("ksrow", "ksB"), ("ks_sw", "ksSB")]
        banks = ["A", "B", "C", "D"]
        for k, (r, d) in enumerate(rows):
            B.emit("T", lambda e, r=r, bk=banks[k]: e.matmul(
                out=ps[bk][:, 0:DH], lhsT=sb["onesrow"][:], rhs=sb[r][:],
                start=True, stop=True), inc="T")
        tt = B.tok("T")
        B.wait("V", "T", tt)
        for k, (r, d) in enumerate(rows):
            B.emit("V", lambda e, d=d, bk=banks[k]: e.tensor_copy(
                out=sb[d][:], in_=ps[bk][:, 0:DH]), inc="V")
        B.phase()

    def load_ffn_params(i):
        B.dma("Y", sb["grow"][:], dr["flng"].ap()[i:i + 1, :], ch="par")
        B.dma("Y", sb["brow"][:], dr["flnb"].ap()[i:i + 1, :], ch="par")
        B.dma("Y", sb["b2row"][:], dr["fb2"].ap()[i:i + 1, :], ch="par")
        B.dma("Y", sb["b1row"][:], dr["fb1"].ap()[i:i + 1, :], ch="par")
        B.dwait("T", "par")
        jobs = []
        for p in range(NPT):
            jobs.append((sb["grow"][:, 128 * p:128 * (p + 1)], sb["gcol"][:, p:p + 1]))
            jobs.append((sb["brow"][:, 128 * p:128 * (p + 1)], sb["bcol"][:, p:p + 1]))
            jobs.append((sb["b2row"][:, 128 * p:128 * (p + 1)], sb["b2col"][:, p:p + 1]))
        colmaker(jobs)
        B.phase()

    def half_tables(hf):
        T0 = 1024 * hf
        for name, cn in (("c1qh", "c1q"), ("c2qh", "c2q"), ("c1kh", "c1k"), ("c2kh", "c2k")):
            B.dma("Y", sb[name][:].rearrange("p (a d) -> p a d", d=DH),
                  co[cn].ap()[T0:T0 + 1024, :].rearrange("(a p) d -> p a d", p=128),
                  ch="tab")
        B.dwait("V", "tab")
        for dst, src, qb in (("aq", "c1qh", "qsB"), ("bq", "c2qh", "qsSB"),
                             ("ak", "c1kh", "ksB"), ("bk", "c2kh", "ksSB")):
            B.emit("V", lambda e, dst=dst, src=src, qb=qb: e.tensor_tensor(
                out=sb[dst][:].rearrange("p (a d) -> p a d", d=DH),
                in0=sb[src][:].rearrange("p (a d) -> p a d", d=DH),
                in1=sb[qb][:].unsqueeze(1).broadcast_to([128, 8, DH]),
                op=ALU.mult), inc="V")

    def boundary_copies():
        for p in range(NPT):
            B.emit("V", lambda e, p=p: e.tensor_copy(
                out=sb[f"kbuf{p}"][:, 0:256], in_=sb[f"kbuf{p}"][:, 1024:1280]), inc="V")
        B.emit("V", lambda e: e.tensor_copy(out=sb["vbuf0"][:], in_=sb["vbuf8"][:]), inc="V")
        B.emit("V", lambda e: e.tensor_copy(out=sb["vbuf1"][:], in_=sb["vbuf9"][:]), inc="V")

    def build_tshift(hf):
        T0 = 1024 * hf
        for p in range(NPT):
            if p < 4:
                B.emit("V", lambda e, p=p, T0=T0: e.tensor_copy(
                    out=tshift[p], in_=Z(p)[:, ZOFF + T0:ZOFF + T0 + 1024]), inc="V")
            elif hf == 0:
                B.emit("V", lambda e, p=p: e.tensor_copy(
                    out=tshift[p], in_=Z(p)[:, 0:1024]), inc="V")
            else:
                B.emit("V", lambda e, p=p: e.tensor_copy(
                    out=tshift[p][:, 1:1024], in_=Z(p)[:, ZOFF + 1024:ZOFF + 2047]), inc="V")
                B.emit("V", lambda e, p=p: e.tensor_copy(
                    out=tshift[p][:, 0:1], in_=sb["savecol"][:, p - 4:p - 3]), inc="V")
        if hf == 0:
            for p in range(4, NPT):
                B.emit("V", lambda e, p=p: e.tensor_copy(
                    out=sb["savecol"][:, p - 4:p - 3],
                    in_=Z(p)[:, ZOFF + 1023:ZOFF + 1024]), inc="V")

    def inner_qkv(i, hf):
        conv_tok = {0: 0, 1: 0}
        mm_tok = {0: 0, 1: 0}
        cons_tok = {0: 0, 1: 0}
        trwar = {0: 0, 1: 0}
        for c in range(24):
            par = c % 2
            kind = "q" if c < 8 else ("k" if c < 16 else "v")
            B.wait("Y", "G", conv_tok[par])
            B.dma("Y", sb[f"wland{par}"][:].rearrange("p (a n) -> p a n", n=128),
                  dr["fwqkv"].ap()[i, :, 128 * c:128 * (c + 1)].rearrange(
                      "(a p) n -> p a n", p=128), ch=f"wq{par}")
            B.dwait("G", f"wq{par}")
            B.wait("G", "T", mm_tok[par])
            B.emit("G", lambda e, par=par: e.tensor_copy(
                out=sb[f"wbf{par}"][:], in_=sb[f"wland{par}"][:]), inc="G")
            conv_tok[par] = B.tok("G")
            gt = B.tok("G")
            for ttp in range(0, 8, 2):
                pend = []
                for tt in (ttp, ttp + 1):
                    bp = tt % 2
                    bank = ps["A"] if bp == 0 else ps["B"]
                    B.wait("T", "G", gt)
                    B.wait("T", "S", cons_tok[bp])
                    for dt in range(NPT):
                        B.emit("T", lambda e, c=c, tt=tt, dt=dt, par=par, bank=bank:
                               e.matmul(out=bank[:, 0:128],
                                        lhsT=tshift[dt][:, 128 * tt:128 * (tt + 1)],
                                        rhs=sb[f"wbf{par}"][:, 128 * dt:128 * (dt + 1)],
                                        start=(dt == 0), stop=(dt == NPT - 1)), inc="T")
                    mmt = B.tok("T")
                    pend.append((tt, bank, mmt))
                if kind == "v":
                    for tt, bank, mmt in pend:
                        B.wait("S", "T", mmt)
                        B.emit("S", lambda e, c=c, tt=tt, bank=bank: e.copy(
                            out=sb[f"vbuf{2 + tt}"][:, 128 * (c - 16):128 * (c - 15)],
                            in_=bank[:, 0:128]), inc="S")
                        cons_tok[tt % 2] = B.tok("S")
                else:
                    qk_process(c, hf, pend, trwar, cons_tok)
            mm_tok[par] = B.tok("T")

    def qk_process(c, hf, pend, trwar, cons_tok):
        isq = c < 8
        cc = c if isq else c - 8
        At = sb["aq"] if isq else sb["ak"]
        Bt = sb["bq"] if isq else sb["bk"]
        NH = 2
        for tt, bank, mmt in pend:
            j3 = tt % 3
            B.wait("S", "T", mmt)
            B.wait("S", "V", qk_war.get(j3, 0))
            B.emit("S", lambda e, j3=j3, bank=bank: e.copy(
                out=sb[f"qraw{j3}"][:], in_=bank[:, 0:128]), inc="S")
            cons_tok[tt % 2] = B.tok("S")
            sev = B.tok("S")
            B.wait("G", "S", sev)
            B.wait("G", "V", qk_war.get(j3, 0))
            B.emit("G", lambda e, j3=j3: e.tensor_tensor(
                out=sb[f"sqb{j3}"][:], in0=sb[f"qraw{j3}"][:], in1=sb[f"qraw{j3}"][:],
                op=ALU.mult), inc="G")
            gt = B.tok("G")
            B.wait("V", "G", gt)
            B.emit("V", lambda e, j3=j3: e.tensor_reduce(
                out=sb["ssb"][:, 2 * j3:2 * j3 + 2],
                in_=sb[f"sqb{j3}"][:].rearrange("p (h d) -> p h d", h=NH),
                axis=AX.X, op=ALU.add), inc="V")
            vt = B.tok("V")
            B.wait("G", "V", vt)
            B.emit("G", lambda e, j3=j3: e.tensor_scalar_max(
                out=sb["rnb"][:, 2 * j3:2 * j3 + 2], in0=sb["ssb"][:, 2 * j3:2 * j3 + 2],
                scalar1=1e-24), inc="G")
            gt = B.tok("G")
            B.wait("S", "G", gt)
            lnt = B.emit("S", lambda e, j3=j3: e.activation(
                out=sb["nmax"][:, 2 * j3:2 * j3 + 2], in_=sb["rnb"][:, 2 * j3:2 * j3 + 2],
                func=AF.Ln), inc="S")
            B.wait("S", "S", lnt)
            B.emit("S", lambda e, j3=j3: e.activation(
                out=sb["rnb"][:, 2 * j3:2 * j3 + 2], in_=sb["nmax"][:, 2 * j3:2 * j3 + 2],
                func=AF.Exp, scale=-0.5), inc="S")
            pend_s[tt] = B.tok("S")
        hd = DH // 2
        qh_t = {}
        t1_t = {}
        t2_t = {}
        for step in ("qhat", "t1b", "t2a", "t2c", "qpp"):
            for tt, bank, mmt in pend:
                j3 = tt % 3
                b = tt % 2
                tsl = 64 * tt
                if step == "qhat":
                    B.wait("V", "S", pend_s[tt])
                    qh_t[tt] = B.emit("V", lambda e, j3=j3: e.tensor_tensor(
                        out=sb[f"qhat{j3}"][:].rearrange("p (h d) -> p h d", h=NH),
                        in0=sb[f"qraw{j3}"][:].rearrange("p (h d) -> p h d", h=NH),
                        in1=sb["rnb"][:, 2 * j3:2 * j3 + 2].unsqueeze(2)
                            .broadcast_to([128, NH, DH]),
                        op=ALU.mult), inc="V")
                elif step == "t1b":
                    B.wait("V", "V", qh_t[tt])
                    t1_t[tt] = B.emit("V", lambda e, j3=j3, tsl=tsl, At=At: e.tensor_tensor(
                        out=sb[f"t1b{j3}"][:].rearrange("p (h d) -> p h d", h=NH),
                        in0=sb[f"qhat{j3}"][:].rearrange("p (h d) -> p h d", h=NH),
                        in1=At[:, tsl:tsl + DH].unsqueeze(1)
                            .broadcast_to([128, NH, DH]),
                        op=ALU.mult), inc="V")
                elif step == "t2a":
                    B.emit("V", lambda e, j3=j3, tsl=tsl, Bt=Bt: e.tensor_tensor(
                        out=sb[f"t2b{j3}"][:].rearrange("p (h d) -> p h d", h=NH)[:, :, 0:hd],
                        in0=sb[f"qhat{j3}"][:].rearrange("p (h d) -> p h d", h=NH)[:, :, hd:DH],
                        in1=Bt[:, tsl:tsl + hd].unsqueeze(1).broadcast_to([128, NH, hd]),
                        op=ALU.mult), inc="V")
                elif step == "t2c":
                    t2_t[tt] = B.emit("V", lambda e, j3=j3, tsl=tsl, Bt=Bt: e.tensor_tensor(
                        out=sb[f"t2b{j3}"][:].rearrange("p (h d) -> p h d", h=NH)[:, :, hd:DH],
                        in0=sb[f"qhat{j3}"][:].rearrange("p (h d) -> p h d", h=NH)[:, :, 0:hd],
                        in1=Bt[:, tsl + hd:tsl + DH].unsqueeze(1).broadcast_to([128, NH, hd]),
                        op=ALU.mult), inc="V")
                else:
                    B.wait("V", "V", max(t1_t[tt], t2_t[tt]))
                    B.wait("V", "T", qpp_war.get(j3, 0))
                    B.emit("V", lambda e, j3=j3: e.tensor_tensor(
                        out=sb[f"qpp{j3}"][:], in0=sb[f"t1b{j3}"][:], in1=sb[f"t2b{j3}"][:],
                        op=ALU.add), inc="V")
                    pend_v[tt] = B.tok("V")
        for tt, bank, mmt in pend:
            j3 = tt % 3
            bk = ps["C"] if tt % 2 == 0 else ps["G"]
            B.wait("T", "V", pend_v[tt])
            B.wait("T", "V", trwar[tt % 2])
            B.emit("T", lambda e, j3=j3, bk=bk: e.matmul(
                out=bk[:, 0:128], lhsT=sb[f"qpp{j3}"][:],
                rhs=I16(), start=True, stop=True), inc="T")
            trt = B.tok("T")
            qpp_war[j3] = trt
            B.wait("V", "T", trt)
            if isq:
                dst = qfull[cc][:, 128 * tt:128 * (tt + 1)]
            else:
                dst = sb[f"kbuf{cc}"][:, 256 + 128 * tt:256 + 128 * (tt + 1)]
            B.emit("V", lambda e, dst=dst, bk=bk: e.tensor_copy(
                out=dst, in_=bk[:, 0:128]), inc="V")
            trwar[tt % 2] = B.tok("V")
            qk_war[j3] = B.tok("V")

    qk_war = {}
    qpp_war = {}
    pend_s = {}
    pend_v = {}

    def attn_loop(hf, flavor):
        if flavor == "inner":
            subs = []
            for wi in range(4):
                for h in range(H):
                    for qh in range(2):
                        subs.append((wi, h, qh))
        else:
            subs = [(qt, h, 0) for qt in range(8) for h in range(H)]
        LAG = 3
        sim_info = {}
        war_sim = {}
        st = {"war_ptb": 0, "pt_sw": {}, "pt_seq": 0, "pnorm_war": {},
              "nmax_war": {}, "rsum_war": {}, "rrec_war": {}}
        pv_done = {0: 0, 1: 0}
        n = len(subs)
        for s in range(n + LAG):
            if s >= LAG:
                emit_tail(hf, flavor, s - LAG, subs[s - LAG], sim_info, war_sim,
                          st, pv_done)
            if s < n:
                emit_sim(hf, flavor, s, subs[s], sim_info, war_sim, st)

    def emit_sim(hf, flavor, s, sub, sim_info, war_sim, st):
        bank = ps[["A", "B", "C"][s % 3]]
        if flavor == "inner":
            wi, h, qh = sub
            gw = 4 * hf + wi
            if gw == 0:
                ncols = 128 if qh == 0 else 256
                mask = sb["m_zl"][:, 128:256] if qh == 0 else sb["m_zl"][:]
                colbase = 256
            else:
                ncols = 384
                mask = sb["m_inner"][:]
                colbase = 256 * wi + 128 * qh
            qcols = (256 * wi + 128 * qh, 256 * wi + 128 * qh + 128)
            tile = h // 2
            po = 64 * (h % 2)
            lhs = qview(flavor, hf, tile)[po:po + 64, qcols[0]:qcols[1]]
            rhs = sb[f"kbuf{tile}"][po:po + 64, colbase:colbase + ncols]
        else:
            qt, h, _ = sub
            gqt = 8 * hf + qt
            j0 = max(0, gqt - 2)
            nkt = gqt - j0 + 1
            ncols = 128 * nkt
            if nkt == 3:
                mask = sb["m_outer"][:]
            elif nkt == 2:
                mask = sb["m_zl"][:]
            else:
                mask = sb["m_zl"][:, 128:256]
            colbase = 256 + 128 * (j0 - 8 * hf)
            tile = h // 2
            po = 64 * (h % 2)
            lhs = qview(flavor, hf, tile)[po:po + 64, 128 * qt:128 * (qt + 1)]
            rhs = sb[f"kbuf{tile}"][po:po + 64, colbase:colbase + ncols]
        B.wait("T", "V", war_sim.get(s % 3, 0))
        B.wait("T", "S", war_sim.get(("S", s % 3), 0))
        B.emit("T", lambda e, bank=bank, mask=mask, ncols=ncols: e.matmul(
            out=bank[:, 0:ncols], lhsT=I16(), rhs=mask, start=True, stop=False), inc="T")
        B.emit("T", lambda e, bank=bank, lhs=lhs, rhs=rhs, ncols=ncols: e.matmul(
            out=bank[:, 0:ncols], lhsT=lhs, rhs=rhs, start=False, stop=True), inc="T")
        simt = B.tok("T")
        r = s % 8
        B.wait("V", "T", simt)
        B.wait("V", "S", st["nmax_war"].get(r, 0))
        B.emit("V", lambda e, bank=bank, r=r, ncols=ncols: e.tensor_reduce(
            out=sb["nmax"][:, r:r + 1], in_=bank[:, 0:ncols], axis=AX.X, op=ALU.max,
            negate=True), inc="V")
        vt = B.tok("V")
        war_sim[s % 3] = vt
        B.wait("S", "V", vt)
        B.wait("S", "V", st["rsum_war"].get(r, 0))
        B.emit("S", lambda e, bank=bank, r=r, s3=s % 3, ncols=ncols: e.activation(
            out=sb[f"pbuf{s3}"][:, 0:ncols], in_=bank[:, 0:ncols], func=AF.Exp,
            bias=sb["nmax"][:, r:r + 1], accum_out=sb["rsum"][:, r:r + 1]), inc="S")
        war_sim[("S", s % 3)] = B.tok("S")
        st["nmax_war"][r] = B.tok("S")
        stok = B.tok("S")
        B.wait("V", "S", stok)
        B.wait("V", "S", st["rrec_war"].get(r, 0))
        B.emit("V", lambda e, r=r: e.reciprocal(
            out=sb["rrec"][:, r:r + 1], in_=sb["rsum"][:, r:r + 1]), inc="V")
        st["rsum_war"][r] = B.tok("V")
        vt = B.tok("V")
        B.wait("S", "V", vt)
        B.wait("S", "T", st["pnorm_war"].get(s % 3, 0))
        B.wait("S", "S", war_sim.get(("S", s % 3), 0))
        B.emit("S", lambda e, r=r, s3=s % 3, ncols=ncols: e.activation(
            out=sb[f"pnorm{s3}"][:, 0:ncols], in_=sb[f"pbuf{s3}"][:, 0:ncols],
            func=AF.Copy, scale=sb["rrec"][:, r:r + 1]), inc="S")
        st["rrec_war"][r] = B.tok("S")
        sim_info[s] = (B.tok("S"), ncols)

    def emit_tail(hf, flavor, s, sub, sim_info, war_sim, st, pv_done):
        norm_tok, ncols = sim_info.pop(s)
        nkt = ncols // 128
        s3 = s % 3
        if flavor == "inner":
            wi, h, qh = sub
            gw = 4 * hf + wi
            if gw == 0:
                if qh == 0:
                    blocks = [(0, 0)]
                else:
                    blocks = [(0, 128), (1, 256)]
            else:
                if qh == 0:
                    blocks = [(0, 0), (1, 128), (2, 384)]
                else:
                    blocks = [(0, 256), (1, 512), (2, 640)]
        else:
            qt, h, _ = sub
            blocks = [(j, 128 * j) for j in range(nkt)]
        B.wait("T", "S", norm_tok)
        first = True
        for jj, col in blocks:
            sl = st["pt_seq"] % 2
            st["pt_seq"] += 1
            bk = ps["G"] if sl == 0 else ps["H"]
            B.wait("T", "V", st["pt_sw"].get(sl, 0))
            B.emit("T", lambda e, s3=s3, jj=jj, bk=bk: e.matmul(
                out=bk[:, 0:128],
                lhsT=sb[f"pnorm{s3}"][:, 128 * jj:128 * (jj + 1)], rhs=I16(),
                start=True, stop=True), inc="T")
            trt = B.tok("T")
            B.wait("V", "T", trt)
            if first:
                B.wait("V", "T", st["war_ptb"])
                first = False
            B.emit("V", lambda e, col=col, bk=bk: e.tensor_copy(
                out=sb["ptb"][:, col:col + 128], in_=bk[:, 0:128]), inc="V")
            st["pt_sw"][sl] = B.tok("V")
        st["pnorm_war"][s3] = B.tok("T")
        evt = B.tok("V")
        do_pv = (flavor == "outer") or (qh == 1)
        if do_pv:
            par = h % 2
            pvbank = ps["D"] if par == 0 else ps["E"]
            B.wait("T", "V", evt)
            B.wait("T", "S", pv_done[par])
            if flavor == "inner":
                wi, h, _ = sub
                gw = 4 * hf + wi
                if gw == 0:
                    mms = [(0, 0, 256, 0), (1, 256, 128, 128)]
                else:
                    mms = [(1, 128, 256, 0), (0, 0, 128, 0), (2, 384, 256, 0),
                           (3, 640, 128, 128)]
                for mi, (vo, ptcol, wdt, oc) in enumerate(mms):
                    vidx = 2 * wi + vo if gw != 0 else 2 + vo
                    B.emit("T", lambda e, vidx=vidx, h=h, ptcol=ptcol, wdt=wdt, oc=oc,
                           pvbank=pvbank, st=(mi == 0), sp=(mi == len(mms) - 1):
                           e.matmul(out=pvbank[0:64, oc:oc + wdt],
                                    lhsT=sb[f"vbuf{vidx}"][:, 64 * h:64 * h + 64],
                                    rhs=sb["ptb"][:, ptcol:ptcol + wdt],
                                    start=st, stop=sp), inc="T")
            else:
                qt, h, _ = sub
                gqt = 8 * hf + qt
                j0 = max(0, gqt - 2)
                for mi in range(nkt):
                    vidx = (j0 + mi) - 8 * hf + 2
                    B.emit("T", lambda e, vidx=vidx, h=h, mi=mi, pvbank=pvbank,
                           st=(mi == 0), sp=(mi == nkt - 1):
                           e.matmul(out=pvbank[0:64, 0:128],
                                    lhsT=sb[f"vbuf{vidx}"][:, 64 * h:64 * h + 64],
                                    rhs=sb["ptb"][:, 128 * mi:128 * (mi + 1)],
                                    start=st, stop=sp), inc="T")
            pvt = B.tok("T")
            if par == 1:
                width = 256 if flavor == "inner" else 128
                if flavor == "inner":
                    oc0 = 256 * sub[0]
                else:
                    oc0 = 128 * sub[0]
                m = h // 2
                B.wait("S", "T", pvt)
                for pq, bkn in ((0, "D"), (64, "E")):
                    B.emit("S", lambda e, m=m, oc0=oc0, width=width, pq=pq, bkn=bkn:
                           e.copy(out=sb[f"otbuf{m}"][pq:pq + 64, oc0:oc0 + width],
                                  in_=ps[bkn][0:64, 0:width]), inc="S")
                pv_done[0] = B.tok("S")
                pv_done[1] = B.tok("S")
        st["war_ptb"] = B.tok("T")

    def wo_stage(hf, i, flavor):
        T0 = 1024 * hf
        conv_tok = {0: 0, 1: 0}
        use_tok = {0: 0, 1: 0}
        add_tok = {0: 0, 1: 0}
        for ept in range(NPT):
            par = ept % 2
            B.wait("Y", "G", conv_tok[par])
            if flavor == "inner":
                B.dma("Y", sb[f"woland{par}"][:].rearrange("p (a n) -> p a n", n=128),
                      dr["fwo"].ap()[i, :, 128 * ept:128 * (ept + 1)].rearrange(
                          "(a p) n -> p a n", p=128), ch=f"wo{par}")
            else:
                B.dma("Y", sb[f"woland{par}"][:],
                      dr["wo"].ap()[128 * ept:128 * (ept + 1), :], ch=f"wo{par}")
            B.dwait("G", f"wo{par}")
            B.wait("G", "T", use_tok[par])
            B.emit("G", lambda e, par=par: e.tensor_copy(
                out=sb[f"wobf{par}"][:], in_=sb[f"woland{par}"][:]), inc="G")
            conv_tok[par] = B.tok("G")
            gt = B.tok("G")
            if flavor == "outer":
                B.wait("T", "G", gt)
                for dt in range(NPT):
                    bk = ps[("C", "D", "E", "F")[dt % 4]]
                    B.wait("T", "V", wo_trwar.get((par, dt % 4), 0))
                    B.emit("T", lambda e, par=par, dt=dt, bk=bk: e.matmul(
                        out=bk[:, 0:128],
                        lhsT=sb[f"wobf{par}"][:, 128 * dt:128 * (dt + 1)],
                        rhs=I16(), start=True, stop=True), inc="T")
                    trt = B.tok("T")
                    B.wait("V", "T", trt)
                    B.emit("V", lambda e, par=par, dt=dt, bk=bk: e.tensor_copy(
                        out=sb[f"wT{par}"][:, 128 * dt:128 * (dt + 1)],
                        in_=bk[:, 0:128]), inc="V")
                    wo_trwar[(par, dt % 4)] = B.tok("V")
                gt = B.tok("V")
                lhsbuf = sb[f"wT{par}"]
                wsem = "V"
            else:
                lhsbuf = sb[f"wobf{par}"]
                wsem = "G"
            for tc in range(2):
                bank = ps["A"] if tc == 0 else ps["B"]
                B.wait("T", wsem, gt)
                B.wait("T", "V", add_tok[tc])
                for ipt in range(NPT):
                    B.emit("T", lambda e, lhsbuf=lhsbuf, ipt=ipt, tc=tc, bank=bank:
                           e.matmul(out=bank[:, 0:512],
                                    lhsT=lhsbuf[:, 128 * ipt:128 * (ipt + 1)],
                                    rhs=sb[f"otbuf{ipt}"][:, 512 * tc:512 * (tc + 1)],
                                    start=(ipt == 0), stop=(ipt == NPT - 1)), inc="T")
                mmt = B.tok("T")
                B.wait("V", "T", mmt)
                B.emit("V", lambda e, ept=ept, tc=tc, T0=T0, bank=bank: e.tensor_tensor(
                    out=Z(ept)[:, ZOFF + T0 + 512 * tc:ZOFF + T0 + 512 * (tc + 1)],
                    in0=Z(ept)[:, ZOFF + T0 + 512 * tc:ZOFF + T0 + 512 * (tc + 1)],
                    in1=bank[:, 0:512], op=ALU.add), inc="V")
                add_tok[tc] = B.tok("V")
            use_tok[par] = B.tok("T")

    wo_trwar = {}

    def outer_qkv(hf):
        T0 = 1024 * hf
        conv_tok = {0: 0, 1: 0}
        use_tok = {0: 0, 1: 0}
        ev_tok = {0: 0, 1: 0}
        seq = 0
        for ec in range(CFG.get("oqkv_ecs", NPT)):
            for wn, kind in (("wq", "q"), ("wk", "k"), ("wv", "v")):
                if kind not in CFG.get("oqkv_kinds", "qkv"):
                    continue
                par = seq % 2
                seq += 1
                steps = CFG.get("oqkv_steps", 7)
                B.wait("Y", "G", conv_tok[par])
                B.dma("Y", sb[f"woland{par}"][:],
                      dr[wn].ap()[128 * ec:128 * (ec + 1), :], ch=f"wo{par}")
                B.dwait("G", f"wo{par}")
                B.wait("G", "T", use_tok[par])
                if kind == "q":
                    B.emit("G", lambda e, par=par: e.tensor_scalar_mul(
                        out=sb[f"wobf{par}"][:], in0=sb[f"woland{par}"][:],
                        scalar1=0.125), inc="G")
                else:
                    B.emit("G", lambda e, par=par: e.tensor_copy(
                        out=sb[f"wobf{par}"][:], in_=sb[f"woland{par}"][:]), inc="G")
                conv_tok[par] = B.tok("G")
                gt = B.tok("G")
                if steps < 2:
                    continue
                B.wait("T", "G", gt)
                for dt in range(NPT):
                    bk = ps[("C", "D", "E", "F")[dt % 4]]
                    B.wait("T", "V", wo_trwar.get((par, dt % 4), 0))
                    B.emit("T", lambda e, par=par, dt=dt, bk=bk: e.matmul(
                        out=bk[:, 0:128],
                        lhsT=sb[f"wobf{par}"][:, 128 * dt:128 * (dt + 1)],
                        rhs=I16(), start=True, stop=True), inc="T")
                    trt = B.tok("T")
                    B.wait("V", "T", trt)
                    B.emit("V", lambda e, par=par, dt=dt, bk=bk: e.tensor_copy(
                        out=sb[f"wT{par}"][:, 128 * dt:128 * (dt + 1)],
                        in_=bk[:, 0:128]), inc="V")
                    wo_trwar[(par, dt % 4)] = B.tok("V")
                wtt = B.tok("V")
                if steps < 4:
                    continue
                if kind in ("q", "k"):
                    for tc in range(2):
                        bp = tc % 2
                        bank = ps["A"] if bp == 0 else ps["B"]
                        B.wait("T", "V", wtt)
                        B.wait("T", "V", ev_tok[bp])
                        for dt in range(NPT):
                            B.emit("T", lambda e, par=par, dt=dt, tc=tc, T0=T0, bank=bank:
                                   e.matmul(out=bank[:, 0:512],
                                            lhsT=sb[f"wT{par}"][:, 128 * dt:128 * (dt + 1)],
                                            rhs=sb[f"xn{dt}"][:, T0 + 512 * tc:
                                                              T0 + 512 * (tc + 1)],
                                            start=(dt == 0), stop=(dt == NPT - 1)), inc="T")
                        mmt = B.tok("T")
                        B.wait("V", "T", mmt)
                        if kind == "q":
                            dst = qview("outer", hf, ec)[:, 512 * tc:512 * (tc + 1)]
                        else:
                            dst = sb[f"kbuf{ec}"][:, 256 + 512 * tc:256 + 512 * (tc + 1)]
                        B.emit("V", lambda e, dst=dst, bank=bank: e.tensor_copy(
                            out=dst, in_=bank[:, 0:512]), inc="V")
                        ev_tok[bp] = B.tok("V")
                    use_tok[par] = B.tok("T")
                else:
                    for tt in range(8):
                        bp = tt % 2
                        bank = ps["A"] if bp == 0 else ps["B"]
                        B.wait("T", "V", wtt)
                        B.wait("T", "V", ev_tok[bp])
                        for dt in range(NPT):
                            B.emit("T", lambda e, par=par, dt=dt, tt=tt, T0=T0, bank=bank:
                                   e.matmul(out=bank[:, 0:128],
                                            lhsT=sb[f"xn{dt}"][:, T0 + 128 * tt:
                                                               T0 + 128 * (tt + 1)],
                                            rhs=sb[f"wT{par}"][:, 128 * dt:128 * (dt + 1)],
                                            start=(dt == 0), stop=(dt == NPT - 1)), inc="T")
                        mmt = B.tok("T")
                        B.wait("V", "T", mmt)
                        B.emit("V", lambda e, tt=tt, ec=ec, bank=bank: e.tensor_copy(
                            out=sb[f"vbuf{2 + tt}"][:, 128 * ec:128 * (ec + 1)],
                            in_=bank[:, 0:128]), inc="V")
                        ev_tok[bp] = B.tok("V")
                    use_tok[par] = B.tok("T")

    def ffn(i):
        load_ffn_params(i)
        ln_full(use_shift=True, affine=True, dst="xn")
        groups = [(0, 6), (6, 12), (12, 18), (18, 22)]
        conv_tok = 0
        mmuse_tok = 0
        for g0, g1 in groups:
            bias_war = 0
            ag_cons = {0: 0, 1: 0}
            act_v = 0
            for j in range(g0, g1):
                w = min(128, DI - 128 * j)
                slot = j - g0
                par = j % 2
                B.wait("T", "S", bias_war)
                B.emit("T", lambda e, j=j, w=w: e.matmul(
                    out=ps["F"][0:w, 0:1], lhsT=sb["b1row"][:, 128 * j:128 * j + w],
                    rhs=sb["ones1"][:], start=True, stop=True), inc="T")
                B.emit("T", lambda e, j=j, w=w: e.matmul(
                    out=ps["E"][0:w, 0:1], lhsT=sb["b1row"][:, DI + 128 * j:DI + 128 * j + w],
                    rhs=sb["ones1"][:], start=True, stop=True), inc="T")
                tbt = B.tok("T")
                B.wait("V", "T", tbt)
                B.emit("V", lambda e, par=par, w=w: e.tensor_copy(
                    out=sb["b1c"][0:w, 2 * par:2 * par + 1], in_=ps["F"][0:w, 0:1]), inc="V")
                B.emit("V", lambda e, par=par, w=w: e.tensor_copy(
                    out=sb["b1c"][0:w, 2 * par + 1:2 * par + 2], in_=ps["E"][0:w, 0:1]), inc="V")
                b1v = B.tok("V")
                B.wait("Y", "G", conv_tok)
                B.dma("Y", sb["w1land"][:, 0:8 * w].rearrange("p (a n) -> p a n", n=w),
                      dr["fw1"].ap()[i, :, 128 * j:128 * j + w].rearrange(
                          "(a p) n -> p a n", p=128), ch="ffnw")
                B.dma("Y", sb["w1land"][:, 1024:1024 + 8 * w].rearrange(
                          "p (a n) -> p a n", n=w),
                      dr["fw1"].ap()[i, :, DI + 128 * j:DI + 128 * j + w].rearrange(
                          "(a p) n -> p a n", p=128), ch="ffnw")
                B.dma("Y", sb["w2land"][0:w, :],
                      dr["fw2"].ap()[i, 128 * j:128 * j + w, :], ch="ffnw")
                B.dwait("G", "ffnw")
                B.wait("G", "T", mmuse_tok)
                B.emit("G", lambda e, w=w: e.tensor_copy(
                    out=sb["w1abf"][:, 0:8 * w], in_=sb["w1land"][:, 0:8 * w]), inc="G")
                B.emit("G", lambda e, w=w: e.tensor_copy(
                    out=sb["w1gbf"][:, 0:8 * w],
                    in_=sb["w1land"][:, 1024:1024 + 8 * w]), inc="G")
                B.emit("G", lambda e, slot=slot, w=w: e.tensor_copy(
                    out=sb[f"w2bf{slot}"][0:w, :], in_=sb["w2land"][0:w, :]), inc="G")
                conv_tok = B.tok("G")
                gt = B.tok("G")
                for tc in range(4):
                    tp = tc % 2
                    abank = ps["A"] if tp == 0 else ps["B"]
                    gbank = ps["C"] if tp == 0 else ps["G"]
                    B.wait("T", "G", gt)
                    B.wait("T", "S", ag_cons[tp])
                    B.wait("T", "V", act_v)
                    for dt in range(NPT):
                        B.emit("T", lambda e, dt=dt, tc=tc, w=w, abank=abank:
                               e.matmul(out=abank[0:w, 0:512],
                                        lhsT=sb["w1abf"][:, w * dt:w * (dt + 1)],
                                        rhs=sb[f"xn{dt}"][:, 512 * tc:512 * (tc + 1)],
                                        start=(dt == 0), stop=(dt == NPT - 1)), inc="T")
                    for dt in range(NPT):
                        B.emit("T", lambda e, dt=dt, tc=tc, w=w, gbank=gbank:
                               e.matmul(out=gbank[0:w, 0:512],
                                        lhsT=sb["w1gbf"][:, w * dt:w * (dt + 1)],
                                        rhs=sb[f"xn{dt}"][:, 512 * tc:512 * (tc + 1)],
                                        start=(dt == 0), stop=(dt == NPT - 1)), inc="T")
                    mmt = B.tok("T")
                    B.wait("S", "T", mmt)
                    if tc == 0:
                        B.wait("S", "V", b1v)
                    B.wait("S", "V", act_v)
                    B.emit("S", lambda e, tp=tp, par=par, w=w, gbank=gbank: e.activation(
                        out=sb[f"ggel{tp}"][0:w, :], in_=gbank[0:w, 0:512],
                        func=AF.Gelu, bias=sb["b1c"][0:w, 2 * par + 1:2 * par + 2]), inc="S")
                    ag_cons[tp] = B.tok("S")
                    st = B.tok("S")
                    B.wait("V", "S", st)
                    B.emit("V", lambda e, slot=slot, tc=tc, tp=tp, w=w, par=par,
                           abank=abank: e.scalar_tensor_tensor(
                        out=sb[f"act{slot}"][0:w, 512 * tc:512 * (tc + 1)],
                        in0=abank[0:w, 0:512], scalar=sb["b1c"][0:w, 2 * par:2 * par + 1],
                        in1=sb[f"ggel{tp}"][0:w, :], op0=ALU.add, op1=ALU.mult), inc="V")
                    act_v = B.tok("V")
                mmuse_tok = B.tok("T")
                bias_war = B.tok("S")
            B.wait("T", "V", act_v)
            B.wait("T", "G", B.tok("G"))
            add_tok = {0: 0, 1: 0}
            for ept in range(NPT):
                for tc in range(4):
                    bp = (ept * 4 + tc) % 2
                    bank = ps["A"] if bp == 0 else ps["B"]
                    B.wait("T", "V", add_tok[bp])
                    for j in range(g0, g1):
                        w = min(128, DI - 128 * j)
                        slot = j - g0
                        B.emit("T", lambda e, slot=slot, ept=ept, tc=tc, w=w, bank=bank,
                               st=(slot == 0), sp=(j == g1 - 1), j=j:
                               e.matmul(out=bank[:, 0:512],
                                        lhsT=sb[f"w2bf{slot}"][0:w, 128 * ept:128 * (ept + 1)],
                                        rhs=sb[f"act{slot}"][0:w, 512 * tc:512 * (tc + 1)],
                                        start=st, stop=sp), inc="T")
                    mmt = B.tok("T")
                    B.wait("V", "T", mmt)
                    if g0 == 0:
                        B.emit("V", lambda e, ept=ept, tc=tc, bank=bank:
                               e.scalar_tensor_tensor(
                                   out=Z(ept)[:, ZOFF + 512 * tc:ZOFF + 512 * (tc + 1)],
                                   in0=bank[:, 0:512], scalar=sb["b2col"][:, ept:ept + 1],
                                   in1=Z(ept)[:, ZOFF + 512 * tc:ZOFF + 512 * (tc + 1)],
                                   op0=ALU.add, op1=ALU.add), inc="V")
                    else:
                        B.emit("V", lambda e, ept=ept, tc=tc, bank=bank: e.tensor_tensor(
                            out=Z(ept)[:, ZOFF + 512 * tc:ZOFF + 512 * (tc + 1)],
                            in0=Z(ept)[:, ZOFF + 512 * tc:ZOFF + 512 * (tc + 1)],
                            in1=bank[:, 0:512], op=ALU.add), inc="V")
                    add_tok[bp] = B.tok("V")
            B.phase()

    def spill():
        for p in range(NPT):
            B.dma("Y", dr["xmid"].ap().rearrange("t d -> (t d)")[128 * T * p:128 * T * (p + 1)]
                  .rearrange("(q t) -> q t", q=128), Z(p)[:, ZOFF:ZOFF + T], ch="spill")
        B.dwait("Y", "spill")
        B.phase()

    def final_out():
        for p in range(NPT if CFG.get("do_spill", True) else 0):
            for hh in range(2):
                par = (2 * p + hh) % 2
                B.wait("Y", "V", fin_war.get(par, 0))
                B.dma("Y", sb[f"xland{par}"][:],
                      dr["xmid"].ap().rearrange("t d -> (t d)")
                      [128 * T * p:128 * T * (p + 1)]
                      .rearrange("(q t) -> q t", q=128)[:, 1024 * hh:1024 * (hh + 1)],
                      ch=f"fin{par}")
                B.dwait("V", f"fin{par}")
                B.emit("V", lambda e, p=p, hh=hh, par=par: e.tensor_tensor(
                    out=Z(p)[:, ZOFF + 1024 * hh:ZOFF + 1024 * (hh + 1)],
                    in0=Z(p)[:, ZOFF + 1024 * hh:ZOFF + 1024 * (hh + 1)],
                    in1=sb[f"xland{par}"][:], op=ALU.add), inc="V")
                fin_war[par] = B.tok("V")
        B.phase()
        regwar = {}
        for tt in range(NTT):
            par = tt % 2
            obuf = sb[f"outb{par}"]
            B.dwait("T", f"st{par}", st_tok.get(par, 0))
            for p in range(NPT):
                bk = ("A", "B", "C", "D")[p % 4]
                bank = ps[bk]
                B.wait("T", "V", regwar.get(bk, 0))
                B.emit("T", lambda e, p=p, tt=tt, bank=bank: e.matmul(
                    out=bank[:, 0:128],
                    lhsT=Z(p)[:, ZOFF + 128 * tt:ZOFF + 128 * (tt + 1)],
                    rhs=sb["ident32"][:], start=True, stop=True), inc="T")
                tts = B.tok("T")
                B.wait("V", "T", tts)
                B.emit("V", lambda e, p=p, obuf=obuf, bank=bank: e.tensor_copy(
                    out=obuf[:, 128 * p:128 * (p + 1)], in_=bank[:, 0:128]), inc="V")
                regwar[bk] = B.tok("V")
            vt = B.tok("V")
            B.wait("Y", "V", vt)
            B.dma("Y", dr["y"].ap()[128 * tt:128 * (tt + 1), :], obuf[:], ch=f"st{par}")
            st_tok[par] = B.dtok(f"st{par}")
        B.wait_all_dma("Y")
        B.wait_all_dma("G")

    fin_war = {}
    st_tok = {}

    cut = CFG.get("cut", "")
    load_consts()
    if cut == "consts":
        B.dwait("V", "init")
        B.emit("V", lambda e: e.tensor_copy(out=sb["outb0"][:, 0:128],
                                            in_=sb["ident32"][:]), inc="V")
        B.wait("Y", "V", B.tok("V"))
        B.dma("Y", dr["y"].ap()[0:128, 0:128], sb["outb0"][:, 0:128], ch="st0")
        B.wait_all_dma("Y")
        with nc.Block() as block:
            B.finalize(block)
        return nc
    load_x()
    if cut == "loadx":
        B.emit("V", lambda e: e.tensor_copy(out=sb["outb0"][:], in_=Z(0)[:, ZOFF:ZOFF + 1024]),
               inc="V")
        B.wait("Y", "V", B.tok("V"))
        B.dma("Y", dr["y"].ap()[0:128, :], sb["outb0"][:], ch="st0")
        B.wait_all_dma("Y")
        with nc.Block() as block:
            B.finalize(block)
        return nc
    def _cut_store(src_ap):
        B.wait("V", "T", B.tok("T"))
        B.wait("V", "S", B.tok("S"))
        B.wait("V", "G", B.tok("G"))
        B.emit("V", lambda e: e.tensor_copy(out=sb["outb0"][:], in_=src_ap), inc="V")
        B.wait("Y", "V", B.tok("V"))
        B.dma("Y", dr["y"].ap()[0:128, :], sb["outb0"][:], ch="st0")
        B.wait_all_dma("Y")
        with nc.Block() as block:
            B.finalize(block)

    if CFG["do_outer"]:
        for hf in (0, 1):
            ln_full(use_shift=False, affine=False, dst="xn", tcs=(2 * hf, 2 * hf + 1))
            if cut == "oln" and hf == 0:
                _cut_store(sb["xn0"][:, 0:1024])
                return nc
            if hf == 1:
                boundary_copies()
                B.phase()
            outer_qkv(hf)
            B.phase()
            if cut == "oqkv0" and hf == 0:
                _cut_store(qview("outer", 0, 0))
                return nc
            if cut == "oqkv" and hf == 1:
                _cut_store(qview("outer", 1, 0))
                return nc
            attn_loop(hf, "outer")
            B.phase()
            if cut == "oattn":
                _cut_store(sb["otbuf0"][:, 0:1024])
                return nc
            wo_stage(hf, None, "outer")
            B.phase()
    if CFG.get("do_spill", True):
        spill()
        ln_full(use_shift=False, affine=False, dst="z")
    if cut == "z0":
        _cut_store(Z(0)[:, ZOFF:ZOFF + 1024])
        return nc
    if CFG["do_inner"]:
        for i in range(CFG["depth"]):
            load_attn_params(i)
            for hf in (0, 1):
                half_tables(hf)
                if hf == 1:
                    boundary_copies()
                build_tshift(hf)
                B.phase()
                inner_qkv(i, hf)
                B.phase()
                if cut == "iqkv" and hf == 0:
                    _cut_store(qview("inner", hf, 0))
                    return nc
                attn_loop(hf, "inner")
                B.phase()
                if cut == "iattn" and hf == 0:
                    _cut_store(sb["otbuf0"][:, 0:1024])
                    return nc
                wo_stage(hf, i, "inner")
                B.phase()
                if cut == "izh0" and hf == 0:
                    _cut_store(Z(0)[:, ZOFF:ZOFF + 1024])
                    return nc
            if cut == "izatt":
                _cut_store(Z(0)[:, ZOFF:ZOFF + 1024])
                return nc
            ffn(i)
    final_out()

    with nc.Block() as block:
        B.finalize(block)
    return nc


_NC_CACHE = {}


def _get_nc():
    key = (CFG["depth"], CFG["do_outer"], CFG["do_inner"], CFG["barrier_level"],
           CFG.get("do_spill", True), CFG.get("cut", ""))
    if key not in _NC_CACHE:
        nc = bass.Bass("TRN2", target_bir_lowering=False, dynamic_dma_scratch_size=512)
        build_program(nc)
        _NC_CACHE[key] = nc
    return _NC_CACHE[key]


def kernel(L0_attn_hidden, L0_attn_Wq, L0_attn_Wk, L0_attn_Wv, L0_attn_Wo,
           L0_ffn_Wqkv, L0_ffn_q_scale, L0_ffn_k_scale, L0_ffn_Wo, L0_ffn_ln_g,
           L0_ffn_ln_b, L0_ffn_W1, L0_ffn_b1, L0_ffn_W2, L0_ffn_b2):
    nc = _get_nc()
    weights = {
        "wq": np.ascontiguousarray(L0_attn_Wq, np.float32),
        "wk": np.ascontiguousarray(L0_attn_Wk, np.float32),
        "wv": np.ascontiguousarray(L0_attn_Wv, np.float32),
        "wo": np.ascontiguousarray(L0_attn_Wo, np.float32),
        "fwqkv": np.ascontiguousarray(L0_ffn_Wqkv, np.float32),
        "fqs": np.ascontiguousarray(L0_ffn_q_scale, np.float32),
        "fks": np.ascontiguousarray(L0_ffn_k_scale, np.float32),
        "fwo": np.ascontiguousarray(L0_ffn_Wo, np.float32),
        "flng": np.ascontiguousarray(L0_ffn_ln_g, np.float32),
        "flnb": np.ascontiguousarray(L0_ffn_ln_b, np.float32),
        "fw1": np.ascontiguousarray(L0_ffn_W1, np.float32),
        "fb1": np.ascontiguousarray(L0_ffn_b1, np.float32),
        "fw2": np.ascontiguousarray(L0_ffn_W2, np.float32),
        "fb2": np.ascontiguousarray(L0_ffn_b2, np.float32),
    }
    x = np.ascontiguousarray(L0_attn_hidden, np.float32)
    in_maps = [dict(weights, x=x[b]) for b in range(8)]
    res = run_bass_kernel_spmd(nc, in_maps, core_ids=list(range(8)))
    out = np.stack([res.results[b]["y"] for b in range(8)], 0)
    return out.astype(np.float32)


# version 04
# speedup vs baseline: 1.1064x; 1.1064x over previous
import numpy as np
import ml_dtypes
import concourse.bass as bass
import concourse.mybir as mybir
from concourse.bass_utils import run_bass_kernel_spmd

F32 = mybir.dt.float32
BF16 = mybir.dt.bfloat16
U16 = mybir.dt.uint16
AF = mybir.ActivationFunctionType
ALU = mybir.AluOpType
AX = mybir.AxisListType
BF = ml_dtypes.bfloat16

D, T, H, DH, WIN, DEPTH, DI = 1024, 2048, 16, 64, 256, 6, 2730
QKS = 8.0
NPT, NTT = 8, 16
ZOFF = 1
ZW = T + ZOFF
NEG = -1.0e30

CFG = {"depth": DEPTH, "do_outer": True, "do_inner": True, "barrier_level": 1,
       "do_spill": True}


def rotary_tables():
    half = np.arange(0, DH, 2, dtype=np.float64)
    inv_freq = 1.0 / 10000.0 ** (half / DH)
    t = np.arange(T, dtype=np.float64)
    freqs = t[:, None] * inv_freq[None, :]
    freqs = np.concatenate([freqs, freqs], -1)
    base = (half + 0.4 * DH) / (1.4 * DH)
    power = (t - T // 2) / (WIN // 2)
    scale = base[None, :] ** power[:, None]
    scale = np.concatenate([scale, scale], -1)
    cos, sin = np.cos(freqs), np.sin(freqs)
    sgn = np.ones((1, DH))
    sgn[:, : DH // 2] = -1.0
    c1q = QKS * cos * scale
    c2q = QKS * sin * scale * sgn
    c1k = cos / scale
    c2k = sin / scale * sgn
    return [np.ascontiguousarray(a, dtype=np.float32) for a in (c1q, c2q, c1k, c2k)]


def mask_tiles():
    u = np.arange(128)[:, None]
    tt = np.arange(128)[None, :]
    L = np.where(tt <= u, 0.0, NEG)
    Ui = np.where(tt >= u, 0.0, NEG)
    Us = np.where(tt > u, 0.0, NEG)
    Zt = np.zeros((128, 128))
    inner = np.concatenate([Ui, Zt, L], 1).astype(BF)
    outer = np.concatenate([Us, Zt, L], 1).astype(BF)
    zl = np.concatenate([Zt, L], 1).astype(BF)
    return inner, outer, zl


class Bld:
    def __init__(self, nc):
        self.nc = nc
        self.q = {k: [] for k in ("T", "V", "S", "G", "Y")}
        self.sem = {}
        self.cnt = {}
        self.cleared = {}
        for name in ("T", "V", "S", "G", "BAR"):
            self.sem[name] = nc.alloc_semaphore("sem_" + name)
            self.cnt[name] = 0

    def _ch(self, ch):
        name = "DW_" + ch
        if name not in self.sem:
            self.sem[name] = self.nc.alloc_semaphore("sem_" + name)
            self.cnt[name] = 0
        return name

    def _app(self, eng, fn):
        self.q[eng].append(fn)

    def wait(self, eng, name, val=None):
        v = self.cnt[name] if val is None else val
        v -= self.cleared.get(name, 0)
        if v <= 0:
            return
        h = self.sem[name]
        self._app(eng, lambda e, h=h, v=v: e.wait_ge(h, v))

    def tok(self, name):
        return self.cnt[name]

    def emit(self, eng, fn, inc=None, by=1):
        if inc is not None:
            self.cnt[inc] += by
            h = self.sem[inc]
            self._app(eng, lambda e, fn=fn, h=h, by=by: fn(e).then_inc(h, by))
            return self.cnt[inc]
        self._app(eng, fn)
        return None

    def dma(self, eng, out_ap, in_ap, ch="misc"):
        return self.emit(eng, lambda e, o=out_ap, i=in_ap: e.dma_start(out=o, in_=i),
                         inc=self._ch(ch), by=16)

    def dtok(self, ch):
        return self.cnt[self._ch(ch)]

    def dwait(self, eng, ch, val=None):
        self.wait(eng, self._ch(ch), val)

    def wait_all_dma(self, eng):
        for name in list(self.sem):
            if name.startswith("DW_"):
                self.wait(eng, name)

    def barrier(self):
        if not hasattr(self, "_bar_dw"):
            self._bar_dw = {}
        for name in ("T", "V", "S"):
            self.wait("G", name)
        for name in list(self.sem):
            if name.startswith("DW_") and self.cnt[name] > self._bar_dw.get(name, -1):
                self.wait("G", name)
                self._bar_dw[name] = self.cnt[name]
        hG = self.sem["G"]
        self.cnt["G"] += 1
        self._app("G", lambda e, h=hG: e.sem_inc(h, 1))
        g1 = self.cnt["G"]
        for eng in ("T", "V", "S"):
            self.wait(eng, "G", g1)
            h = self.sem[eng]
            self.cnt[eng] += 1
            self._app(eng, lambda e, h=h: e.sem_inc(h, 1))
        self.wait("Y", "G", g1)
        hB = self.sem["BAR"]
        self.cnt["BAR"] += 1
        self._app("Y", lambda e, h=hB: e.sem_inc(h, 1))
        for name in ("T", "V", "S", "BAR"):
            self.wait("G", name)
        for name in ("T", "V", "S"):
            h = self.sem[name]
            self._app("G", lambda e, h=h: e.sem_clear(h))
            self.cleared[name] = self.cnt[name]
        self.cnt["G"] += 1
        self._app("G", lambda e, h=hG: e.sem_inc(h, 1))
        g2 = self.cnt["G"]
        for eng in ("T", "V", "S", "Y"):
            self.wait(eng, "G", g2)

    def phase(self):
        if CFG["barrier_level"] >= 1:
            self.barrier()

    def finalize(self, block):
        qq = self.q

        @block.tensor
        def _(e):
            for fn in qq["T"]:
                fn(e)

        @block.vector
        def _(e):
            for fn in qq["V"]:
                fn(e)

        @block.scalar
        def _(e):
            for fn in qq["S"]:
                fn(e)

        @block.gpsimd
        def _(e):
            for fn in qq["G"]:
                fn(e)

        @block.sync
        def _(e):
            for fn in qq["Y"]:
                fn(e)


def build_program(nc, ext_dr=None):
    if ext_dr is not None:
        dr = dict(ext_dr)
    else:
        dr = {}
        dr["x"] = nc.dram_tensor("x", [T, D], F32, kind="ExternalInput")
        for n, sh in (("wq", [D, D]), ("wk", [D, D]), ("wv", [D, D]), ("wo", [D, D]),
                      ("fwqkv", [DEPTH, D, 3 * D]), ("fqs", [DEPTH, DH]),
                      ("fks", [DEPTH, DH]),
                      ("fwo", [DEPTH, D, D]), ("flng", [DEPTH, D]), ("flnb", [DEPTH, D]),
                      ("fw1", [DEPTH, D, 2 * DI]), ("fb1", [DEPTH, 2 * DI]),
                      ("fw2", [DEPTH, DI, D]), ("fb2", [DEPTH, D])):
            dr[n] = nc.dram_tensor(n, sh, F32, kind="ExternalInput")
        dr["y"] = nc.dram_tensor("y", [T, D], F32, kind="ExternalOutput")
    dr["xmid"] = nc.dram_tensor("xmid", [T, D], F32, kind="Internal")

    c1q, c2q, c1k, c2k = rotary_tables()
    m_inner, m_outer, m_zl = mask_tiles()
    co = {}
    co["c1q"] = nc.inline_tensor(c1q, name="c1q")
    co["c2q"] = nc.inline_tensor(c2q, name="c2q")
    co["c1k"] = nc.inline_tensor(c1k, name="c1k")
    co["c2k"] = nc.inline_tensor(c2k, name="c2k")
    co["m_inner"] = nc.inline_tensor(m_inner.view(np.uint16), name="mask_inner")
    co["m_outer"] = nc.inline_tensor(m_outer.view(np.uint16), name="mask_outer")
    co["m_zl"] = nc.inline_tensor(m_zl.view(np.uint16), name="mask_zl")
    co["i16"] = nc.inline_tensor(np.eye(128, dtype=BF).view(np.uint16), name="ident16")
    co["i32"] = nc.inline_tensor(np.eye(128, dtype=np.float32), name="ident32")
    co["onesrow"] = nc.inline_tensor(np.ones((1, 128), np.float32), name="onesrow")
    co["onescol16"] = nc.inline_tensor(np.ones((128, 1), dtype=BF).view(np.uint16),
                                       name="onescol16")
    co["ones1"] = nc.inline_tensor(np.ones((1, 1), np.float32), name="ones1")
    co["epsc"] = nc.inline_tensor(np.full((1, 1), 1e-5, np.float32), name="epsc")

    sb = {}

    def alo(name, shape, dtype):
        sb[name] = nc.alloc_sbuf_tensor("s_" + name, list(shape), dtype)
        return sb[name]

    for p in range(NPT):
        alo(f"z{p}", [128, ZW], F32)
    for p in range(NPT):
        alo(f"xn{p}", [128, T], BF16)
    for n, sh, dt in (("ident16", [128, 128], BF16), ("ident32", [128, 128], F32),
                      ("m_inner", [128, 384], BF16), ("m_outer", [128, 384], BF16),
                      ("m_zl", [128, 256], BF16), ("onesrow", [1, 128], F32),
                      ("onescol16", [128, 1], BF16), ("ones1", [1, 1], F32),
                      ("epsc", [1, 1], F32),
                      ("qsrow", [1, DH], F32), ("ksrow", [1, DH], F32),
                      ("qs_sw", [1, DH], F32), ("ks_sw", [1, DH], F32),
                      ("qsB", [128, DH], F32), ("qsSB", [128, DH], F32),
                      ("ksB", [128, DH], F32), ("ksSB", [128, DH], F32),
                      ("gcol", [128, NPT], F32), ("bcol", [128, NPT], F32),
                      ("b2col", [128, NPT], F32), ("b1c", [128, 4], F32),
                      ("savecol", [128, 4], F32), ("nmax", [128, 8], F32),
                      ("rsum", [128, 8], F32), ("rrec", [128, 8], F32),
                      ("ssb", [128, 8], F32), ("rnb", [128, 8], F32)):
        alo(n, sh, dt)

    ARENA = 120 * 1024
    arena = nc.bump_sbuf(ARENA, "left")
    assert arena is not None, "arena bump failed"
    abase = arena[0]

    def alo_at(name, shape, dtype, off):
        sz = 1
        for d in shape[1:]:
            sz *= d
        sz *= 4 if dtype == F32 else 2
        assert off + sz <= ARENA, (name, off, sz)
        sb[name] = nc.alloc_sbuf_tensor_at("s_" + name, list(shape), dtype,
                                           offset=abase + off)
        return off + ((sz + 31) // 32) * 32

    o = 0
    for p in range(NPT):
        o = alo_at(f"kbuf{p}", [128, 1280], BF16, o)
    for i in range(10):
        o = alo_at(f"vbuf{i}", [128, 1024], BF16, o)
    for p in range(NPT):
        o = alo_at(f"otbuf{p}", [128, 1024], BF16, o)
    for i in range(2):
        o = alo_at(f"wland{i}", [128, 1024], F32, o)
        o = alo_at(f"wbf{i}", [128, 1024], BF16, o)
        o = alo_at(f"woland{i}", [128, 1024], F32, o)
        o = alo_at(f"wobf{i}", [128, 1024], BF16, o)
        o = alo_at(f"wT{i}", [128, 1024], BF16, o)
    for n in ("c1qh", "c2qh", "c1kh", "c2kh", "aq", "bq", "ak", "bk"):
        o = alo_at(n, [128, 512], F32, o)
    for i in range(3):
        o = alo_at(f"qraw{i}", [128, 128], F32, o)
        o = alo_at(f"qhat{i}", [128, 128], F32, o)
        o = alo_at(f"t1b{i}", [128, 128], F32, o)
        o = alo_at(f"t2b{i}", [128, 128], F32, o)
        o = alo_at(f"qpp{i}", [128, 128], BF16, o)
        o = alo_at(f"pbuf{i}", [128, 384], BF16, o)
        o = alo_at(f"pnorm{i}", [128, 384], BF16, o)
    o = alo_at("pbuf3", [128, 384], BF16, o)
    o = alo_at("pnorm3", [128, 384], BF16, o)
    o = alo_at("sqb0", [128, 128], F32, o)
    o = alo_at("sqb1", [128, 128], F32, o)
    o = alo_at("sqb2", [128, 128], F32, o)
    o = alo_at("ptb", [128, 768], BF16, o)
    assert o <= ARENA, f"attn set {o}"

    o = 0
    for i in range(6):
        o = alo_at(f"act{i}", [128, 2048], BF16, o)
    for i in range(6):
        o = alo_at(f"w2bf{i}", [128, 1024], BF16, o)
    for i in range(2):
        o = alo_at(f"w1land{i}", [128, 2048], F32, o)
        o = alo_at(f"w1abf{i}", [128, 1024], BF16, o)
        o = alo_at(f"w1gbf{i}", [128, 1024], BF16, o)
        o = alo_at(f"w2land{i}", [128, 1024], F32, o)
    o = alo_at("b1row", [1, 2 * DI], F32, o)
    o = alo_at("grow", [1, D], F32, o)
    o = alo_at("brow", [1, D], F32, o)
    o = alo_at("b2row", [1, D], F32, o)
    for n in ("srow", "srow2", "mrow", "vrow"):
        o = alo_at(n, [1, 512], F32, o)
    for i in range(2):
        o = alo_at(f"ggel{i}", [128, 512], BF16, o)
        o = alo_at(f"tb{i}", [128, 512], BF16, o)
        o = alo_at(f"tsq{i}", [128, 512], BF16, o)
        o = alo_at(f"s0buf{i}", [128, 512], F32, o)
    ffn_top = o
    assert o <= ARENA, f"ffn set {o}"

    o = 0
    for i in range(2):
        o = alo_at(f"xland{i}", [128, 1024], F32, o)
        o = alo_at(f"outb{i}", [128, 1024], F32, o)
    assert o <= ARENA, f"final set {o}"

    tshift = [sb[f"xn{p}"][:, 0:1024] for p in range(NPT)]
    qfull = [sb[f"xn{p}"][:, 1024:2048] for p in range(NPT)]

    def qview(flavor, hf, p):
        if flavor == "inner":
            return qfull[p]
        o = 1024 - 1024 * hf
        return sb[f"xn{p}"][:, o:o + 1024]

    ps = {}
    for n in ("A", "B", "C", "G", "H", "D", "E", "F"):
        ps[n] = nc.alloc_psum_tensor("ps" + n, [128, 512], F32)

    B = Bld(nc)
    Z = lambda p: sb[f"z{p}"]
    I16 = lambda: sb["ident16"][:]

    def load_consts():
        for n, c in (("m_inner", "m_inner"), ("m_outer", "m_outer"), ("m_zl", "m_zl"),
                     ("ident16", "i16"), ("onescol16", "onescol16")):
            B.dma("Y", sb[n][:].bitcast(U16), co[c].ap(), ch="init")
        for n, c in (("ident32", "i32"), ("onesrow", "onesrow"), ("ones1", "ones1"),
                     ("epsc", "epsc")):
            B.dma("Y", sb[n][:], co[c].ap(), ch="init")
        for eng in ("T", "V", "S", "G", "Y"):
            B.dwait(eng, "init")

    def load_x():
        for p in range(NPT):
            B.emit("V", lambda e, p=p: e.memset(Z(p)[:, 0:ZOFF], 0.0), inc="V")
        war = {0: 0, 1: 0}
        regwar = {}
        for tt in range(NTT):
            par = tt % 2
            buf = sb[f"xland{par}"]
            B.wait("Y", "V", war[par])
            B.dma("Y", buf[:], dr["x"].ap()[128 * tt:128 * (tt + 1), :], ch=f"x{par}")
            dwt = B.dtok(f"x{par}")
            B.dwait("T", f"x{par}", dwt)
            for p in range(NPT):
                bk = ("A", "B", "C", "D")[p % 4]
                bank = ps[bk]
                B.wait("T", "V", regwar.get(bk, 0))
                B.emit("T", lambda e, buf=buf, p=p, bank=bank:
                       e.matmul(out=bank[:, 0:128],
                                lhsT=buf[:, 128 * p:128 * (p + 1)],
                                rhs=sb["ident32"][:], start=True, stop=True), inc="T")
                tts = B.tok("T")
                B.wait("V", "T", tts)
                B.emit("V", lambda e, p=p, tt=tt, bank=bank:
                       e.tensor_copy(out=Z(p)[:, ZOFF + 128 * tt: ZOFF + 128 * (tt + 1)],
                                     in_=bank[:, 0:128]), inc="V")
                regwar[bk] = B.tok("V")
            war[par] = B.tok("V")
        B.phase()

    def ln_full(use_shift, affine, dst, tcs=range(4)):
        sum_tok = {0: 0, 1: 0}
        stat_read = 0
        apply_done = 0
        for tc in tcs:
            c0 = ZOFF + 512 * tc
            for p in range(NPT):
                par = p % 2
                tbuf, sqb = sb[f"tb{par}"], sb[f"tsq{par}"]
                B.wait("V", "T", sum_tok[par])
                if p == 0:
                    B.wait("T", "V", stat_read)
                off = c0 - (1 if (use_shift and p >= 4) else 0)
                cv = B.emit("V", lambda e, p=p, off=off, tbuf=tbuf:
                            e.tensor_copy(out=tbuf[:], in_=Z(p)[:, off:off + 512]), inc="V")
                B.wait("V", "V", cv)
                B.emit("V", lambda e, tbuf=tbuf, sqb=sqb:
                       e.tensor_tensor(out=sqb[:], in0=tbuf[:], in1=tbuf[:],
                                       op=ALU.mult), inc="V")
                vt = B.tok("V")
                B.wait("T", "V", vt)
                B.emit("T", lambda e, p=p, tbuf=tbuf:
                       e.matmul(out=ps["D"][0:1, 0:512], lhsT=sb["onescol16"][:],
                                rhs=tbuf[:], start=(p == 0), stop=(p == NPT - 1)), inc="T")
                B.emit("T", lambda e, p=p, sqb=sqb:
                       e.matmul(out=ps["E"][0:1, 0:512], lhsT=sb["onescol16"][:],
                                rhs=sqb[:], start=(p == 0), stop=(p == NPT - 1)), inc="T")
                sum_tok[par] = B.tok("T")
            t_sums = B.tok("T")
            rstd = sb["srow"][:]
            nmr = sb["srow2"][:]
            mu, vr = sb["mrow"][:], sb["vrow"][:]
            B.wait("V", "T", t_sums)
            B.emit("V", lambda e, mu=mu: e.tensor_scalar_mul(
                out=mu, in0=ps["D"][0:1, 0:512], scalar1=1.0 / D), inc="V")
            vrt = B.emit("V", lambda e, vr=vr: e.tensor_scalar_mul(
                out=vr, in0=ps["E"][0:1, 0:512], scalar1=1.0 / D), inc="V")
            stat_read = B.tok("V")
            vt = B.tok("V")
            B.wait("S", "V", vt)
            B.emit("S", lambda e, mu=mu, nmr=nmr: e.activation(
                out=nmr, in_=mu, func=AF.Square), inc="S")
            st = B.tok("S")
            B.wait("V", "S", st)
            B.wait("V", "V", vrt)
            B.emit("V", lambda e, vr=vr, nmr=nmr: e.tensor_tensor(
                out=vr, in0=vr, in1=nmr, op=ALU.subtract), inc="V")
            vt = B.tok("V")
            B.wait("S", "V", vt)
            lnt = B.emit("S", lambda e, vr=vr, mu=mu: e.activation(
                out=vr, in_=vr, func=AF.Ln, bias=sb["epsc"][:]), inc="S")
            B.wait("S", "S", lnt)
            B.emit("S", lambda e, vr=vr, rstd=rstd: e.activation(
                out=rstd, in_=vr, func=AF.Exp, scale=-0.5), inc="S")
            st = B.tok("S")
            B.wait("V", "S", st)
            B.wait("V", "V", vrt)
            B.emit("V", lambda e, mu=mu, rstd=rstd, nmr=nmr: e.scalar_tensor_tensor(
                out=nmr, in0=mu, scalar=-1.0, in1=rstd, op0=ALU.mult, op1=ALU.mult),
                   inc="V")
            vt_rows = B.tok("V")
            B.wait("T", "V", vt_rows)
            B.wait("T", "V", apply_done)
            B.emit("T", lambda e, rstd=rstd: e.matmul(
                out=ps["F"][:, 0:512], lhsT=sb["onesrow"][:], rhs=rstd,
                start=True, stop=True), inc="T")
            B.emit("T", lambda e, nmr=nmr: e.matmul(
                out=ps["C"][:, 0:512], lhsT=sb["onesrow"][:], rhs=nmr,
                start=True, stop=True), inc="T")
            t_bc = B.tok("T")
            B.wait("V", "T", t_bc)
            s3t = {0: 0, 1: 0}
            for pp in range(0, NPT, 2):
                s0t = {}
                s1t = {}
                for p in (pp, pp + 1):
                    off = c0 - (1 if (use_shift and p >= 4) else 0)
                    buf = sb[f"s0buf{p % 2}"]
                    B.wait("V", "V", s3t[p % 2])
                    s0t[p] = B.emit("V", lambda e, p=p, off=off, buf=buf: e.tensor_tensor(
                        out=buf[:], in0=Z(p)[:, off:off + 512], in1=ps["F"][:, 0:512],
                        op=ALU.mult), inc="V")
                for p in (pp, pp + 1):
                    buf = sb[f"s0buf{p % 2}"]
                    B.wait("V", "V", s0t[p])
                    if affine or dst == "z":
                        s1t[p] = B.emit("V", lambda e, buf=buf: e.tensor_tensor(
                            out=buf[:], in0=buf[:], in1=ps["C"][:, 0:512],
                            op=ALU.add), inc="V")
                    else:
                        s1t[p] = B.emit("V", lambda e, p=p, tc=tc, buf=buf: e.tensor_tensor(
                            out=sb[f"xn{p}"][:, 512 * tc:512 * (tc + 1)], in0=buf[:],
                            in1=ps["C"][:, 0:512], op=ALU.add), inc="V")
                for p in (pp, pp + 1):
                    buf = sb[f"s0buf{p % 2}"]
                    B.wait("V", "V", s1t[p])
                    if affine:
                        s3t[p % 2] = B.emit("V", lambda e, p=p, tc=tc, buf=buf:
                                            e.tensor_scalar(
                            out=sb[f"xn{p}"][:, 512 * tc:512 * (tc + 1)], in0=buf[:],
                            scalar1=sb["gcol"][:, p:p + 1], scalar2=sb["bcol"][:, p:p + 1],
                            op0=ALU.mult, op1=ALU.add), inc="V")
                    elif dst == "z":
                        s3t[p % 2] = B.emit("V", lambda e, p=p, c0=c0, buf=buf:
                                            e.tensor_copy(
                            out=Z(p)[:, c0:c0 + 512], in_=buf[:]), inc="V")
                    else:
                        s3t[p % 2] = s1t[p]
            apply_done = B.tok("V")
        B.phase()

    def colmaker(rows_and_dsts):
        banks = ["A", "B", "C", "D", "E", "F"]
        i = 0
        while i < len(rows_and_dsts):
            grp = rows_and_dsts[i:i + 6]
            for k, (rap, _) in enumerate(grp):
                B.emit("T", lambda e, rap=rap, bk=banks[k]: e.matmul(
                    out=ps[bk][:, 0:1], lhsT=rap, rhs=sb["ones1"][:],
                    start=True, stop=True), inc="T")
            tt = B.tok("T")
            B.wait("V", "T", tt)
            for k, (_, dap) in enumerate(grp):
                B.emit("V", lambda e, dap=dap, bk=banks[k]: e.tensor_copy(
                    out=dap, in_=ps[bk][:, 0:1]), inc="V")
            vt = B.tok("V")
            B.wait("T", "V", vt)
            i += 6

    def load_attn_params(i):
        B.dma("Y", sb["qsrow"][:], dr["fqs"].ap()[i:i + 1, :], ch="par")
        B.dma("Y", sb["ksrow"][:], dr["fks"].ap()[i:i + 1, :], ch="par")
        B.dwait("V", "par")
        hd = DH // 2
        B.emit("V", lambda e: e.tensor_copy(out=sb["qs_sw"][:, hd:DH],
                                            in_=sb["qsrow"][:, 0:hd]), inc="V")
        B.emit("V", lambda e: e.tensor_copy(out=sb["qs_sw"][:, 0:hd],
                                            in_=sb["qsrow"][:, hd:DH]), inc="V")
        B.emit("V", lambda e: e.tensor_copy(out=sb["ks_sw"][:, hd:DH],
                                            in_=sb["ksrow"][:, 0:hd]), inc="V")
        B.emit("V", lambda e: e.tensor_copy(out=sb["ks_sw"][:, 0:hd],
                                            in_=sb["ksrow"][:, hd:DH]), inc="V")
        vt = B.tok("V")
        B.wait("T", "V", vt)
        rows = [("qsrow", "qsB"), ("qs_sw", "qsSB"), ("ksrow", "ksB"), ("ks_sw", "ksSB")]
        banks = ["A", "B", "C", "D"]
        for k, (r, d) in enumerate(rows):
            B.emit("T", lambda e, r=r, bk=banks[k]: e.matmul(
                out=ps[bk][:, 0:DH], lhsT=sb["onesrow"][:], rhs=sb[r][:],
                start=True, stop=True), inc="T")
        tt = B.tok("T")
        B.wait("V", "T", tt)
        for k, (r, d) in enumerate(rows):
            B.emit("V", lambda e, d=d, bk=banks[k]: e.tensor_copy(
                out=sb[d][:], in_=ps[bk][:, 0:DH]), inc="V")
        B.phase()

    def load_ffn_params(i):
        B.dma("Y", sb["grow"][:], dr["flng"].ap()[i:i + 1, :], ch="par")
        B.dma("Y", sb["brow"][:], dr["flnb"].ap()[i:i + 1, :], ch="par")
        B.dma("Y", sb["b2row"][:], dr["fb2"].ap()[i:i + 1, :], ch="par")
        B.dma("Y", sb["b1row"][:], dr["fb1"].ap()[i:i + 1, :], ch="par")
        B.dwait("T", "par")
        jobs = []
        for p in range(NPT):
            jobs.append((sb["grow"][:, 128 * p:128 * (p + 1)], sb["gcol"][:, p:p + 1]))
            jobs.append((sb["brow"][:, 128 * p:128 * (p + 1)], sb["bcol"][:, p:p + 1]))
            jobs.append((sb["b2row"][:, 128 * p:128 * (p + 1)], sb["b2col"][:, p:p + 1]))
        colmaker(jobs)
        B.phase()

    def half_tables(hf):
        T0 = 1024 * hf
        for name, cn in (("c1qh", "c1q"), ("c2qh", "c2q"), ("c1kh", "c1k"), ("c2kh", "c2k")):
            B.dma("Y", sb[name][:].rearrange("p (a d) -> p a d", d=DH),
                  co[cn].ap()[T0:T0 + 1024, :].rearrange("(a p) d -> p a d", p=128),
                  ch="tab")
        B.dwait("V", "tab")
        for dst, src, qb in (("aq", "c1qh", "qsB"), ("bq", "c2qh", "qsSB"),
                             ("ak", "c1kh", "ksB"), ("bk", "c2kh", "ksSB")):
            B.emit("V", lambda e, dst=dst, src=src, qb=qb: e.tensor_tensor(
                out=sb[dst][:].rearrange("p (a d) -> p a d", d=DH),
                in0=sb[src][:].rearrange("p (a d) -> p a d", d=DH),
                in1=sb[qb][:].unsqueeze(1).broadcast_to([128, 8, DH]),
                op=ALU.mult), inc="V")

    def boundary_copies():
        for p in range(NPT):
            B.emit("V", lambda e, p=p: e.tensor_copy(
                out=sb[f"kbuf{p}"][:, 0:256], in_=sb[f"kbuf{p}"][:, 1024:1280]), inc="V")
        B.emit("V", lambda e: e.tensor_copy(out=sb["vbuf0"][:], in_=sb["vbuf8"][:]), inc="V")
        B.emit("V", lambda e: e.tensor_copy(out=sb["vbuf1"][:], in_=sb["vbuf9"][:]), inc="V")

    def build_tshift(hf):
        T0 = 1024 * hf
        for p in range(NPT):
            if p < 4:
                B.emit("V", lambda e, p=p, T0=T0: e.tensor_copy(
                    out=tshift[p], in_=Z(p)[:, ZOFF + T0:ZOFF + T0 + 1024]), inc="V")
            elif hf == 0:
                B.emit("V", lambda e, p=p: e.tensor_copy(
                    out=tshift[p], in_=Z(p)[:, 0:1024]), inc="V")
            else:
                B.emit("V", lambda e, p=p: e.tensor_copy(
                    out=tshift[p][:, 1:1024], in_=Z(p)[:, ZOFF + 1024:ZOFF + 2047]), inc="V")
                B.emit("V", lambda e, p=p: e.tensor_copy(
                    out=tshift[p][:, 0:1], in_=sb["savecol"][:, p - 4:p - 3]), inc="V")
        if hf == 0:
            for p in range(4, NPT):
                B.emit("V", lambda e, p=p: e.tensor_copy(
                    out=sb["savecol"][:, p - 4:p - 3],
                    in_=Z(p)[:, ZOFF + 1023:ZOFF + 1024]), inc="V")

    def inner_qkv(i, hf):
        conv_tok = {0: 0, 1: 0}
        mm_tok = {0: 0, 1: 0}
        cons_tok = {0: 0, 1: 0}
        trwar = {0: 0, 1: 0}
        tailq = []
        for c in range(24):
            par = c % 2
            kind = "q" if c < 8 else ("k" if c < 16 else "v")
            B.wait("Y", "G", conv_tok[par])
            B.dma("Y", sb[f"wland{par}"][:].rearrange("p (a n) -> p a n", n=128),
                  dr["fwqkv"].ap()[i, :, 128 * c:128 * (c + 1)].rearrange(
                      "(a p) n -> p a n", p=128), ch=f"wq{par}")
            B.dwait("G", f"wq{par}")
            B.wait("G", "T", mm_tok[par])
            B.emit("G", lambda e, par=par: e.tensor_copy(
                out=sb[f"wbf{par}"][:], in_=sb[f"wland{par}"][:]), inc="G")
            conv_tok[par] = B.tok("G")
            gt = B.tok("G")
            for ttp in range(0, 8, 2):
                pend = []
                for tt in (ttp, ttp + 1):
                    bp = tt % 2
                    bank = ps["A"] if bp == 0 else ps["B"]
                    B.wait("T", "G", gt)
                    B.wait("T", "S", cons_tok[bp])
                    for dt in range(NPT):
                        B.emit("T", lambda e, c=c, tt=tt, dt=dt, par=par, bank=bank:
                               e.matmul(out=bank[:, 0:128],
                                        lhsT=tshift[dt][:, 128 * tt:128 * (tt + 1)],
                                        rhs=sb[f"wbf{par}"][:, 128 * dt:128 * (dt + 1)],
                                        start=(dt == 0), stop=(dt == NPT - 1)), inc="T")
                    mmt = B.tok("T")
                    pend.append((tt, bank, mmt))
                if tailq:
                    qk_tail(*tailq.pop())
                if kind == "v":
                    for tt, bank, mmt in pend:
                        B.wait("S", "T", mmt)
                        B.emit("S", lambda e, c=c, tt=tt, bank=bank: e.copy(
                            out=sb[f"vbuf{2 + tt}"][:, 128 * (c - 16):128 * (c - 15)],
                            in_=bank[:, 0:128]), inc="S")
                        cons_tok[tt % 2] = B.tok("S")
                else:
                    qk_front(c, hf, pend, cons_tok)
                    tailq.append((c, pend, trwar))
            mm_tok[par] = B.tok("T")
        if tailq:
            qk_tail(*tailq.pop())

    def qk_front(c, hf, pend, cons_tok):
        isq = c < 8
        cc = c if isq else c - 8
        At = sb["aq"] if isq else sb["ak"]
        Bt = sb["bq"] if isq else sb["bk"]
        NH = 2
        for tt, bank, mmt in pend:
            j3 = tt % 3
            B.wait("S", "T", mmt)
            B.wait("S", "V", qk_war.get(j3, 0))
            B.emit("S", lambda e, j3=j3, bank=bank: e.copy(
                out=sb[f"qraw{j3}"][:], in_=bank[:, 0:128]), inc="S")
            cons_tok[tt % 2] = B.tok("S")
            sev = B.tok("S")
            B.wait("G", "S", sev)
            B.wait("G", "V", qk_war.get(j3, 0))
            B.emit("G", lambda e, j3=j3: e.tensor_tensor(
                out=sb[f"sqb{j3}"][:], in0=sb[f"qraw{j3}"][:], in1=sb[f"qraw{j3}"][:],
                op=ALU.mult), inc="G")
            gt = B.tok("G")
            B.wait("V", "G", gt)
            B.emit("V", lambda e, j3=j3: e.tensor_reduce(
                out=sb["ssb"][:, 2 * j3:2 * j3 + 2],
                in_=sb[f"sqb{j3}"][:].rearrange("p (h d) -> p h d", h=NH),
                axis=AX.X, op=ALU.add), inc="V")
            vt = B.tok("V")
            B.wait("G", "V", vt)
            B.emit("G", lambda e, j3=j3: e.tensor_scalar_max(
                out=sb["rnb"][:, 2 * j3:2 * j3 + 2], in0=sb["ssb"][:, 2 * j3:2 * j3 + 2],
                scalar1=1e-24), inc="G")
            gt = B.tok("G")
            B.wait("S", "G", gt)
            lnt = B.emit("S", lambda e, j3=j3: e.activation(
                out=sb["nmax"][:, 2 * j3:2 * j3 + 2], in_=sb["rnb"][:, 2 * j3:2 * j3 + 2],
                func=AF.Ln), inc="S")
            B.wait("S", "S", lnt)
            B.emit("S", lambda e, j3=j3: e.activation(
                out=sb["rnb"][:, 2 * j3:2 * j3 + 2], in_=sb["nmax"][:, 2 * j3:2 * j3 + 2],
                func=AF.Exp, scale=-0.5), inc="S")
            pend_s[tt] = B.tok("S")
        hd = DH // 2
        qh_t = {}
        t1_t = {}
        t2_t = {}
        for step in ("qhat", "t1b", "t2a", "t2c", "qpp"):
            for tt, bank, mmt in pend:
                j3 = tt % 3
                b = tt % 2
                tsl = 64 * tt
                if step == "qhat":
                    B.wait("V", "S", pend_s[tt])
                    qh_t[tt] = B.emit("V", lambda e, j3=j3: e.tensor_tensor(
                        out=sb[f"qhat{j3}"][:].rearrange("p (h d) -> p h d", h=NH),
                        in0=sb[f"qraw{j3}"][:].rearrange("p (h d) -> p h d", h=NH),
                        in1=sb["rnb"][:, 2 * j3:2 * j3 + 2].unsqueeze(2)
                            .broadcast_to([128, NH, DH]),
                        op=ALU.mult), inc="V")
                    qk_war[j3] = qh_t[tt]
                elif step == "t1b":
                    B.wait("V", "V", qh_t[tt])
                    t1_t[tt] = B.emit("V", lambda e, j3=j3, tsl=tsl, At=At: e.tensor_tensor(
                        out=sb[f"t1b{j3}"][:].rearrange("p (h d) -> p h d", h=NH),
                        in0=sb[f"qhat{j3}"][:].rearrange("p (h d) -> p h d", h=NH),
                        in1=At[:, tsl:tsl + DH].unsqueeze(1)
                            .broadcast_to([128, NH, DH]),
                        op=ALU.mult), inc="V")
                elif step == "t2a":
                    B.emit("V", lambda e, j3=j3, tsl=tsl, Bt=Bt: e.tensor_tensor(
                        out=sb[f"t2b{j3}"][:].rearrange("p (h d) -> p h d", h=NH)[:, :, 0:hd],
                        in0=sb[f"qhat{j3}"][:].rearrange("p (h d) -> p h d", h=NH)[:, :, hd:DH],
                        in1=Bt[:, tsl:tsl + hd].unsqueeze(1).broadcast_to([128, NH, hd]),
                        op=ALU.mult), inc="V")
                elif step == "t2c":
                    t2_t[tt] = B.emit("V", lambda e, j3=j3, tsl=tsl, Bt=Bt: e.tensor_tensor(
                        out=sb[f"t2b{j3}"][:].rearrange("p (h d) -> p h d", h=NH)[:, :, hd:DH],
                        in0=sb[f"qhat{j3}"][:].rearrange("p (h d) -> p h d", h=NH)[:, :, 0:hd],
                        in1=Bt[:, tsl + hd:tsl + DH].unsqueeze(1).broadcast_to([128, NH, hd]),
                        op=ALU.mult), inc="V")
                else:
                    B.wait("V", "V", max(t1_t[tt], t2_t[tt]))
                    B.wait("V", "T", qpp_war.get(j3, 0))
                    B.emit("V", lambda e, j3=j3: e.tensor_tensor(
                        out=sb[f"qpp{j3}"][:], in0=sb[f"t1b{j3}"][:], in1=sb[f"t2b{j3}"][:],
                        op=ALU.add), inc="V")
                    pend_v[tt] = B.tok("V")
    def qk_tail(c, pend, trwar):
        isq = c < 8
        cc = c if isq else c - 8
        for tt, bank, mmt in pend:
            j3 = tt % 3
            bk = ps["C"] if tt % 2 == 0 else ps["G"]
            B.wait("T", "V", pend_v[tt])
            B.wait("T", "V", trwar[tt % 2])
            B.emit("T", lambda e, j3=j3, bk=bk: e.matmul(
                out=bk[:, 0:128], lhsT=sb[f"qpp{j3}"][:],
                rhs=I16(), start=True, stop=True), inc="T")
            trt = B.tok("T")
            qpp_war[j3] = trt
            B.wait("V", "T", trt)
            if isq:
                dst = qfull[cc][:, 128 * tt:128 * (tt + 1)]
            else:
                dst = sb[f"kbuf{cc}"][:, 256 + 128 * tt:256 + 128 * (tt + 1)]
            B.emit("V", lambda e, dst=dst, bk=bk: e.tensor_copy(
                out=dst, in_=bk[:, 0:128]), inc="V")
            trwar[tt % 2] = B.tok("V")

    qk_war = {}
    qpp_war = {}
    pend_s = {}
    pend_v = {}

    def attn_loop(hf, flavor):
        if flavor == "inner":
            subs = []
            for wi in range(4):
                for h in range(H):
                    for qh in range(2):
                        subs.append((wi, h, qh))
        else:
            subs = [(qt, h, 0) for qt in range(8) for h in range(H)]
        LAG = 4
        sim_info = {}
        war_sim = {}
        st = {"war_ptb": 0, "pt_sw": {}, "pt_seq": 0, "pnorm_war": {},
              "nmax_war": {}, "rsum_war": {}, "rrec_war": {}}
        pv_done = {0: 0, 1: 0}
        n = len(subs)
        for s in range(n + LAG):
            if s >= LAG:
                emit_tail(hf, flavor, s - LAG, subs[s - LAG], sim_info, war_sim,
                          st, pv_done)
            if s < n:
                emit_sim(hf, flavor, s, subs[s], sim_info, war_sim, st)

    def emit_sim(hf, flavor, s, sub, sim_info, war_sim, st):
        bank = ps[["A", "B", "C", "F"][s % 4]]
        if flavor == "inner":
            wi, h, qh = sub
            gw = 4 * hf + wi
            if gw == 0:
                ncols = 128 if qh == 0 else 256
                mask = sb["m_zl"][:, 128:256] if qh == 0 else sb["m_zl"][:]
                colbase = 256
            else:
                ncols = 384
                mask = sb["m_inner"][:]
                colbase = 256 * wi + 128 * qh
            qcols = (256 * wi + 128 * qh, 256 * wi + 128 * qh + 128)
            tile = h // 2
            po = 64 * (h % 2)
            lhs = qview(flavor, hf, tile)[po:po + 64, qcols[0]:qcols[1]]
            rhs = sb[f"kbuf{tile}"][po:po + 64, colbase:colbase + ncols]
        else:
            qt, h, _ = sub
            gqt = 8 * hf + qt
            j0 = max(0, gqt - 2)
            nkt = gqt - j0 + 1
            ncols = 128 * nkt
            if nkt == 3:
                mask = sb["m_outer"][:]
            elif nkt == 2:
                mask = sb["m_zl"][:]
            else:
                mask = sb["m_zl"][:, 128:256]
            colbase = 256 + 128 * (j0 - 8 * hf)
            tile = h // 2
            po = 64 * (h % 2)
            lhs = qview(flavor, hf, tile)[po:po + 64, 128 * qt:128 * (qt + 1)]
            rhs = sb[f"kbuf{tile}"][po:po + 64, colbase:colbase + ncols]
        B.wait("T", "V", war_sim.get(s % 4, 0))
        B.wait("T", "S", war_sim.get(("S", s % 4), 0))
        B.emit("T", lambda e, bank=bank, mask=mask, ncols=ncols: e.matmul(
            out=bank[:, 0:ncols], lhsT=I16(), rhs=mask, start=True, stop=False), inc="T")
        B.emit("T", lambda e, bank=bank, lhs=lhs, rhs=rhs, ncols=ncols: e.matmul(
            out=bank[:, 0:ncols], lhsT=lhs, rhs=rhs, start=False, stop=True), inc="T")
        simt = B.tok("T")
        r = s % 8
        B.wait("V", "T", simt)
        B.wait("V", "S", st["nmax_war"].get(r, 0))
        B.emit("V", lambda e, bank=bank, r=r, ncols=ncols: e.tensor_reduce(
            out=sb["nmax"][:, r:r + 1], in_=bank[:, 0:ncols], axis=AX.X, op=ALU.max,
            negate=True), inc="V")
        vt = B.tok("V")
        war_sim[s % 4] = vt
        B.wait("S", "V", vt)
        B.wait("S", "V", st["rsum_war"].get(r, 0))
        B.emit("S", lambda e, bank=bank, r=r, s3=s % 4, ncols=ncols: e.activation(
            out=sb[f"pbuf{s3}"][:, 0:ncols], in_=bank[:, 0:ncols], func=AF.Exp,
            bias=sb["nmax"][:, r:r + 1], accum_out=sb["rsum"][:, r:r + 1]), inc="S")
        war_sim[("S", s % 4)] = B.tok("S")
        st["nmax_war"][r] = B.tok("S")
        stok = B.tok("S")
        B.wait("V", "S", stok)
        B.wait("V", "S", st["rrec_war"].get(r, 0))
        B.emit("V", lambda e, r=r: e.reciprocal(
            out=sb["rrec"][:, r:r + 1], in_=sb["rsum"][:, r:r + 1]), inc="V")
        st["rsum_war"][r] = B.tok("V")
        vt = B.tok("V")
        B.wait("S", "V", vt)
        B.wait("S", "T", st["pnorm_war"].get(s % 4, 0))
        B.wait("S", "S", war_sim.get(("S", s % 4), 0))
        B.emit("S", lambda e, r=r, s3=s % 4, ncols=ncols: e.activation(
            out=sb[f"pnorm{s3}"][:, 0:ncols], in_=sb[f"pbuf{s3}"][:, 0:ncols],
            func=AF.Copy, scale=sb["rrec"][:, r:r + 1]), inc="S")
        st["rrec_war"][r] = B.tok("S")
        sim_info[s] = (B.tok("S"), ncols)

    def emit_tail(hf, flavor, s, sub, sim_info, war_sim, st, pv_done):
        norm_tok, ncols = sim_info.pop(s)
        nkt = ncols // 128
        s3 = s % 4
        if flavor == "inner":
            wi, h, qh = sub
            gw = 4 * hf + wi
            if gw == 0:
                if qh == 0:
                    blocks = [(0, 0)]
                else:
                    blocks = [(0, 128), (1, 256)]
            else:
                if qh == 0:
                    blocks = [(0, 0), (1, 128), (2, 384)]
                else:
                    blocks = [(0, 256), (1, 512), (2, 640)]
        else:
            qt, h, _ = sub
            blocks = [(j, 128 * j) for j in range(nkt)]
        B.wait("T", "S", norm_tok)
        first = True
        for jj, col in blocks:
            sl = st["pt_seq"] % 2
            st["pt_seq"] += 1
            bk = ps["G"] if sl == 0 else ps["H"]
            B.wait("T", "V", st["pt_sw"].get(sl, 0))
            B.emit("T", lambda e, s3=s3, jj=jj, bk=bk: e.matmul(
                out=bk[:, 0:128],
                lhsT=sb[f"pnorm{s3}"][:, 128 * jj:128 * (jj + 1)], rhs=I16(),
                start=True, stop=True), inc="T")
            trt = B.tok("T")
            B.wait("V", "T", trt)
            if first:
                B.wait("V", "T", st["war_ptb"])
                first = False
            B.emit("V", lambda e, col=col, bk=bk: e.tensor_copy(
                out=sb["ptb"][:, col:col + 128], in_=bk[:, 0:128]), inc="V")
            st["pt_sw"][sl] = B.tok("V")
        st["pnorm_war"][s3] = B.tok("T")
        evt = B.tok("V")
        do_pv = (flavor == "outer") or (qh == 1)
        if do_pv:
            par = h % 2
            pvbank = ps["D"] if par == 0 else ps["E"]
            B.wait("T", "V", evt)
            B.wait("T", "S", pv_done[par])
            if flavor == "inner":
                wi, h, _ = sub
                gw = 4 * hf + wi
                if gw == 0:
                    mms = [(0, 0, 256, 0), (1, 256, 128, 128)]
                else:
                    mms = [(1, 128, 256, 0), (0, 0, 128, 0), (2, 384, 256, 0),
                           (3, 640, 128, 128)]
                for mi, (vo, ptcol, wdt, oc) in enumerate(mms):
                    vidx = 2 * wi + vo if gw != 0 else 2 + vo
                    B.emit("T", lambda e, vidx=vidx, h=h, ptcol=ptcol, wdt=wdt, oc=oc,
                           pvbank=pvbank, st=(mi == 0), sp=(mi == len(mms) - 1):
                           e.matmul(out=pvbank[0:64, oc:oc + wdt],
                                    lhsT=sb[f"vbuf{vidx}"][:, 64 * h:64 * h + 64],
                                    rhs=sb["ptb"][:, ptcol:ptcol + wdt],
                                    start=st, stop=sp), inc="T")
            else:
                qt, h, _ = sub
                gqt = 8 * hf + qt
                j0 = max(0, gqt - 2)
                for mi in range(nkt):
                    vidx = (j0 + mi) - 8 * hf + 2
                    B.emit("T", lambda e, vidx=vidx, h=h, mi=mi, pvbank=pvbank,
                           st=(mi == 0), sp=(mi == nkt - 1):
                           e.matmul(out=pvbank[0:64, 0:128],
                                    lhsT=sb[f"vbuf{vidx}"][:, 64 * h:64 * h + 64],
                                    rhs=sb["ptb"][:, 128 * mi:128 * (mi + 1)],
                                    start=st, stop=sp), inc="T")
            pvt = B.tok("T")
            if par == 1:
                width = 256 if flavor == "inner" else 128
                if flavor == "inner":
                    oc0 = 256 * sub[0]
                else:
                    oc0 = 128 * sub[0]
                m = h // 2
                B.wait("S", "T", pvt)
                for pq, bkn in ((0, "D"), (64, "E")):
                    B.emit("S", lambda e, m=m, oc0=oc0, width=width, pq=pq, bkn=bkn:
                           e.copy(out=sb[f"otbuf{m}"][pq:pq + 64, oc0:oc0 + width],
                                  in_=ps[bkn][0:64, 0:width]), inc="S")
                pv_done[0] = B.tok("S")
                pv_done[1] = B.tok("S")
        st["war_ptb"] = B.tok("T")

    def wo_stage(hf, i, flavor):
        T0 = 1024 * hf
        conv_tok = {0: 0, 1: 0}
        use_tok = {0: 0, 1: 0}
        add_tok = {0: 0, 1: 0}
        for ept in range(NPT):
            par = ept % 2
            B.wait("Y", "G", conv_tok[par])
            if flavor == "inner":
                B.dma("Y", sb[f"woland{par}"][:].rearrange("p (a n) -> p a n", n=128),
                      dr["fwo"].ap()[i, :, 128 * ept:128 * (ept + 1)].rearrange(
                          "(a p) n -> p a n", p=128), ch=f"wo{par}")
            else:
                B.dma("Y", sb[f"woland{par}"][:],
                      dr["wo"].ap()[128 * ept:128 * (ept + 1), :], ch=f"wo{par}")
            B.dwait("G", f"wo{par}")
            B.wait("G", "T", use_tok[par])
            B.emit("G", lambda e, par=par: e.tensor_copy(
                out=sb[f"wobf{par}"][:], in_=sb[f"woland{par}"][:]), inc="G")
            conv_tok[par] = B.tok("G")
            gt = B.tok("G")
            if flavor == "outer":
                B.wait("T", "G", gt)
                for dt in range(NPT):
                    bk = ps[("C", "D", "E", "F")[dt % 4]]
                    B.wait("T", "V", wo_trwar.get((par, dt % 4), 0))
                    B.emit("T", lambda e, par=par, dt=dt, bk=bk: e.matmul(
                        out=bk[:, 0:128],
                        lhsT=sb[f"wobf{par}"][:, 128 * dt:128 * (dt + 1)],
                        rhs=I16(), start=True, stop=True), inc="T")
                    trt = B.tok("T")
                    B.wait("V", "T", trt)
                    B.emit("V", lambda e, par=par, dt=dt, bk=bk: e.tensor_copy(
                        out=sb[f"wT{par}"][:, 128 * dt:128 * (dt + 1)],
                        in_=bk[:, 0:128]), inc="V")
                    wo_trwar[(par, dt % 4)] = B.tok("V")
                gt = B.tok("V")
                lhsbuf = sb[f"wT{par}"]
                wsem = "V"
            else:
                lhsbuf = sb[f"wobf{par}"]
                wsem = "G"
            for tc in range(2):
                bank = ps["A"] if tc == 0 else ps["B"]
                B.wait("T", wsem, gt)
                B.wait("T", "V", add_tok[tc])
                for ipt in range(NPT):
                    B.emit("T", lambda e, lhsbuf=lhsbuf, ipt=ipt, tc=tc, bank=bank:
                           e.matmul(out=bank[:, 0:512],
                                    lhsT=lhsbuf[:, 128 * ipt:128 * (ipt + 1)],
                                    rhs=sb[f"otbuf{ipt}"][:, 512 * tc:512 * (tc + 1)],
                                    start=(ipt == 0), stop=(ipt == NPT - 1)), inc="T")
                mmt = B.tok("T")
                B.wait("V", "T", mmt)
                B.emit("V", lambda e, ept=ept, tc=tc, T0=T0, bank=bank: e.tensor_tensor(
                    out=Z(ept)[:, ZOFF + T0 + 512 * tc:ZOFF + T0 + 512 * (tc + 1)],
                    in0=Z(ept)[:, ZOFF + T0 + 512 * tc:ZOFF + T0 + 512 * (tc + 1)],
                    in1=bank[:, 0:512], op=ALU.add), inc="V")
                add_tok[tc] = B.tok("V")
            use_tok[par] = B.tok("T")

    wo_trwar = {}

    def outer_qkv(hf):
        T0 = 1024 * hf
        conv_tok = {0: 0, 1: 0}
        use_tok = {0: 0, 1: 0}
        ev_tok = {0: 0, 1: 0}
        seq = 0
        for ec in range(CFG.get("oqkv_ecs", NPT)):
            for wn, kind in (("wq", "q"), ("wk", "k"), ("wv", "v")):
                if kind not in CFG.get("oqkv_kinds", "qkv"):
                    continue
                par = seq % 2
                seq += 1
                steps = CFG.get("oqkv_steps", 7)
                B.wait("Y", "G", conv_tok[par])
                B.dma("Y", sb[f"woland{par}"][:],
                      dr[wn].ap()[128 * ec:128 * (ec + 1), :], ch=f"wo{par}")
                B.dwait("G", f"wo{par}")
                B.wait("G", "T", use_tok[par])
                if kind == "q":
                    B.emit("G", lambda e, par=par: e.tensor_scalar_mul(
                        out=sb[f"wobf{par}"][:], in0=sb[f"woland{par}"][:],
                        scalar1=0.125), inc="G")
                else:
                    B.emit("G", lambda e, par=par: e.tensor_copy(
                        out=sb[f"wobf{par}"][:], in_=sb[f"woland{par}"][:]), inc="G")
                conv_tok[par] = B.tok("G")
                gt = B.tok("G")
                if steps < 2:
                    continue
                B.wait("T", "G", gt)
                for dt in range(NPT):
                    bk = ps[("C", "D", "E", "F")[dt % 4]]
                    B.wait("T", "V", wo_trwar.get((par, dt % 4), 0))
                    B.emit("T", lambda e, par=par, dt=dt, bk=bk: e.matmul(
                        out=bk[:, 0:128],
                        lhsT=sb[f"wobf{par}"][:, 128 * dt:128 * (dt + 1)],
                        rhs=I16(), start=True, stop=True), inc="T")
                    trt = B.tok("T")
                    B.wait("V", "T", trt)
                    B.emit("V", lambda e, par=par, dt=dt, bk=bk: e.tensor_copy(
                        out=sb[f"wT{par}"][:, 128 * dt:128 * (dt + 1)],
                        in_=bk[:, 0:128]), inc="V")
                    wo_trwar[(par, dt % 4)] = B.tok("V")
                wtt = B.tok("V")
                if steps < 4:
                    continue
                if kind in ("q", "k"):
                    for tc in range(2):
                        bp = tc % 2
                        bank = ps["A"] if bp == 0 else ps["B"]
                        B.wait("T", "V", wtt)
                        B.wait("T", "V", ev_tok[bp])
                        for dt in range(NPT):
                            B.emit("T", lambda e, par=par, dt=dt, tc=tc, T0=T0, bank=bank:
                                   e.matmul(out=bank[:, 0:512],
                                            lhsT=sb[f"wT{par}"][:, 128 * dt:128 * (dt + 1)],
                                            rhs=sb[f"xn{dt}"][:, T0 + 512 * tc:
                                                              T0 + 512 * (tc + 1)],
                                            start=(dt == 0), stop=(dt == NPT - 1)), inc="T")
                        mmt = B.tok("T")
                        B.wait("V", "T", mmt)
                        if kind == "q":
                            dst = qview("outer", hf, ec)[:, 512 * tc:512 * (tc + 1)]
                        else:
                            dst = sb[f"kbuf{ec}"][:, 256 + 512 * tc:256 + 512 * (tc + 1)]
                        B.emit("V", lambda e, dst=dst, bank=bank: e.tensor_copy(
                            out=dst, in_=bank[:, 0:512]), inc="V")
                        ev_tok[bp] = B.tok("V")
                    use_tok[par] = B.tok("T")
                else:
                    for tt in range(8):
                        bp = tt % 2
                        bank = ps["A"] if bp == 0 else ps["B"]
                        B.wait("T", "V", wtt)
                        B.wait("T", "V", ev_tok[bp])
                        for dt in range(NPT):
                            B.emit("T", lambda e, par=par, dt=dt, tt=tt, T0=T0, bank=bank:
                                   e.matmul(out=bank[:, 0:128],
                                            lhsT=sb[f"xn{dt}"][:, T0 + 128 * tt:
                                                               T0 + 128 * (tt + 1)],
                                            rhs=sb[f"wT{par}"][:, 128 * dt:128 * (dt + 1)],
                                            start=(dt == 0), stop=(dt == NPT - 1)), inc="T")
                        mmt = B.tok("T")
                        B.wait("V", "T", mmt)
                        B.emit("V", lambda e, tt=tt, ec=ec, bank=bank: e.tensor_copy(
                            out=sb[f"vbuf{2 + tt}"][:, 128 * ec:128 * (ec + 1)],
                            in_=bank[:, 0:128]), inc="V")
                        ev_tok[bp] = B.tok("V")
                    use_tok[par] = B.tok("T")

    def ffn(i):
        load_ffn_params(i)
        ln_full(use_shift=True, affine=True, dst="xn")
        groups = [(0, 6), (6, 12), (12, 18), (18, 22)]
        conv_tok = {0: 0, 1: 0}
        mmuse_tok = {0: 0, 1: 0}
        for g0, g1 in groups:
            bias_war = 0
            ag_cons = {0: 0, 1: 0}
            act_v = 0
            for j in range(g0, g1):
                w = min(128, DI - 128 * j)
                slot = j - g0
                par = j % 2
                B.wait("T", "S", bias_war)
                B.emit("T", lambda e, j=j, w=w: e.matmul(
                    out=ps["F"][0:w, 0:1], lhsT=sb["b1row"][:, 128 * j:128 * j + w],
                    rhs=sb["ones1"][:], start=True, stop=True), inc="T")
                B.emit("T", lambda e, j=j, w=w: e.matmul(
                    out=ps["E"][0:w, 0:1], lhsT=sb["b1row"][:, DI + 128 * j:DI + 128 * j + w],
                    rhs=sb["ones1"][:], start=True, stop=True), inc="T")
                tbt = B.tok("T")
                B.wait("V", "T", tbt)
                B.emit("V", lambda e, par=par, w=w: e.tensor_copy(
                    out=sb["b1c"][0:w, 2 * par:2 * par + 1], in_=ps["F"][0:w, 0:1]), inc="V")
                B.emit("V", lambda e, par=par, w=w: e.tensor_copy(
                    out=sb["b1c"][0:w, 2 * par + 1:2 * par + 2], in_=ps["E"][0:w, 0:1]), inc="V")
                b1v = B.tok("V")
                B.wait("Y", "G", conv_tok[par])
                B.dma("Y", sb[f"w1land{par}"][:, 0:8 * w].rearrange("p (a n) -> p a n", n=w),
                      dr["fw1"].ap()[i, :, 128 * j:128 * j + w].rearrange(
                          "(a p) n -> p a n", p=128), ch=f"ffnw{par}")
                B.dma("Y", sb[f"w1land{par}"][:, 1024:1024 + 8 * w].rearrange(
                          "p (a n) -> p a n", n=w),
                      dr["fw1"].ap()[i, :, DI + 128 * j:DI + 128 * j + w].rearrange(
                          "(a p) n -> p a n", p=128), ch=f"ffnw{par}")
                B.dma("Y", sb[f"w2land{par}"][0:w, :],
                      dr["fw2"].ap()[i, 128 * j:128 * j + w, :], ch=f"ffnw{par}")
                B.dwait("G", f"ffnw{par}")
                B.wait("G", "T", mmuse_tok[par])
                B.emit("G", lambda e, w=w, par=par: e.tensor_copy(
                    out=sb[f"w1abf{par}"][:, 0:8 * w],
                    in_=sb[f"w1land{par}"][:, 0:8 * w]), inc="G")
                B.emit("G", lambda e, w=w, par=par: e.tensor_copy(
                    out=sb[f"w1gbf{par}"][:, 0:8 * w],
                    in_=sb[f"w1land{par}"][:, 1024:1024 + 8 * w]), inc="G")
                B.emit("G", lambda e, slot=slot, w=w, par=par: e.tensor_copy(
                    out=sb[f"w2bf{slot}"][0:w, :], in_=sb[f"w2land{par}"][0:w, :]), inc="G")
                conv_tok[par] = B.tok("G")
                gt = B.tok("G")
                for tc in range(4):
                    tp = tc % 2
                    abank = ps["A"] if tp == 0 else ps["B"]
                    gbank = ps["C"] if tp == 0 else ps["G"]
                    B.wait("T", "G", gt)
                    B.wait("T", "S", ag_cons[tp])
                    B.wait("T", "V", act_v)
                    for dt in range(NPT):
                        B.emit("T", lambda e, dt=dt, tc=tc, w=w, abank=abank, par=par:
                               e.matmul(out=abank[0:w, 0:512],
                                        lhsT=sb[f"w1abf{par}"][:, w * dt:w * (dt + 1)],
                                        rhs=sb[f"xn{dt}"][:, 512 * tc:512 * (tc + 1)],
                                        start=(dt == 0), stop=(dt == NPT - 1)), inc="T")
                    for dt in range(NPT):
                        B.emit("T", lambda e, dt=dt, tc=tc, w=w, gbank=gbank, par=par:
                               e.matmul(out=gbank[0:w, 0:512],
                                        lhsT=sb[f"w1gbf{par}"][:, w * dt:w * (dt + 1)],
                                        rhs=sb[f"xn{dt}"][:, 512 * tc:512 * (tc + 1)],
                                        start=(dt == 0), stop=(dt == NPT - 1)), inc="T")
                    mmt = B.tok("T")
                    B.wait("S", "T", mmt)
                    if tc == 0:
                        B.wait("S", "V", b1v)
                    B.wait("S", "V", act_v)
                    B.emit("S", lambda e, tp=tp, par=par, w=w, gbank=gbank: e.activation(
                        out=sb[f"ggel{tp}"][0:w, :], in_=gbank[0:w, 0:512],
                        func=AF.Gelu, bias=sb["b1c"][0:w, 2 * par + 1:2 * par + 2]), inc="S")
                    ag_cons[tp] = B.tok("S")
                    st = B.tok("S")
                    B.wait("V", "S", st)
                    B.emit("V", lambda e, slot=slot, tc=tc, tp=tp, w=w, par=par,
                           abank=abank: e.scalar_tensor_tensor(
                        out=sb[f"act{slot}"][0:w, 512 * tc:512 * (tc + 1)],
                        in0=abank[0:w, 0:512], scalar=sb["b1c"][0:w, 2 * par:2 * par + 1],
                        in1=sb[f"ggel{tp}"][0:w, :], op0=ALU.add, op1=ALU.mult), inc="V")
                    act_v = B.tok("V")
                mmuse_tok[par] = B.tok("T")
                bias_war = B.tok("S")
            B.wait("T", "V", act_v)
            B.wait("T", "G", B.tok("G"))
            add_tok = {0: 0, 1: 0}
            for ept in range(NPT):
                for tc in range(4):
                    bp = (ept * 4 + tc) % 2
                    bank = ps["A"] if bp == 0 else ps["B"]
                    B.wait("T", "V", add_tok[bp])
                    for j in range(g0, g1):
                        w = min(128, DI - 128 * j)
                        slot = j - g0
                        B.emit("T", lambda e, slot=slot, ept=ept, tc=tc, w=w, bank=bank,
                               st=(slot == 0), sp=(j == g1 - 1), j=j:
                               e.matmul(out=bank[:, 0:512],
                                        lhsT=sb[f"w2bf{slot}"][0:w, 128 * ept:128 * (ept + 1)],
                                        rhs=sb[f"act{slot}"][0:w, 512 * tc:512 * (tc + 1)],
                                        start=st, stop=sp), inc="T")
                    mmt = B.tok("T")
                    B.wait("V", "T", mmt)
                    if g0 == 0:
                        B.emit("V", lambda e, ept=ept, tc=tc, bank=bank:
                               e.scalar_tensor_tensor(
                                   out=Z(ept)[:, ZOFF + 512 * tc:ZOFF + 512 * (tc + 1)],
                                   in0=bank[:, 0:512], scalar=sb["b2col"][:, ept:ept + 1],
                                   in1=Z(ept)[:, ZOFF + 512 * tc:ZOFF + 512 * (tc + 1)],
                                   op0=ALU.add, op1=ALU.add), inc="V")
                    else:
                        B.emit("V", lambda e, ept=ept, tc=tc, bank=bank: e.tensor_tensor(
                            out=Z(ept)[:, ZOFF + 512 * tc:ZOFF + 512 * (tc + 1)],
                            in0=Z(ept)[:, ZOFF + 512 * tc:ZOFF + 512 * (tc + 1)],
                            in1=bank[:, 0:512], op=ALU.add), inc="V")
                    add_tok[bp] = B.tok("V")
            B.phase()

    def spill():
        for p in range(NPT):
            B.dma("Y", dr["xmid"].ap().rearrange("t d -> (t d)")[128 * T * p:128 * T * (p + 1)]
                  .rearrange("(q t) -> q t", q=128), Z(p)[:, ZOFF:ZOFF + T], ch="spill")
        B.dwait("Y", "spill")
        B.phase()

    def final_out():
        for p in range(NPT if CFG.get("do_spill", True) else 0):
            for hh in range(2):
                par = (2 * p + hh) % 2
                B.wait("Y", "V", fin_war.get(par, 0))
                B.dma("Y", sb[f"xland{par}"][:],
                      dr["xmid"].ap().rearrange("t d -> (t d)")
                      [128 * T * p:128 * T * (p + 1)]
                      .rearrange("(q t) -> q t", q=128)[:, 1024 * hh:1024 * (hh + 1)],
                      ch=f"fin{par}")
                B.dwait("V", f"fin{par}")
                B.emit("V", lambda e, p=p, hh=hh, par=par: e.tensor_tensor(
                    out=Z(p)[:, ZOFF + 1024 * hh:ZOFF + 1024 * (hh + 1)],
                    in0=Z(p)[:, ZOFF + 1024 * hh:ZOFF + 1024 * (hh + 1)],
                    in1=sb[f"xland{par}"][:], op=ALU.add), inc="V")
                fin_war[par] = B.tok("V")
        B.phase()
        regwar = {}
        for tt in range(NTT):
            par = tt % 2
            obuf = sb[f"outb{par}"]
            B.dwait("T", f"st{par}", st_tok.get(par, 0))
            for p in range(NPT):
                bk = ("A", "B", "C", "D")[p % 4]
                bank = ps[bk]
                B.wait("T", "V", regwar.get(bk, 0))
                B.emit("T", lambda e, p=p, tt=tt, bank=bank: e.matmul(
                    out=bank[:, 0:128],
                    lhsT=Z(p)[:, ZOFF + 128 * tt:ZOFF + 128 * (tt + 1)],
                    rhs=sb["ident32"][:], start=True, stop=True), inc="T")
                tts = B.tok("T")
                B.wait("V", "T", tts)
                B.emit("V", lambda e, p=p, obuf=obuf, bank=bank: e.tensor_copy(
                    out=obuf[:, 128 * p:128 * (p + 1)], in_=bank[:, 0:128]), inc="V")
                regwar[bk] = B.tok("V")
            vt = B.tok("V")
            B.wait("Y", "V", vt)
            B.dma("Y", dr["y"].ap()[128 * tt:128 * (tt + 1), :], obuf[:], ch=f"st{par}")
            st_tok[par] = B.dtok(f"st{par}")
        B.wait_all_dma("Y")
        B.wait_all_dma("G")

    fin_war = {}
    st_tok = {}

    cut = CFG.get("cut", "")
    load_consts()
    if cut == "consts":
        B.dwait("V", "init")
        B.emit("V", lambda e: e.tensor_copy(out=sb["outb0"][:, 0:128],
                                            in_=sb["ident32"][:]), inc="V")
        B.wait("Y", "V", B.tok("V"))
        B.dma("Y", dr["y"].ap()[0:128, 0:128], sb["outb0"][:, 0:128], ch="st0")
        B.wait_all_dma("Y")
        with nc.Block() as block:
            B.finalize(block)
        return nc
    load_x()
    if cut == "loadx":
        B.emit("V", lambda e: e.tensor_copy(out=sb["outb0"][:], in_=Z(0)[:, ZOFF:ZOFF + 1024]),
               inc="V")
        B.wait("Y", "V", B.tok("V"))
        B.dma("Y", dr["y"].ap()[0:128, :], sb["outb0"][:], ch="st0")
        B.wait_all_dma("Y")
        with nc.Block() as block:
            B.finalize(block)
        return nc
    def _cut_store(src_ap):
        B.wait("V", "T", B.tok("T"))
        B.wait("V", "S", B.tok("S"))
        B.wait("V", "G", B.tok("G"))
        B.emit("V", lambda e: e.tensor_copy(out=sb["outb0"][:], in_=src_ap), inc="V")
        B.wait("Y", "V", B.tok("V"))
        B.dma("Y", dr["y"].ap()[0:128, :], sb["outb0"][:], ch="st0")
        B.wait_all_dma("Y")
        with nc.Block() as block:
            B.finalize(block)

    if CFG["do_outer"]:
        for hf in (0, 1):
            ln_full(use_shift=False, affine=False, dst="xn", tcs=(2 * hf, 2 * hf + 1))
            if cut == "oln" and hf == 0:
                _cut_store(sb["xn0"][:, 0:1024])
                return nc
            if hf == 1:
                boundary_copies()
                B.phase()
            outer_qkv(hf)
            B.phase()
            if cut == "oqkv0" and hf == 0:
                _cut_store(qview("outer", 0, 0))
                return nc
            if cut == "oqkv" and hf == 1:
                _cut_store(qview("outer", 1, 0))
                return nc
            attn_loop(hf, "outer")
            B.phase()
            if cut == "oattn":
                _cut_store(sb["otbuf0"][:, 0:1024])
                return nc
            wo_stage(hf, None, "outer")
            B.phase()
    if CFG.get("do_spill", True):
        spill()
        ln_full(use_shift=False, affine=False, dst="z")
    if cut == "z0":
        _cut_store(Z(0)[:, ZOFF:ZOFF + 1024])
        return nc
    if CFG["do_inner"]:
        for i in range(CFG["depth"]):
            load_attn_params(i)
            for hf in (0, 1):
                half_tables(hf)
                if hf == 1:
                    boundary_copies()
                build_tshift(hf)
                B.phase()
                inner_qkv(i, hf)
                B.phase()
                if cut == "iqkv" and hf == 0:
                    _cut_store(qview("inner", hf, 0))
                    return nc
                attn_loop(hf, "inner")
                B.phase()
                if cut == "iattn" and hf == 0:
                    _cut_store(sb["otbuf0"][:, 0:1024])
                    return nc
                wo_stage(hf, i, "inner")
                B.phase()
                if cut == "izh0" and hf == 0:
                    _cut_store(Z(0)[:, ZOFF:ZOFF + 1024])
                    return nc
            if cut == "izatt":
                _cut_store(Z(0)[:, ZOFF:ZOFF + 1024])
                return nc
            ffn(i)
    final_out()

    with nc.Block() as block:
        B.finalize(block)
    return nc


_NC_CACHE = {}


def _get_nc():
    key = (CFG["depth"], CFG["do_outer"], CFG["do_inner"], CFG["barrier_level"],
           CFG.get("do_spill", True), CFG.get("cut", ""))
    if key not in _NC_CACHE:
        nc = bass.Bass("TRN2", target_bir_lowering=False, dynamic_dma_scratch_size=512)
        build_program(nc)
        _NC_CACHE[key] = nc
    return _NC_CACHE[key]


def kernel(L0_attn_hidden, L0_attn_Wq, L0_attn_Wk, L0_attn_Wv, L0_attn_Wo,
           L0_ffn_Wqkv, L0_ffn_q_scale, L0_ffn_k_scale, L0_ffn_Wo, L0_ffn_ln_g,
           L0_ffn_ln_b, L0_ffn_W1, L0_ffn_b1, L0_ffn_W2, L0_ffn_b2):
    nc = _get_nc()
    weights = {
        "wq": np.ascontiguousarray(L0_attn_Wq, np.float32),
        "wk": np.ascontiguousarray(L0_attn_Wk, np.float32),
        "wv": np.ascontiguousarray(L0_attn_Wv, np.float32),
        "wo": np.ascontiguousarray(L0_attn_Wo, np.float32),
        "fwqkv": np.ascontiguousarray(L0_ffn_Wqkv, np.float32),
        "fqs": np.ascontiguousarray(L0_ffn_q_scale, np.float32),
        "fks": np.ascontiguousarray(L0_ffn_k_scale, np.float32),
        "fwo": np.ascontiguousarray(L0_ffn_Wo, np.float32),
        "flng": np.ascontiguousarray(L0_ffn_ln_g, np.float32),
        "flnb": np.ascontiguousarray(L0_ffn_ln_b, np.float32),
        "fw1": np.ascontiguousarray(L0_ffn_W1, np.float32),
        "fb1": np.ascontiguousarray(L0_ffn_b1, np.float32),
        "fw2": np.ascontiguousarray(L0_ffn_W2, np.float32),
        "fb2": np.ascontiguousarray(L0_ffn_b2, np.float32),
    }
    x = np.ascontiguousarray(L0_attn_hidden, np.float32)
    in_maps = [dict(weights, x=x[b]) for b in range(8)]
    res = run_bass_kernel_spmd(nc, in_maps, core_ids=list(range(8)))
    out = np.stack([res.results[b]["y"] for b in range(8)], 0)
    return out.astype(np.float32)


# version 05 (final)
# speedup vs baseline: 1.1957x; 1.0807x over previous
import numpy as np
import ml_dtypes
import concourse.bass as bass
import concourse.mybir as mybir
from concourse.bass_utils import run_bass_kernel_spmd

F32 = mybir.dt.float32
BF16 = mybir.dt.bfloat16
U16 = mybir.dt.uint16
AF = mybir.ActivationFunctionType
ALU = mybir.AluOpType
AX = mybir.AxisListType
BF = ml_dtypes.bfloat16

D, T, H, DH, WIN, DEPTH, DI = 1024, 2048, 16, 64, 256, 6, 2730
QKS = 8.0
NPT, NTT = 8, 16
ZOFF = 1
ZW = T + ZOFF
NEG = -1.0e30

CFG = {"depth": DEPTH, "do_outer": True, "do_inner": True, "barrier_level": 1,
       "do_spill": True}


def rotary_tables():
    half = np.arange(0, DH, 2, dtype=np.float64)
    inv_freq = 1.0 / 10000.0 ** (half / DH)
    t = np.arange(T, dtype=np.float64)
    freqs = t[:, None] * inv_freq[None, :]
    freqs = np.concatenate([freqs, freqs], -1)
    base = (half + 0.4 * DH) / (1.4 * DH)
    power = (t - T // 2) / (WIN // 2)
    scale = base[None, :] ** power[:, None]
    scale = np.concatenate([scale, scale], -1)
    cos, sin = np.cos(freqs), np.sin(freqs)
    sgn = np.ones((1, DH))
    sgn[:, : DH // 2] = -1.0
    c1q = QKS * cos * scale
    c2q = QKS * sin * scale * sgn
    c1k = cos / scale
    c2k = sin / scale * sgn
    return [np.ascontiguousarray(a, dtype=np.float32) for a in (c1q, c2q, c1k, c2k)]


def mask_tiles():
    u = np.arange(128)[:, None]
    tt = np.arange(128)[None, :]
    L = np.where(tt <= u, 0.0, NEG)
    Ui = np.where(tt >= u, 0.0, NEG)
    Us = np.where(tt > u, 0.0, NEG)
    Zt = np.zeros((128, 128))
    inner = np.concatenate([Ui, Zt, L], 1).astype(BF)
    outer = np.concatenate([Us, Zt, L], 1).astype(BF)
    zl = np.concatenate([Zt, L], 1).astype(BF)
    return inner, outer, zl


class Bld:
    def __init__(self, nc):
        self.nc = nc
        self.q = {k: [] for k in ("T", "V", "S", "G", "Y")}
        self.sem = {}
        self.cnt = {}
        self.cleared = {}
        for name in ("T", "V", "S", "G", "BAR"):
            self.sem[name] = nc.alloc_semaphore("sem_" + name)
            self.cnt[name] = 0

    def _ch(self, ch):
        name = "DW_" + ch
        if name not in self.sem:
            self.sem[name] = self.nc.alloc_semaphore("sem_" + name)
            self.cnt[name] = 0
        return name

    def _app(self, eng, fn):
        self.q[eng].append(fn)

    def wait(self, eng, name, val=None):
        v = self.cnt[name] if val is None else val
        v -= self.cleared.get(name, 0)
        if v <= 0:
            return
        h = self.sem[name]
        self._app(eng, lambda e, h=h, v=v: e.wait_ge(h, v))

    def tok(self, name):
        return self.cnt[name]

    def emit(self, eng, fn, inc=None, by=1):
        if inc is not None:
            self.cnt[inc] += by
            h = self.sem[inc]
            self._app(eng, lambda e, fn=fn, h=h, by=by: fn(e).then_inc(h, by))
            return self.cnt[inc]
        self._app(eng, fn)
        return None

    def dma(self, eng, out_ap, in_ap, ch="misc"):
        return self.emit(eng, lambda e, o=out_ap, i=in_ap: e.dma_start(out=o, in_=i),
                         inc=self._ch(ch), by=16)

    def dtok(self, ch):
        return self.cnt[self._ch(ch)]

    def dwait(self, eng, ch, val=None):
        self.wait(eng, self._ch(ch), val)

    def wait_all_dma(self, eng):
        for name in list(self.sem):
            if name.startswith("DW_"):
                self.wait(eng, name)

    def barrier(self):
        if not hasattr(self, "_bar_dw"):
            self._bar_dw = {}
        for name in ("T", "V", "S"):
            self.wait("G", name)
        for name in list(self.sem):
            if name.startswith("DW_") and self.cnt[name] > self._bar_dw.get(name, -1):
                self.wait("G", name)
                self._bar_dw[name] = self.cnt[name]
        hG = self.sem["G"]
        self.cnt["G"] += 1
        self._app("G", lambda e, h=hG: e.sem_inc(h, 1))
        g1 = self.cnt["G"]
        for eng in ("T", "V", "S"):
            self.wait(eng, "G", g1)
            h = self.sem[eng]
            self.cnt[eng] += 1
            self._app(eng, lambda e, h=h: e.sem_inc(h, 1))
        self.wait("Y", "G", g1)
        hB = self.sem["BAR"]
        self.cnt["BAR"] += 1
        self._app("Y", lambda e, h=hB: e.sem_inc(h, 1))
        for name in ("T", "V", "S", "BAR"):
            self.wait("G", name)
        for name in ("T", "V", "S"):
            h = self.sem[name]
            self._app("G", lambda e, h=h: e.sem_clear(h))
            self.cleared[name] = self.cnt[name]
        self.cnt["G"] += 1
        self._app("G", lambda e, h=hG: e.sem_inc(h, 1))
        g2 = self.cnt["G"]
        for eng in ("T", "V", "S", "Y"):
            self.wait(eng, "G", g2)

    def phase(self):
        if CFG["barrier_level"] >= 1:
            self.barrier()

    def finalize(self, block):
        qq = self.q

        @block.tensor
        def _(e):
            for fn in qq["T"]:
                fn(e)

        @block.vector
        def _(e):
            for fn in qq["V"]:
                fn(e)

        @block.scalar
        def _(e):
            for fn in qq["S"]:
                fn(e)

        @block.gpsimd
        def _(e):
            for fn in qq["G"]:
                fn(e)

        @block.sync
        def _(e):
            for fn in qq["Y"]:
                fn(e)


def build_program(nc, ext_dr=None):
    if ext_dr is not None:
        dr = dict(ext_dr)
    else:
        dr = {}
        dr["x"] = nc.dram_tensor("x", [T, D], F32, kind="ExternalInput")
        for n, sh in (("wq", [D, D]), ("wk", [D, D]), ("wv", [D, D]), ("wo", [D, D]),
                      ("fwqkv", [DEPTH, D, 3 * D]), ("fqs", [DEPTH, DH]),
                      ("fks", [DEPTH, DH]),
                      ("fwo", [DEPTH, D, D]), ("flng", [DEPTH, D]), ("flnb", [DEPTH, D]),
                      ("fw1", [DEPTH, D, 2 * DI]), ("fb1", [DEPTH, 2 * DI]),
                      ("fw2", [DEPTH, DI, D]), ("fb2", [DEPTH, D])):
            dr[n] = nc.dram_tensor(n, sh, F32, kind="ExternalInput")
        dr["y"] = nc.dram_tensor("y", [T, D], F32, kind="ExternalOutput")
    dr["xmid"] = nc.dram_tensor("xmid", [T, D], F32, kind="Internal")

    c1q, c2q, c1k, c2k = rotary_tables()
    m_inner, m_outer, m_zl = mask_tiles()
    co = {}
    co["c1q"] = nc.inline_tensor(c1q, name="c1q")
    co["c2q"] = nc.inline_tensor(c2q, name="c2q")
    co["c1k"] = nc.inline_tensor(c1k, name="c1k")
    co["c2k"] = nc.inline_tensor(c2k, name="c2k")
    co["m_inner"] = nc.inline_tensor(m_inner.view(np.uint16), name="mask_inner")
    co["m_outer"] = nc.inline_tensor(m_outer.view(np.uint16), name="mask_outer")
    co["m_zl"] = nc.inline_tensor(m_zl.view(np.uint16), name="mask_zl")
    co["i16"] = nc.inline_tensor(np.eye(128, dtype=BF).view(np.uint16), name="ident16")
    co["i32"] = nc.inline_tensor(np.eye(128, dtype=np.float32), name="ident32")
    co["onesrow"] = nc.inline_tensor(np.ones((1, 128), np.float32), name="onesrow")
    co["onescol16"] = nc.inline_tensor(np.ones((128, 1), dtype=BF).view(np.uint16),
                                       name="onescol16")
    co["ones1"] = nc.inline_tensor(np.ones((1, 1), np.float32), name="ones1")
    co["epsc"] = nc.inline_tensor(np.full((1, 1), 1e-5, np.float32), name="epsc")

    sb = {}

    def alo(name, shape, dtype):
        sb[name] = nc.alloc_sbuf_tensor("s_" + name, list(shape), dtype)
        return sb[name]

    for p in range(NPT):
        alo(f"z{p}", [128, ZW], F32)
    for p in range(NPT):
        alo(f"xn{p}", [128, T], BF16)
    for n, sh, dt in (("ident16", [128, 128], BF16), ("ident32", [128, 128], F32),
                      ("m_inner", [128, 384], BF16), ("m_outer", [128, 384], BF16),
                      ("m_zl", [128, 256], BF16), ("onesrow", [1, 128], F32),
                      ("onescol16", [128, 1], BF16), ("ones1", [1, 1], F32),
                      ("epsc", [1, 1], F32),
                      ("qsrow", [1, DH], F32), ("ksrow", [1, DH], F32),
                      ("qs_sw", [1, DH], F32), ("ks_sw", [1, DH], F32),
                      ("qsB", [128, DH], F32), ("qsSB", [128, DH], F32),
                      ("ksB", [128, DH], F32), ("ksSB", [128, DH], F32),
                      ("gcol", [128, NPT], F32), ("bcol", [128, NPT], F32),
                      ("b2col", [128, NPT], F32), ("b1c", [128, 4], F32),
                      ("savecol", [128, 4], F32), ("nmax", [128, 8], F32),
                      ("rsum", [128, 8], F32), ("rrec", [128, 8], F32),
                      ("ssb", [128, 8], F32), ("rnb", [128, 8], F32)):
        alo(n, sh, dt)

    ARENA = 120 * 1024
    arena = nc.bump_sbuf(ARENA, "left")
    assert arena is not None, "arena bump failed"
    abase = arena[0]

    def alo_at(name, shape, dtype, off):
        sz = 1
        for d in shape[1:]:
            sz *= d
        sz *= 4 if dtype == F32 else 2
        assert off + sz <= ARENA, (name, off, sz)
        sb[name] = nc.alloc_sbuf_tensor_at("s_" + name, list(shape), dtype,
                                           offset=abase + off)
        return off + ((sz + 31) // 32) * 32

    o = 0
    for p in range(NPT):
        o = alo_at(f"kbuf{p}", [128, 1280], BF16, o)
    for i in range(10):
        o = alo_at(f"vbuf{i}", [128, 1024], BF16, o)
    for p in range(NPT):
        o = alo_at(f"otbuf{p}", [128, 1024], BF16, o)
    for i in range(2):
        o = alo_at(f"wland{i}", [128, 1024], F32, o)
        o = alo_at(f"wbf{i}", [128, 1024], BF16, o)
        o = alo_at(f"woland{i}", [128, 1024], F32, o)
        o = alo_at(f"wobf{i}", [128, 1024], BF16, o)
        o = alo_at(f"wT{i}", [128, 1024], BF16, o)
    for n in ("c1qh", "c2qh", "c1kh", "c2kh", "aq", "bq", "ak", "bk"):
        o = alo_at(n, [128, 512], F32, o)
    for i in range(3):
        o = alo_at(f"qraw{i}", [128, 128], F32, o)
        o = alo_at(f"qhat{i}", [128, 128], F32, o)
        o = alo_at(f"t1b{i}", [128, 128], F32, o)
        o = alo_at(f"t2b{i}", [128, 128], F32, o)
        o = alo_at(f"qpp{i}", [128, 128], BF16, o)
        o = alo_at(f"pbuf{i}", [128, 384], BF16, o)
        o = alo_at(f"pnorm{i}", [128, 384], BF16, o)
    o = alo_at("pbuf3", [128, 384], BF16, o)
    o = alo_at("pnorm3", [128, 384], BF16, o)
    o = alo_at("sqb0", [128, 128], F32, o)
    o = alo_at("sqb1", [128, 128], F32, o)
    o = alo_at("sqb2", [128, 128], F32, o)
    o = alo_at("ptb", [128, 768], BF16, o)
    assert o <= ARENA, f"attn set {o}"

    o = 0
    for i in range(6):
        o = alo_at(f"act{i}", [128, 2048], BF16, o)
    for i in range(6):
        o = alo_at(f"w2bf{i}", [128, 1024], BF16, o)
    for i in range(2):
        o = alo_at(f"w1land{i}", [128, 2048], F32, o)
        o = alo_at(f"w1abf{i}", [128, 1024], BF16, o)
        o = alo_at(f"w1gbf{i}", [128, 1024], BF16, o)
        o = alo_at(f"w2land{i}", [128, 1024], F32, o)
    o = alo_at("b1row", [1, 2 * DI], F32, o)
    o = alo_at("grow", [1, D], F32, o)
    o = alo_at("brow", [1, D], F32, o)
    o = alo_at("b2row", [1, D], F32, o)
    for n in ("srow", "srow2", "mrow", "vrow"):
        o = alo_at(n, [1, 512], F32, o)
    for i in range(2):
        o = alo_at(f"ggel{i}", [128, 512], BF16, o)
        o = alo_at(f"tb{i}", [128, 512], BF16, o)
        o = alo_at(f"tsq{i}", [128, 512], BF16, o)
        o = alo_at(f"s0buf{i}", [128, 512], F32, o)
    ffn_top = o
    assert o <= ARENA, f"ffn set {o}"

    o = 0
    for i in range(2):
        o = alo_at(f"xland{i}", [128, 1024], F32, o)
        o = alo_at(f"outb{i}", [128, 1024], F32, o)
    assert o <= ARENA, f"final set {o}"

    tshift = [sb[f"xn{p}"][:, 0:1024] for p in range(NPT)]
    qfull = [sb[f"xn{p}"][:, 1024:2048] for p in range(NPT)]

    def qview(flavor, hf, p):
        if flavor == "inner":
            return qfull[p]
        o = 1024 - 1024 * hf
        return sb[f"xn{p}"][:, o:o + 1024]

    ps = {}
    for n in ("A", "B", "C", "G", "H", "D", "E", "F"):
        ps[n] = nc.alloc_psum_tensor("ps" + n, [128, 512], F32)

    B = Bld(nc)
    Z = lambda p: sb[f"z{p}"]
    I16 = lambda: sb["ident16"][:]

    def load_consts():
        for n, c in (("m_inner", "m_inner"), ("m_outer", "m_outer"), ("m_zl", "m_zl"),
                     ("ident16", "i16"), ("onescol16", "onescol16")):
            B.dma("Y", sb[n][:].bitcast(U16), co[c].ap(), ch="init")
        for n, c in (("ident32", "i32"), ("onesrow", "onesrow"), ("ones1", "ones1"),
                     ("epsc", "epsc")):
            B.dma("Y", sb[n][:], co[c].ap(), ch="init")
        for eng in ("T", "V", "S", "G", "Y"):
            B.dwait(eng, "init")

    def load_x():
        for p in range(NPT):
            B.emit("V", lambda e, p=p: e.memset(Z(p)[:, 0:ZOFF], 0.0), inc="V")
        war = {0: 0, 1: 0}
        regwar = {}
        for tt in range(NTT):
            par = tt % 2
            buf = sb[f"xland{par}"]
            B.wait("Y", "V", war[par])
            B.dma("Y", buf[:], dr["x"].ap()[128 * tt:128 * (tt + 1), :], ch=f"x{par}")
            dwt = B.dtok(f"x{par}")
            B.dwait("T", f"x{par}", dwt)
            for p in range(NPT):
                bk = ("A", "B", "C", "D")[p % 4]
                bank = ps[bk]
                B.wait("T", "V", regwar.get(bk, 0))
                B.emit("T", lambda e, buf=buf, p=p, bank=bank:
                       e.matmul(out=bank[:, 0:128],
                                lhsT=buf[:, 128 * p:128 * (p + 1)],
                                rhs=sb["ident32"][:], start=True, stop=True), inc="T")
                tts = B.tok("T")
                B.wait("V", "T", tts)
                B.emit("V", lambda e, p=p, tt=tt, bank=bank:
                       e.tensor_copy(out=Z(p)[:, ZOFF + 128 * tt: ZOFF + 128 * (tt + 1)],
                                     in_=bank[:, 0:128]), inc="V")
                regwar[bk] = B.tok("V")
            war[par] = B.tok("V")
        B.phase()

    def ln_full(use_shift, affine, dst, tcs=range(4)):
        sum_tok = {0: 0, 1: 0}
        stat_read = 0
        apply_done = 0
        for tc in tcs:
            c0 = ZOFF + 512 * tc
            for p in range(NPT):
                par = p % 2
                tbuf, sqb = sb[f"tb{par}"], sb[f"tsq{par}"]
                B.wait("V", "T", sum_tok[par])
                if p == 0:
                    B.wait("T", "V", stat_read)
                off = c0 - (1 if (use_shift and p >= 4) else 0)
                cv = B.emit("V", lambda e, p=p, off=off, tbuf=tbuf:
                            e.tensor_copy(out=tbuf[:], in_=Z(p)[:, off:off + 512]), inc="V")
                B.wait("V", "V", cv)
                B.emit("V", lambda e, tbuf=tbuf, sqb=sqb:
                       e.tensor_tensor(out=sqb[:], in0=tbuf[:], in1=tbuf[:],
                                       op=ALU.mult), inc="V")
                vt = B.tok("V")
                B.wait("T", "V", vt)
                B.emit("T", lambda e, p=p, tbuf=tbuf:
                       e.matmul(out=ps["D"][0:1, 0:512], lhsT=sb["onescol16"][:],
                                rhs=tbuf[:], start=(p == 0), stop=(p == NPT - 1)), inc="T")
                B.emit("T", lambda e, p=p, sqb=sqb:
                       e.matmul(out=ps["E"][0:1, 0:512], lhsT=sb["onescol16"][:],
                                rhs=sqb[:], start=(p == 0), stop=(p == NPT - 1)), inc="T")
                sum_tok[par] = B.tok("T")
            t_sums = B.tok("T")
            rstd = sb["srow"][:]
            nmr = sb["srow2"][:]
            mu, vr = sb["mrow"][:], sb["vrow"][:]
            B.wait("V", "T", t_sums)
            B.emit("V", lambda e, mu=mu: e.tensor_scalar_mul(
                out=mu, in0=ps["D"][0:1, 0:512], scalar1=1.0 / D), inc="V")
            vrt = B.emit("V", lambda e, vr=vr: e.tensor_scalar_mul(
                out=vr, in0=ps["E"][0:1, 0:512], scalar1=1.0 / D), inc="V")
            stat_read = B.tok("V")
            vt = B.tok("V")
            B.wait("S", "V", vt)
            B.emit("S", lambda e, mu=mu, nmr=nmr: e.activation(
                out=nmr, in_=mu, func=AF.Square), inc="S")
            st = B.tok("S")
            B.wait("V", "S", st)
            B.wait("V", "V", vrt)
            B.emit("V", lambda e, vr=vr, nmr=nmr: e.tensor_tensor(
                out=vr, in0=vr, in1=nmr, op=ALU.subtract), inc="V")
            vt = B.tok("V")
            B.wait("S", "V", vt)
            lnt = B.emit("S", lambda e, vr=vr, mu=mu: e.activation(
                out=vr, in_=vr, func=AF.Ln, bias=sb["epsc"][:]), inc="S")
            B.wait("S", "S", lnt)
            B.emit("S", lambda e, vr=vr, rstd=rstd: e.activation(
                out=rstd, in_=vr, func=AF.Exp, scale=-0.5), inc="S")
            st = B.tok("S")
            B.wait("V", "S", st)
            B.wait("V", "V", vrt)
            B.emit("V", lambda e, mu=mu, rstd=rstd, nmr=nmr: e.scalar_tensor_tensor(
                out=nmr, in0=mu, scalar=-1.0, in1=rstd, op0=ALU.mult, op1=ALU.mult),
                   inc="V")
            vt_rows = B.tok("V")
            B.wait("T", "V", vt_rows)
            B.wait("T", "V", apply_done)
            B.emit("T", lambda e, rstd=rstd: e.matmul(
                out=ps["F"][:, 0:512], lhsT=sb["onesrow"][:], rhs=rstd,
                start=True, stop=True), inc="T")
            B.emit("T", lambda e, nmr=nmr: e.matmul(
                out=ps["C"][:, 0:512], lhsT=sb["onesrow"][:], rhs=nmr,
                start=True, stop=True), inc="T")
            t_bc = B.tok("T")
            B.wait("V", "T", t_bc)
            s3t = {0: 0, 1: 0}
            for pp in range(0, NPT, 2):
                s0t = {}
                s1t = {}
                for p in (pp, pp + 1):
                    off = c0 - (1 if (use_shift and p >= 4) else 0)
                    buf = sb[f"s0buf{p % 2}"]
                    B.wait("V", "V", s3t[p % 2])
                    s0t[p] = B.emit("V", lambda e, p=p, off=off, buf=buf: e.tensor_tensor(
                        out=buf[:], in0=Z(p)[:, off:off + 512], in1=ps["F"][:, 0:512],
                        op=ALU.mult), inc="V")
                for p in (pp, pp + 1):
                    buf = sb[f"s0buf{p % 2}"]
                    B.wait("V", "V", s0t[p])
                    if affine or dst == "z":
                        s1t[p] = B.emit("V", lambda e, buf=buf: e.tensor_tensor(
                            out=buf[:], in0=buf[:], in1=ps["C"][:, 0:512],
                            op=ALU.add), inc="V")
                    else:
                        s1t[p] = B.emit("V", lambda e, p=p, tc=tc, buf=buf: e.tensor_tensor(
                            out=sb[f"xn{p}"][:, 512 * tc:512 * (tc + 1)], in0=buf[:],
                            in1=ps["C"][:, 0:512], op=ALU.add), inc="V")
                for p in (pp, pp + 1):
                    buf = sb[f"s0buf{p % 2}"]
                    B.wait("V", "V", s1t[p])
                    if affine:
                        s3t[p % 2] = B.emit("V", lambda e, p=p, tc=tc, buf=buf:
                                            e.tensor_scalar(
                            out=sb[f"xn{p}"][:, 512 * tc:512 * (tc + 1)], in0=buf[:],
                            scalar1=sb["gcol"][:, p:p + 1], scalar2=sb["bcol"][:, p:p + 1],
                            op0=ALU.mult, op1=ALU.add), inc="V")
                    elif dst == "z":
                        s3t[p % 2] = B.emit("V", lambda e, p=p, c0=c0, buf=buf:
                                            e.tensor_copy(
                            out=Z(p)[:, c0:c0 + 512], in_=buf[:]), inc="V")
                    else:
                        s3t[p % 2] = s1t[p]
            apply_done = B.tok("V")
        B.phase()

    def colmaker(rows_and_dsts):
        banks = ["A", "B", "C", "D", "E", "F"]
        i = 0
        while i < len(rows_and_dsts):
            grp = rows_and_dsts[i:i + 6]
            for k, (rap, _) in enumerate(grp):
                B.emit("T", lambda e, rap=rap, bk=banks[k]: e.matmul(
                    out=ps[bk][:, 0:1], lhsT=rap, rhs=sb["ones1"][:],
                    start=True, stop=True), inc="T")
            tt = B.tok("T")
            B.wait("V", "T", tt)
            for k, (_, dap) in enumerate(grp):
                B.emit("V", lambda e, dap=dap, bk=banks[k]: e.tensor_copy(
                    out=dap, in_=ps[bk][:, 0:1]), inc="V")
            vt = B.tok("V")
            B.wait("T", "V", vt)
            i += 6

    def load_attn_params(i):
        B.dma("Y", sb["qsrow"][:], dr["fqs"].ap()[i:i + 1, :], ch="par")
        B.dma("Y", sb["ksrow"][:], dr["fks"].ap()[i:i + 1, :], ch="par")
        B.dwait("V", "par")
        hd = DH // 2
        B.emit("V", lambda e: e.tensor_copy(out=sb["qs_sw"][:, hd:DH],
                                            in_=sb["qsrow"][:, 0:hd]), inc="V")
        B.emit("V", lambda e: e.tensor_copy(out=sb["qs_sw"][:, 0:hd],
                                            in_=sb["qsrow"][:, hd:DH]), inc="V")
        B.emit("V", lambda e: e.tensor_copy(out=sb["ks_sw"][:, hd:DH],
                                            in_=sb["ksrow"][:, 0:hd]), inc="V")
        B.emit("V", lambda e: e.tensor_copy(out=sb["ks_sw"][:, 0:hd],
                                            in_=sb["ksrow"][:, hd:DH]), inc="V")
        vt = B.tok("V")
        B.wait("T", "V", vt)
        rows = [("qsrow", "qsB"), ("qs_sw", "qsSB"), ("ksrow", "ksB"), ("ks_sw", "ksSB")]
        banks = ["A", "B", "C", "D"]
        for k, (r, d) in enumerate(rows):
            B.emit("T", lambda e, r=r, bk=banks[k]: e.matmul(
                out=ps[bk][:, 0:DH], lhsT=sb["onesrow"][:], rhs=sb[r][:],
                start=True, stop=True), inc="T")
        tt = B.tok("T")
        B.wait("V", "T", tt)
        for k, (r, d) in enumerate(rows):
            B.emit("V", lambda e, d=d, bk=banks[k]: e.tensor_copy(
                out=sb[d][:], in_=ps[bk][:, 0:DH]), inc="V")
        B.phase()

    def load_ffn_params(i):
        B.dma("Y", sb["grow"][:], dr["flng"].ap()[i:i + 1, :], ch="par")
        B.dma("Y", sb["brow"][:], dr["flnb"].ap()[i:i + 1, :], ch="par")
        B.dma("Y", sb["b2row"][:], dr["fb2"].ap()[i:i + 1, :], ch="par")
        B.dma("Y", sb["b1row"][:], dr["fb1"].ap()[i:i + 1, :], ch="par")
        B.dwait("T", "par")
        jobs = []
        for p in range(NPT):
            jobs.append((sb["grow"][:, 128 * p:128 * (p + 1)], sb["gcol"][:, p:p + 1]))
            jobs.append((sb["brow"][:, 128 * p:128 * (p + 1)], sb["bcol"][:, p:p + 1]))
            jobs.append((sb["b2row"][:, 128 * p:128 * (p + 1)], sb["b2col"][:, p:p + 1]))
        colmaker(jobs)
        B.phase()

    def half_tables(hf):
        T0 = 1024 * hf
        for name, cn in (("c1qh", "c1q"), ("c2qh", "c2q"), ("c1kh", "c1k"), ("c2kh", "c2k")):
            B.dma("Y", sb[name][:].rearrange("p (a d) -> p a d", d=DH),
                  co[cn].ap()[T0:T0 + 1024, :].rearrange("(a p) d -> p a d", p=128),
                  ch="tab")
        B.dwait("V", "tab")
        for dst, src, qb in (("aq", "c1qh", "qsB"), ("bq", "c2qh", "qsSB"),
                             ("ak", "c1kh", "ksB"), ("bk", "c2kh", "ksSB")):
            B.emit("V", lambda e, dst=dst, src=src, qb=qb: e.tensor_tensor(
                out=sb[dst][:].rearrange("p (a d) -> p a d", d=DH),
                in0=sb[src][:].rearrange("p (a d) -> p a d", d=DH),
                in1=sb[qb][:].unsqueeze(1).broadcast_to([128, 8, DH]),
                op=ALU.mult), inc="V")

    def boundary_copies():
        for p in range(NPT):
            B.emit("V", lambda e, p=p: e.tensor_copy(
                out=sb[f"kbuf{p}"][:, 0:256], in_=sb[f"kbuf{p}"][:, 1024:1280]), inc="V")
        B.emit("V", lambda e: e.tensor_copy(out=sb["vbuf0"][:], in_=sb["vbuf8"][:]), inc="V")
        B.emit("V", lambda e: e.tensor_copy(out=sb["vbuf1"][:], in_=sb["vbuf9"][:]), inc="V")

    def build_tshift(hf):
        T0 = 1024 * hf
        for p in range(NPT):
            if p < 4:
                B.emit("V", lambda e, p=p, T0=T0: e.tensor_copy(
                    out=tshift[p], in_=Z(p)[:, ZOFF + T0:ZOFF + T0 + 1024]), inc="V")
            elif hf == 0:
                B.emit("V", lambda e, p=p: e.tensor_copy(
                    out=tshift[p], in_=Z(p)[:, 0:1024]), inc="V")
            else:
                B.emit("V", lambda e, p=p: e.tensor_copy(
                    out=tshift[p][:, 1:1024], in_=Z(p)[:, ZOFF + 1024:ZOFF + 2047]), inc="V")
                B.emit("V", lambda e, p=p: e.tensor_copy(
                    out=tshift[p][:, 0:1], in_=sb["savecol"][:, p - 4:p - 3]), inc="V")
        if hf == 0:
            for p in range(4, NPT):
                B.emit("V", lambda e, p=p: e.tensor_copy(
                    out=sb["savecol"][:, p - 4:p - 3],
                    in_=Z(p)[:, ZOFF + 1023:ZOFF + 1024]), inc="V")

    def inner_qkv(i, hf):
        conv_tok = {0: 0, 1: 0}
        mm_tok = {0: 0, 1: 0}
        cons_tok = {0: 0, 1: 0, 2: 0, 3: 0}
        trwar = {0: 0, 1: 0}
        tailq = []
        for c in range(24):
            par = c % 2
            kind = "q" if c < 8 else ("k" if c < 16 else "v")
            B.wait("Y", "G", conv_tok[par])
            B.dma("Y", sb[f"wland{par}"][:].rearrange("p (a n) -> p a n", n=128),
                  dr["fwqkv"].ap()[i, :, 128 * c:128 * (c + 1)].rearrange(
                      "(a p) n -> p a n", p=128), ch=f"wq{par}")
            B.dwait("G", f"wq{par}")
            B.wait("G", "T", mm_tok[par])
            B.emit("G", lambda e, par=par: e.tensor_copy(
                out=sb[f"wbf{par}"][:], in_=sb[f"wland{par}"][:]), inc="G")
            conv_tok[par] = B.tok("G")
            gt = B.tok("G")
            for ttp in range(0, 8, 2):
                pend = []
                for tt in (ttp, ttp + 1):
                    bp = tt % 4
                    bank = ps[("A", "B", "D", "H")[bp]]
                    B.wait("T", "G", gt)
                    B.wait("T", "S", cons_tok[bp])
                    for dt in range(NPT):
                        B.emit("T", lambda e, c=c, tt=tt, dt=dt, par=par, bank=bank:
                               e.matmul(out=bank[:, 0:128],
                                        lhsT=tshift[dt][:, 128 * tt:128 * (tt + 1)],
                                        rhs=sb[f"wbf{par}"][:, 128 * dt:128 * (dt + 1)],
                                        start=(dt == 0), stop=(dt == NPT - 1)), inc="T")
                    mmt = B.tok("T")
                    pend.append((tt, bank, mmt))
                if tailq:
                    qk_tail(*tailq.pop())
                if kind == "v":
                    for tt, bank, mmt in pend:
                        B.wait("S", "T", mmt)
                        B.emit("S", lambda e, c=c, tt=tt, bank=bank: e.copy(
                            out=sb[f"vbuf{2 + tt}"][:, 128 * (c - 16):128 * (c - 15)],
                            in_=bank[:, 0:128]), inc="S")
                        cons_tok[tt % 4] = B.tok("S")
                else:
                    qk_front(c, hf, pend, cons_tok)
                    tailq.append((c, pend, trwar))
            mm_tok[par] = B.tok("T")
        if tailq:
            qk_tail(*tailq.pop())

    def qk_front(c, hf, pend, cons_tok):
        isq = c < 8
        cc = c if isq else c - 8
        At = sb["aq"] if isq else sb["ak"]
        Bt = sb["bq"] if isq else sb["bk"]
        NH = 2
        for tt, bank, mmt in pend:
            j3 = tt % 3
            B.wait("S", "T", mmt)
            B.wait("S", "V", qk_war.get(j3, 0))
            B.emit("S", lambda e, j3=j3, bank=bank: e.copy(
                out=sb[f"qraw{j3}"][:], in_=bank[:, 0:128]), inc="S")
            cons_tok[tt % 4] = B.tok("S")
            sev = B.tok("S")
            B.wait("G", "S", sev)
            B.wait("G", "V", qk_war.get(j3, 0))
            B.emit("G", lambda e, j3=j3: e.tensor_tensor(
                out=sb[f"sqb{j3}"][:], in0=sb[f"qraw{j3}"][:], in1=sb[f"qraw{j3}"][:],
                op=ALU.mult), inc="G")
            gt = B.tok("G")
            B.wait("V", "G", gt)
            B.emit("V", lambda e, j3=j3: e.tensor_reduce(
                out=sb["ssb"][:, 2 * j3:2 * j3 + 2],
                in_=sb[f"sqb{j3}"][:].rearrange("p (h d) -> p h d", h=NH),
                axis=AX.X, op=ALU.add), inc="V")
            vt = B.tok("V")
            B.wait("G", "V", vt)
            B.emit("G", lambda e, j3=j3: e.tensor_scalar_max(
                out=sb["rnb"][:, 2 * j3:2 * j3 + 2], in0=sb["ssb"][:, 2 * j3:2 * j3 + 2],
                scalar1=1e-24), inc="G")
            gt = B.tok("G")
            B.wait("S", "G", gt)
            lnt = B.emit("S", lambda e, j3=j3: e.activation(
                out=sb["nmax"][:, 2 * j3:2 * j3 + 2], in_=sb["rnb"][:, 2 * j3:2 * j3 + 2],
                func=AF.Ln), inc="S")
            B.wait("S", "S", lnt)
            B.emit("S", lambda e, j3=j3: e.activation(
                out=sb["rnb"][:, 2 * j3:2 * j3 + 2], in_=sb["nmax"][:, 2 * j3:2 * j3 + 2],
                func=AF.Exp, scale=-0.5), inc="S")
            pend_s[tt] = B.tok("S")
        hd = DH // 2
        qh_t = {}
        t1_t = {}
        t2_t = {}
        for step in ("qhat", "t1b", "t2a", "t2c", "qpp"):
            for tt, bank, mmt in pend:
                j3 = tt % 3
                b = tt % 2
                tsl = 64 * tt
                if step == "qhat":
                    B.wait("V", "S", pend_s[tt])
                    qh_t[tt] = B.emit("V", lambda e, j3=j3: e.tensor_tensor(
                        out=sb[f"qhat{j3}"][:].rearrange("p (h d) -> p h d", h=NH),
                        in0=sb[f"qraw{j3}"][:].rearrange("p (h d) -> p h d", h=NH),
                        in1=sb["rnb"][:, 2 * j3:2 * j3 + 2].unsqueeze(2)
                            .broadcast_to([128, NH, DH]),
                        op=ALU.mult), inc="V")
                    qk_war[j3] = qh_t[tt]
                elif step == "t1b":
                    B.wait("V", "V", qh_t[tt])
                    t1_t[tt] = B.emit("V", lambda e, j3=j3, tsl=tsl, At=At: e.tensor_tensor(
                        out=sb[f"t1b{j3}"][:].rearrange("p (h d) -> p h d", h=NH),
                        in0=sb[f"qhat{j3}"][:].rearrange("p (h d) -> p h d", h=NH),
                        in1=At[:, tsl:tsl + DH].unsqueeze(1)
                            .broadcast_to([128, NH, DH]),
                        op=ALU.mult), inc="V")
                elif step == "t2a":
                    B.emit("V", lambda e, j3=j3, tsl=tsl, Bt=Bt: e.tensor_tensor(
                        out=sb[f"t2b{j3}"][:].rearrange("p (h d) -> p h d", h=NH)[:, :, 0:hd],
                        in0=sb[f"qhat{j3}"][:].rearrange("p (h d) -> p h d", h=NH)[:, :, hd:DH],
                        in1=Bt[:, tsl:tsl + hd].unsqueeze(1).broadcast_to([128, NH, hd]),
                        op=ALU.mult), inc="V")
                elif step == "t2c":
                    t2_t[tt] = B.emit("V", lambda e, j3=j3, tsl=tsl, Bt=Bt: e.tensor_tensor(
                        out=sb[f"t2b{j3}"][:].rearrange("p (h d) -> p h d", h=NH)[:, :, hd:DH],
                        in0=sb[f"qhat{j3}"][:].rearrange("p (h d) -> p h d", h=NH)[:, :, 0:hd],
                        in1=Bt[:, tsl + hd:tsl + DH].unsqueeze(1).broadcast_to([128, NH, hd]),
                        op=ALU.mult), inc="V")
                else:
                    B.wait("V", "V", max(t1_t[tt], t2_t[tt]))
                    B.wait("V", "T", qpp_war.get(j3, 0))
                    B.emit("V", lambda e, j3=j3: e.tensor_tensor(
                        out=sb[f"qpp{j3}"][:], in0=sb[f"t1b{j3}"][:], in1=sb[f"t2b{j3}"][:],
                        op=ALU.add), inc="V")
                    pend_v[tt] = B.tok("V")
    def qk_tail(c, pend, trwar):
        isq = c < 8
        cc = c if isq else c - 8
        for tt, bank, mmt in pend:
            j3 = tt % 3
            bk = ps["C"] if tt % 2 == 0 else ps["G"]
            B.wait("T", "V", pend_v[tt])
            B.wait("T", "V", trwar[tt % 2])
            B.emit("T", lambda e, j3=j3, bk=bk: e.matmul(
                out=bk[:, 0:128], lhsT=sb[f"qpp{j3}"][:],
                rhs=I16(), start=True, stop=True), inc="T")
            trt = B.tok("T")
            qpp_war[j3] = trt
            B.wait("V", "T", trt)
            if isq:
                dst = qfull[cc][:, 128 * tt:128 * (tt + 1)]
            else:
                dst = sb[f"kbuf{cc}"][:, 256 + 128 * tt:256 + 128 * (tt + 1)]
            B.emit("V", lambda e, dst=dst, bk=bk: e.tensor_copy(
                out=dst, in_=bk[:, 0:128]), inc="V")
            trwar[tt % 2] = B.tok("V")

    qk_war = {}
    qpp_war = {}
    pend_s = {}
    pend_v = {}

    def attn_loop(hf, flavor):
        if flavor == "inner":
            subs = []
            for wi in range(4):
                for h in range(H):
                    for qh in range(2):
                        subs.append((wi, h, qh))
        else:
            subs = [(qt, h, 0) for qt in range(8) for h in range(H)]
        LAG = 4
        sim_info = {}
        war_sim = {}
        st = {"war_ptb": 0, "pt_sw": {}, "pt_seq": 0, "pnorm_war": {},
              "nmax_war": {}, "rsum_war": {}, "rrec_war": {}}
        pv_done = {0: 0, 1: 0}
        n = len(subs)
        for s in range(n + LAG):
            if s >= LAG:
                emit_tail(hf, flavor, s - LAG, subs[s - LAG], sim_info, war_sim,
                          st, pv_done)
            if s < n:
                emit_sim(hf, flavor, s, subs[s], sim_info, war_sim, st)

    def emit_sim(hf, flavor, s, sub, sim_info, war_sim, st):
        bank = ps[["A", "B", "C", "F"][s % 4]]
        if flavor == "inner":
            wi, h, qh = sub
            gw = 4 * hf + wi
            if gw == 0:
                ncols = 128 if qh == 0 else 256
                mask = sb["m_zl"][:, 128:256] if qh == 0 else sb["m_zl"][:]
                colbase = 256
            else:
                ncols = 384
                mask = sb["m_inner"][:]
                colbase = 256 * wi + 128 * qh
            qcols = (256 * wi + 128 * qh, 256 * wi + 128 * qh + 128)
            tile = h // 2
            po = 64 * (h % 2)
            lhs = qview(flavor, hf, tile)[po:po + 64, qcols[0]:qcols[1]]
            rhs = sb[f"kbuf{tile}"][po:po + 64, colbase:colbase + ncols]
        else:
            qt, h, _ = sub
            gqt = 8 * hf + qt
            j0 = max(0, gqt - 2)
            nkt = gqt - j0 + 1
            ncols = 128 * nkt
            if nkt == 3:
                mask = sb["m_outer"][:]
            elif nkt == 2:
                mask = sb["m_zl"][:]
            else:
                mask = sb["m_zl"][:, 128:256]
            colbase = 256 + 128 * (j0 - 8 * hf)
            tile = h // 2
            po = 64 * (h % 2)
            lhs = qview(flavor, hf, tile)[po:po + 64, 128 * qt:128 * (qt + 1)]
            rhs = sb[f"kbuf{tile}"][po:po + 64, colbase:colbase + ncols]
        B.wait("T", "S", war_sim.get(("S", s % 4), 0))
        B.emit("T", lambda e, bank=bank, mask=mask, ncols=ncols: e.matmul(
            out=bank[:, 0:ncols], lhsT=I16(), rhs=mask, start=True, stop=False), inc="T")
        B.emit("T", lambda e, bank=bank, lhs=lhs, rhs=rhs, ncols=ncols: e.matmul(
            out=bank[:, 0:ncols], lhsT=lhs, rhs=rhs, start=False, stop=True), inc="T")
        simt = B.tok("T")
        r = s % 8
        B.wait("S", "T", simt)
        B.wait("S", "V", st["rsum_war"].get(r, 0))
        B.emit("S", lambda e, bank=bank, r=r, s3=s % 4, ncols=ncols: e.activation(
            out=sb[f"pbuf{s3}"][:, 0:ncols], in_=bank[:, 0:ncols], func=AF.Exp,
            accum_out=sb["rsum"][:, r:r + 1]), inc="S")
        war_sim[("S", s % 4)] = B.tok("S")
        stok = B.tok("S")
        B.wait("V", "S", stok)
        B.wait("V", "S", st["rrec_war"].get(r, 0))
        B.emit("V", lambda e, r=r: e.reciprocal(
            out=sb["rrec"][:, r:r + 1], in_=sb["rsum"][:, r:r + 1]), inc="V")
        st["rsum_war"][r] = B.tok("V")
        vt = B.tok("V")
        B.wait("S", "V", vt)
        B.wait("S", "T", st["pnorm_war"].get(s % 4, 0))
        B.wait("S", "S", war_sim.get(("S", s % 4), 0))
        B.emit("S", lambda e, r=r, s3=s % 4, ncols=ncols: e.activation(
            out=sb[f"pnorm{s3}"][:, 0:ncols], in_=sb[f"pbuf{s3}"][:, 0:ncols],
            func=AF.Copy, scale=sb["rrec"][:, r:r + 1]), inc="S")
        st["rrec_war"][r] = B.tok("S")
        sim_info[s] = (B.tok("S"), ncols)

    def emit_tail(hf, flavor, s, sub, sim_info, war_sim, st, pv_done):
        norm_tok, ncols = sim_info.pop(s)
        nkt = ncols // 128
        s3 = s % 4
        if flavor == "inner":
            wi, h, qh = sub
            gw = 4 * hf + wi
            if gw == 0:
                if qh == 0:
                    blocks = [(0, 0)]
                else:
                    blocks = [(0, 128), (1, 256)]
            else:
                if qh == 0:
                    blocks = [(0, 0), (1, 128), (2, 384)]
                else:
                    blocks = [(0, 256), (1, 512), (2, 640)]
        else:
            qt, h, _ = sub
            blocks = [(j, 128 * j) for j in range(nkt)]
        B.wait("T", "S", norm_tok)
        first = True
        for jj, col in blocks:
            sl = st["pt_seq"] % 2
            st["pt_seq"] += 1
            bk = ps["G"] if sl == 0 else ps["H"]
            B.wait("T", "V", st["pt_sw"].get(sl, 0))
            B.emit("T", lambda e, s3=s3, jj=jj, bk=bk: e.matmul(
                out=bk[:, 0:128],
                lhsT=sb[f"pnorm{s3}"][:, 128 * jj:128 * (jj + 1)], rhs=I16(),
                start=True, stop=True), inc="T")
            trt = B.tok("T")
            B.wait("V", "T", trt)
            if first:
                B.wait("V", "T", st["war_ptb"])
                first = False
            B.emit("V", lambda e, col=col, bk=bk: e.tensor_copy(
                out=sb["ptb"][:, col:col + 128], in_=bk[:, 0:128]), inc="V")
            st["pt_sw"][sl] = B.tok("V")
        st["pnorm_war"][s3] = B.tok("T")
        evt = B.tok("V")
        do_pv = (flavor == "outer") or (qh == 1)
        if do_pv:
            par = h % 2
            pvbank = ps["D"] if par == 0 else ps["E"]
            B.wait("T", "V", evt)
            B.wait("T", "S", pv_done[par])
            if flavor == "inner":
                wi, h, _ = sub
                gw = 4 * hf + wi
                if gw == 0:
                    mms = [(0, 0, 256, 0), (1, 256, 128, 128)]
                else:
                    mms = [(1, 128, 256, 0), (0, 0, 128, 0), (2, 384, 256, 0),
                           (3, 640, 128, 128)]
                for mi, (vo, ptcol, wdt, oc) in enumerate(mms):
                    vidx = 2 * wi + vo if gw != 0 else 2 + vo
                    B.emit("T", lambda e, vidx=vidx, h=h, ptcol=ptcol, wdt=wdt, oc=oc,
                           pvbank=pvbank, st=(mi == 0), sp=(mi == len(mms) - 1):
                           e.matmul(out=pvbank[0:64, oc:oc + wdt],
                                    lhsT=sb[f"vbuf{vidx}"][:, 64 * h:64 * h + 64],
                                    rhs=sb["ptb"][:, ptcol:ptcol + wdt],
                                    start=st, stop=sp), inc="T")
            else:
                qt, h, _ = sub
                gqt = 8 * hf + qt
                j0 = max(0, gqt - 2)
                for mi in range(nkt):
                    vidx = (j0 + mi) - 8 * hf + 2
                    B.emit("T", lambda e, vidx=vidx, h=h, mi=mi, pvbank=pvbank,
                           st=(mi == 0), sp=(mi == nkt - 1):
                           e.matmul(out=pvbank[0:64, 0:128],
                                    lhsT=sb[f"vbuf{vidx}"][:, 64 * h:64 * h + 64],
                                    rhs=sb["ptb"][:, 128 * mi:128 * (mi + 1)],
                                    start=st, stop=sp), inc="T")
            pvt = B.tok("T")
            if par == 1:
                width = 256 if flavor == "inner" else 128
                if flavor == "inner":
                    oc0 = 256 * sub[0]
                else:
                    oc0 = 128 * sub[0]
                m = h // 2
                B.wait("S", "T", pvt)
                for pq, bkn in ((0, "D"), (64, "E")):
                    B.emit("S", lambda e, m=m, oc0=oc0, width=width, pq=pq, bkn=bkn:
                           e.copy(out=sb[f"otbuf{m}"][pq:pq + 64, oc0:oc0 + width],
                                  in_=ps[bkn][0:64, 0:width]), inc="S")
                pv_done[0] = B.tok("S")
                pv_done[1] = B.tok("S")
        st["war_ptb"] = B.tok("T")

    def wo_stage(hf, i, flavor):
        T0 = 1024 * hf
        conv_tok = {0: 0, 1: 0}
        use_tok = {0: 0, 1: 0}
        add_tok = {0: 0, 1: 0}
        for ept in range(NPT):
            par = ept % 2
            B.wait("Y", "G", conv_tok[par])
            if flavor == "inner":
                B.dma("Y", sb[f"woland{par}"][:].rearrange("p (a n) -> p a n", n=128),
                      dr["fwo"].ap()[i, :, 128 * ept:128 * (ept + 1)].rearrange(
                          "(a p) n -> p a n", p=128), ch=f"wo{par}")
            else:
                B.dma("Y", sb[f"woland{par}"][:],
                      dr["wo"].ap()[128 * ept:128 * (ept + 1), :], ch=f"wo{par}")
            B.dwait("G", f"wo{par}")
            B.wait("G", "T", use_tok[par])
            B.emit("G", lambda e, par=par: e.tensor_copy(
                out=sb[f"wobf{par}"][:], in_=sb[f"woland{par}"][:]), inc="G")
            conv_tok[par] = B.tok("G")
            gt = B.tok("G")
            if flavor == "outer":
                B.wait("T", "G", gt)
                for dt in range(NPT):
                    bk = ps[("C", "D", "E", "F")[dt % 4]]
                    B.wait("T", "V", wo_trwar.get((par, dt % 4), 0))
                    B.emit("T", lambda e, par=par, dt=dt, bk=bk: e.matmul(
                        out=bk[:, 0:128],
                        lhsT=sb[f"wobf{par}"][:, 128 * dt:128 * (dt + 1)],
                        rhs=I16(), start=True, stop=True), inc="T")
                    trt = B.tok("T")
                    B.wait("V", "T", trt)
                    B.emit("V", lambda e, par=par, dt=dt, bk=bk: e.tensor_copy(
                        out=sb[f"wT{par}"][:, 128 * dt:128 * (dt + 1)],
                        in_=bk[:, 0:128]), inc="V")
                    wo_trwar[(par, dt % 4)] = B.tok("V")
                gt = B.tok("V")
                lhsbuf = sb[f"wT{par}"]
                wsem = "V"
            else:
                lhsbuf = sb[f"wobf{par}"]
                wsem = "G"
            for tc in range(2):
                bank = ps["A"] if tc == 0 else ps["B"]
                B.wait("T", wsem, gt)
                B.wait("T", "V", add_tok[tc])
                for ipt in range(NPT):
                    B.emit("T", lambda e, lhsbuf=lhsbuf, ipt=ipt, tc=tc, bank=bank:
                           e.matmul(out=bank[:, 0:512],
                                    lhsT=lhsbuf[:, 128 * ipt:128 * (ipt + 1)],
                                    rhs=sb[f"otbuf{ipt}"][:, 512 * tc:512 * (tc + 1)],
                                    start=(ipt == 0), stop=(ipt == NPT - 1)), inc="T")
                mmt = B.tok("T")
                B.wait("V", "T", mmt)
                B.emit("V", lambda e, ept=ept, tc=tc, T0=T0, bank=bank: e.tensor_tensor(
                    out=Z(ept)[:, ZOFF + T0 + 512 * tc:ZOFF + T0 + 512 * (tc + 1)],
                    in0=Z(ept)[:, ZOFF + T0 + 512 * tc:ZOFF + T0 + 512 * (tc + 1)],
                    in1=bank[:, 0:512], op=ALU.add), inc="V")
                add_tok[tc] = B.tok("V")
            use_tok[par] = B.tok("T")

    wo_trwar = {}

    def outer_qkv(hf):
        T0 = 1024 * hf
        conv_tok = {0: 0, 1: 0}
        use_tok = {0: 0, 1: 0}
        ev_tok = {0: 0, 1: 0}
        seq = 0
        for ec in range(CFG.get("oqkv_ecs", NPT)):
            for wn, kind in (("wq", "q"), ("wk", "k"), ("wv", "v")):
                if kind not in CFG.get("oqkv_kinds", "qkv"):
                    continue
                par = seq % 2
                seq += 1
                steps = CFG.get("oqkv_steps", 7)
                B.wait("Y", "G", conv_tok[par])
                B.dma("Y", sb[f"woland{par}"][:],
                      dr[wn].ap()[128 * ec:128 * (ec + 1), :], ch=f"wo{par}")
                B.dwait("G", f"wo{par}")
                B.wait("G", "T", use_tok[par])
                if kind == "q":
                    B.emit("G", lambda e, par=par: e.tensor_scalar_mul(
                        out=sb[f"wobf{par}"][:], in0=sb[f"woland{par}"][:],
                        scalar1=0.125), inc="G")
                else:
                    B.emit("G", lambda e, par=par: e.tensor_copy(
                        out=sb[f"wobf{par}"][:], in_=sb[f"woland{par}"][:]), inc="G")
                conv_tok[par] = B.tok("G")
                gt = B.tok("G")
                if steps < 2:
                    continue
                B.wait("T", "G", gt)
                for dt in range(NPT):
                    bk = ps[("C", "D", "E", "F")[dt % 4]]
                    B.wait("T", "V", wo_trwar.get((par, dt % 4), 0))
                    B.emit("T", lambda e, par=par, dt=dt, bk=bk: e.matmul(
                        out=bk[:, 0:128],
                        lhsT=sb[f"wobf{par}"][:, 128 * dt:128 * (dt + 1)],
                        rhs=I16(), start=True, stop=True), inc="T")
                    trt = B.tok("T")
                    B.wait("V", "T", trt)
                    B.emit("V", lambda e, par=par, dt=dt, bk=bk: e.tensor_copy(
                        out=sb[f"wT{par}"][:, 128 * dt:128 * (dt + 1)],
                        in_=bk[:, 0:128]), inc="V")
                    wo_trwar[(par, dt % 4)] = B.tok("V")
                wtt = B.tok("V")
                if steps < 4:
                    continue
                if kind in ("q", "k"):
                    for tc in range(2):
                        bp = tc % 2
                        bank = ps["A"] if bp == 0 else ps["B"]
                        B.wait("T", "V", wtt)
                        B.wait("T", "V", ev_tok[bp])
                        for dt in range(NPT):
                            B.emit("T", lambda e, par=par, dt=dt, tc=tc, T0=T0, bank=bank:
                                   e.matmul(out=bank[:, 0:512],
                                            lhsT=sb[f"wT{par}"][:, 128 * dt:128 * (dt + 1)],
                                            rhs=sb[f"xn{dt}"][:, T0 + 512 * tc:
                                                              T0 + 512 * (tc + 1)],
                                            start=(dt == 0), stop=(dt == NPT - 1)), inc="T")
                        mmt = B.tok("T")
                        B.wait("V", "T", mmt)
                        if kind == "q":
                            dst = qview("outer", hf, ec)[:, 512 * tc:512 * (tc + 1)]
                        else:
                            dst = sb[f"kbuf{ec}"][:, 256 + 512 * tc:256 + 512 * (tc + 1)]
                        B.emit("V", lambda e, dst=dst, bank=bank: e.tensor_copy(
                            out=dst, in_=bank[:, 0:512]), inc="V")
                        ev_tok[bp] = B.tok("V")
                    use_tok[par] = B.tok("T")
                else:
                    for tt in range(8):
                        bp = tt % 2
                        bank = ps["A"] if bp == 0 else ps["B"]
                        B.wait("T", "V", wtt)
                        B.wait("T", "V", ev_tok[bp])
                        for dt in range(NPT):
                            B.emit("T", lambda e, par=par, dt=dt, tt=tt, T0=T0, bank=bank:
                                   e.matmul(out=bank[:, 0:128],
                                            lhsT=sb[f"xn{dt}"][:, T0 + 128 * tt:
                                                               T0 + 128 * (tt + 1)],
                                            rhs=sb[f"wT{par}"][:, 128 * dt:128 * (dt + 1)],
                                            start=(dt == 0), stop=(dt == NPT - 1)), inc="T")
                        mmt = B.tok("T")
                        B.wait("V", "T", mmt)
                        B.emit("V", lambda e, tt=tt, ec=ec, bank=bank: e.tensor_copy(
                            out=sb[f"vbuf{2 + tt}"][:, 128 * ec:128 * (ec + 1)],
                            in_=bank[:, 0:128]), inc="V")
                        ev_tok[bp] = B.tok("V")
                    use_tok[par] = B.tok("T")

    def ffn(i):
        load_ffn_params(i)
        ln_full(use_shift=True, affine=True, dst="xn")
        groups = [(0, 6), (6, 12), (12, 18), (18, 22)]
        conv_tok = {0: 0, 1: 0}
        mmuse_tok = {0: 0, 1: 0}
        for g0, g1 in groups:
            bias_war = 0
            ag_cons = {0: 0, 1: 0}
            act_v = 0
            for j in range(g0, g1):
                w = min(128, DI - 128 * j)
                slot = j - g0
                par = j % 2
                B.wait("T", "S", bias_war)
                B.emit("T", lambda e, j=j, w=w: e.matmul(
                    out=ps["F"][0:w, 0:1], lhsT=sb["b1row"][:, 128 * j:128 * j + w],
                    rhs=sb["ones1"][:], start=True, stop=True), inc="T")
                B.emit("T", lambda e, j=j, w=w: e.matmul(
                    out=ps["E"][0:w, 0:1], lhsT=sb["b1row"][:, DI + 128 * j:DI + 128 * j + w],
                    rhs=sb["ones1"][:], start=True, stop=True), inc="T")
                tbt = B.tok("T")
                B.wait("V", "T", tbt)
                B.emit("V", lambda e, par=par, w=w: e.tensor_copy(
                    out=sb["b1c"][0:w, 2 * par:2 * par + 1], in_=ps["F"][0:w, 0:1]), inc="V")
                B.emit("V", lambda e, par=par, w=w: e.tensor_copy(
                    out=sb["b1c"][0:w, 2 * par + 1:2 * par + 2], in_=ps["E"][0:w, 0:1]), inc="V")
                b1v = B.tok("V")
                B.wait("Y", "G", conv_tok[par])
                B.dma("Y", sb[f"w1land{par}"][:, 0:8 * w].rearrange("p (a n) -> p a n", n=w),
                      dr["fw1"].ap()[i, :, 128 * j:128 * j + w].rearrange(
                          "(a p) n -> p a n", p=128), ch=f"ffnw{par}")
                B.dma("Y", sb[f"w1land{par}"][:, 1024:1024 + 8 * w].rearrange(
                          "p (a n) -> p a n", n=w),
                      dr["fw1"].ap()[i, :, DI + 128 * j:DI + 128 * j + w].rearrange(
                          "(a p) n -> p a n", p=128), ch=f"ffnw{par}")
                B.dma("Y", sb[f"w2land{par}"][0:w, :],
                      dr["fw2"].ap()[i, 128 * j:128 * j + w, :], ch=f"ffnw{par}")
                B.dwait("G", f"ffnw{par}")
                B.wait("G", "T", mmuse_tok[par])
                B.emit("G", lambda e, w=w, par=par: e.tensor_copy(
                    out=sb[f"w1abf{par}"][:, 0:8 * w],
                    in_=sb[f"w1land{par}"][:, 0:8 * w]), inc="G")
                B.emit("G", lambda e, w=w, par=par: e.tensor_copy(
                    out=sb[f"w1gbf{par}"][:, 0:8 * w],
                    in_=sb[f"w1land{par}"][:, 1024:1024 + 8 * w]), inc="G")
                B.emit("G", lambda e, slot=slot, w=w, par=par: e.tensor_copy(
                    out=sb[f"w2bf{slot}"][0:w, :], in_=sb[f"w2land{par}"][0:w, :]), inc="G")
                conv_tok[par] = B.tok("G")
                gt = B.tok("G")
                for tc in range(4):
                    tp = tc % 2
                    abank = ps["A"] if tp == 0 else ps["B"]
                    gbank = ps["C"] if tp == 0 else ps["G"]
                    B.wait("T", "G", gt)
                    B.wait("T", "S", ag_cons[tp])
                    B.wait("T", "V", act_v)
                    for dt in range(NPT):
                        B.emit("T", lambda e, dt=dt, tc=tc, w=w, abank=abank, par=par:
                               e.matmul(out=abank[0:w, 0:512],
                                        lhsT=sb[f"w1abf{par}"][:, w * dt:w * (dt + 1)],
                                        rhs=sb[f"xn{dt}"][:, 512 * tc:512 * (tc + 1)],
                                        start=(dt == 0), stop=(dt == NPT - 1)), inc="T")
                    for dt in range(NPT):
                        B.emit("T", lambda e, dt=dt, tc=tc, w=w, gbank=gbank, par=par:
                               e.matmul(out=gbank[0:w, 0:512],
                                        lhsT=sb[f"w1gbf{par}"][:, w * dt:w * (dt + 1)],
                                        rhs=sb[f"xn{dt}"][:, 512 * tc:512 * (tc + 1)],
                                        start=(dt == 0), stop=(dt == NPT - 1)), inc="T")
                    mmt = B.tok("T")
                    B.wait("S", "T", mmt)
                    if tc == 0:
                        B.wait("S", "V", b1v)
                    B.wait("S", "V", act_v)
                    B.emit("S", lambda e, tp=tp, par=par, w=w, gbank=gbank: e.activation(
                        out=sb[f"ggel{tp}"][0:w, :], in_=gbank[0:w, 0:512],
                        func=AF.Gelu, bias=sb["b1c"][0:w, 2 * par + 1:2 * par + 2]), inc="S")
                    ag_cons[tp] = B.tok("S")
                    st = B.tok("S")
                    B.wait("V", "S", st)
                    B.emit("V", lambda e, slot=slot, tc=tc, tp=tp, w=w, par=par,
                           abank=abank: e.scalar_tensor_tensor(
                        out=sb[f"act{slot}"][0:w, 512 * tc:512 * (tc + 1)],
                        in0=abank[0:w, 0:512], scalar=sb["b1c"][0:w, 2 * par:2 * par + 1],
                        in1=sb[f"ggel{tp}"][0:w, :], op0=ALU.add, op1=ALU.mult), inc="V")
                    act_v = B.tok("V")
                mmuse_tok[par] = B.tok("T")
                bias_war = B.tok("S")
            B.wait("T", "V", act_v)
            B.wait("T", "G", B.tok("G"))
            add_tok = {0: 0, 1: 0}
            for ept in range(NPT):
                for tc in range(4):
                    bp = (ept * 4 + tc) % 2
                    bank = ps["A"] if bp == 0 else ps["B"]
                    B.wait("T", "V", add_tok[bp])
                    for j in range(g0, g1):
                        w = min(128, DI - 128 * j)
                        slot = j - g0
                        B.emit("T", lambda e, slot=slot, ept=ept, tc=tc, w=w, bank=bank,
                               st=(slot == 0), sp=(j == g1 - 1), j=j:
                               e.matmul(out=bank[:, 0:512],
                                        lhsT=sb[f"w2bf{slot}"][0:w, 128 * ept:128 * (ept + 1)],
                                        rhs=sb[f"act{slot}"][0:w, 512 * tc:512 * (tc + 1)],
                                        start=st, stop=sp), inc="T")
                    mmt = B.tok("T")
                    B.wait("V", "T", mmt)
                    if g0 == 0:
                        B.emit("V", lambda e, ept=ept, tc=tc, bank=bank:
                               e.scalar_tensor_tensor(
                                   out=Z(ept)[:, ZOFF + 512 * tc:ZOFF + 512 * (tc + 1)],
                                   in0=bank[:, 0:512], scalar=sb["b2col"][:, ept:ept + 1],
                                   in1=Z(ept)[:, ZOFF + 512 * tc:ZOFF + 512 * (tc + 1)],
                                   op0=ALU.add, op1=ALU.add), inc="V")
                    else:
                        B.emit("V", lambda e, ept=ept, tc=tc, bank=bank: e.tensor_tensor(
                            out=Z(ept)[:, ZOFF + 512 * tc:ZOFF + 512 * (tc + 1)],
                            in0=Z(ept)[:, ZOFF + 512 * tc:ZOFF + 512 * (tc + 1)],
                            in1=bank[:, 0:512], op=ALU.add), inc="V")
                    add_tok[bp] = B.tok("V")
            B.phase()

    def spill():
        for p in range(NPT):
            B.dma("Y", dr["xmid"].ap().rearrange("t d -> (t d)")[128 * T * p:128 * T * (p + 1)]
                  .rearrange("(q t) -> q t", q=128), Z(p)[:, ZOFF:ZOFF + T], ch="spill")
        B.dwait("Y", "spill")
        B.phase()

    def final_out():
        for p in range(NPT if CFG.get("do_spill", True) else 0):
            for hh in range(2):
                par = (2 * p + hh) % 2
                B.wait("Y", "V", fin_war.get(par, 0))
                B.dma("Y", sb[f"xland{par}"][:],
                      dr["xmid"].ap().rearrange("t d -> (t d)")
                      [128 * T * p:128 * T * (p + 1)]
                      .rearrange("(q t) -> q t", q=128)[:, 1024 * hh:1024 * (hh + 1)],
                      ch=f"fin{par}")
                B.dwait("V", f"fin{par}")
                B.emit("V", lambda e, p=p, hh=hh, par=par: e.tensor_tensor(
                    out=Z(p)[:, ZOFF + 1024 * hh:ZOFF + 1024 * (hh + 1)],
                    in0=Z(p)[:, ZOFF + 1024 * hh:ZOFF + 1024 * (hh + 1)],
                    in1=sb[f"xland{par}"][:], op=ALU.add), inc="V")
                fin_war[par] = B.tok("V")
        B.phase()
        regwar = {}
        for tt in range(NTT):
            par = tt % 2
            obuf = sb[f"outb{par}"]
            B.dwait("T", f"st{par}", st_tok.get(par, 0))
            for p in range(NPT):
                bk = ("A", "B", "C", "D")[p % 4]
                bank = ps[bk]
                B.wait("T", "V", regwar.get(bk, 0))
                B.emit("T", lambda e, p=p, tt=tt, bank=bank: e.matmul(
                    out=bank[:, 0:128],
                    lhsT=Z(p)[:, ZOFF + 128 * tt:ZOFF + 128 * (tt + 1)],
                    rhs=sb["ident32"][:], start=True, stop=True), inc="T")
                tts = B.tok("T")
                B.wait("V", "T", tts)
                B.emit("V", lambda e, p=p, obuf=obuf, bank=bank: e.tensor_copy(
                    out=obuf[:, 128 * p:128 * (p + 1)], in_=bank[:, 0:128]), inc="V")
                regwar[bk] = B.tok("V")
            vt = B.tok("V")
            B.wait("Y", "V", vt)
            B.dma("Y", dr["y"].ap()[128 * tt:128 * (tt + 1), :], obuf[:], ch=f"st{par}")
            st_tok[par] = B.dtok(f"st{par}")
        B.wait_all_dma("Y")
        B.wait_all_dma("G")

    fin_war = {}
    st_tok = {}

    cut = CFG.get("cut", "")
    load_consts()
    if cut == "consts":
        B.dwait("V", "init")
        B.emit("V", lambda e: e.tensor_copy(out=sb["outb0"][:, 0:128],
                                            in_=sb["ident32"][:]), inc="V")
        B.wait("Y", "V", B.tok("V"))
        B.dma("Y", dr["y"].ap()[0:128, 0:128], sb["outb0"][:, 0:128], ch="st0")
        B.wait_all_dma("Y")
        with nc.Block() as block:
            B.finalize(block)
        return nc
    load_x()
    if cut == "loadx":
        B.emit("V", lambda e: e.tensor_copy(out=sb["outb0"][:], in_=Z(0)[:, ZOFF:ZOFF + 1024]),
               inc="V")
        B.wait("Y", "V", B.tok("V"))
        B.dma("Y", dr["y"].ap()[0:128, :], sb["outb0"][:], ch="st0")
        B.wait_all_dma("Y")
        with nc.Block() as block:
            B.finalize(block)
        return nc
    def _cut_store(src_ap):
        B.wait("V", "T", B.tok("T"))
        B.wait("V", "S", B.tok("S"))
        B.wait("V", "G", B.tok("G"))
        B.emit("V", lambda e: e.tensor_copy(out=sb["outb0"][:], in_=src_ap), inc="V")
        B.wait("Y", "V", B.tok("V"))
        B.dma("Y", dr["y"].ap()[0:128, :], sb["outb0"][:], ch="st0")
        B.wait_all_dma("Y")
        with nc.Block() as block:
            B.finalize(block)

    if CFG["do_outer"]:
        for hf in (0, 1):
            ln_full(use_shift=False, affine=False, dst="xn", tcs=(2 * hf, 2 * hf + 1))
            if cut == "oln" and hf == 0:
                _cut_store(sb["xn0"][:, 0:1024])
                return nc
            if hf == 1:
                boundary_copies()
                B.phase()
            outer_qkv(hf)
            B.phase()
            if cut == "oqkv0" and hf == 0:
                _cut_store(qview("outer", 0, 0))
                return nc
            if cut == "oqkv" and hf == 1:
                _cut_store(qview("outer", 1, 0))
                return nc
            attn_loop(hf, "outer")
            B.phase()
            if cut == "oattn":
                _cut_store(sb["otbuf0"][:, 0:1024])
                return nc
            wo_stage(hf, None, "outer")
            B.phase()
    if CFG.get("do_spill", True):
        spill()
        ln_full(use_shift=False, affine=False, dst="z")
    if cut == "z0":
        _cut_store(Z(0)[:, ZOFF:ZOFF + 1024])
        return nc
    if CFG["do_inner"]:
        for i in range(CFG["depth"]):
            load_attn_params(i)
            for hf in (0, 1):
                half_tables(hf)
                if hf == 1:
                    boundary_copies()
                build_tshift(hf)
                B.phase()
                inner_qkv(i, hf)
                B.phase()
                if cut == "iqkv" and hf == 0:
                    _cut_store(qview("inner", hf, 0))
                    return nc
                attn_loop(hf, "inner")
                B.phase()
                if cut == "iattn" and hf == 0:
                    _cut_store(sb["otbuf0"][:, 0:1024])
                    return nc
                wo_stage(hf, i, "inner")
                B.phase()
                if cut == "izh0" and hf == 0:
                    _cut_store(Z(0)[:, ZOFF:ZOFF + 1024])
                    return nc
            if cut == "izatt":
                _cut_store(Z(0)[:, ZOFF:ZOFF + 1024])
                return nc
            ffn(i)
    final_out()

    with nc.Block() as block:
        B.finalize(block)
    return nc


_NC_CACHE = {}


def _get_nc():
    key = (CFG["depth"], CFG["do_outer"], CFG["do_inner"], CFG["barrier_level"],
           CFG.get("do_spill", True), CFG.get("cut", ""))
    if key not in _NC_CACHE:
        nc = bass.Bass("TRN2", target_bir_lowering=False, dynamic_dma_scratch_size=512)
        build_program(nc)
        _NC_CACHE[key] = nc
    return _NC_CACHE[key]


def kernel(L0_attn_hidden, L0_attn_Wq, L0_attn_Wk, L0_attn_Wv, L0_attn_Wo,
           L0_ffn_Wqkv, L0_ffn_q_scale, L0_ffn_k_scale, L0_ffn_Wo, L0_ffn_ln_g,
           L0_ffn_ln_b, L0_ffn_W1, L0_ffn_b1, L0_ffn_W2, L0_ffn_b2):
    nc = _get_nc()
    weights = {
        "wq": np.ascontiguousarray(L0_attn_Wq, np.float32),
        "wk": np.ascontiguousarray(L0_attn_Wk, np.float32),
        "wv": np.ascontiguousarray(L0_attn_Wv, np.float32),
        "wo": np.ascontiguousarray(L0_attn_Wo, np.float32),
        "fwqkv": np.ascontiguousarray(L0_ffn_Wqkv, np.float32),
        "fqs": np.ascontiguousarray(L0_ffn_q_scale, np.float32),
        "fks": np.ascontiguousarray(L0_ffn_k_scale, np.float32),
        "fwo": np.ascontiguousarray(L0_ffn_Wo, np.float32),
        "flng": np.ascontiguousarray(L0_ffn_ln_g, np.float32),
        "flnb": np.ascontiguousarray(L0_ffn_ln_b, np.float32),
        "fw1": np.ascontiguousarray(L0_ffn_W1, np.float32),
        "fb1": np.ascontiguousarray(L0_ffn_b1, np.float32),
        "fw2": np.ascontiguousarray(L0_ffn_W2, np.float32),
        "fb2": np.ascontiguousarray(L0_ffn_b2, np.float32),
    }
    x = np.ascontiguousarray(L0_attn_hidden, np.float32)
    in_maps = [dict(weights, x=x[b]) for b in range(8)]
    res = run_bass_kernel_spmd(nc, in_maps, core_ids=list(range(8)))
    out = np.stack([res.results[b]["y"] for b in range(8)], 0)
    return out.astype(np.float32)
